# version 09
# baseline (speedup 1.0000x reference)
import numpy as np
import ml_dtypes
import concourse.bass as bass
import concourse.mybir as mybir
from concourse.bass_utils import run_bass_kernel_spmd

F32 = mybir.dt.float32
BF16 = mybir.dt.bfloat16
AF = mybir.ActivationFunctionType
ALU = mybir.AluOpType
AX = mybir.AxisListType

P = 128
TOK = 2048
NT = TOK // P
NCH = 8
D = 1024
NCORE = 8
EPS_RMS = float(np.finfo(np.float32).eps)
EPS_LN = 1e-5
LAMBDA_INIT = 0.8 - 0.6 * float(np.exp(-0.3 * 0.0))
L0_SCALE = 1.0 / float(np.sqrt(512.0))
BF = ml_dtypes.bfloat16
DEBUG_A = False
DEBUG_B2 = False
B2_LOADS_ONLY = False
NO_CC = False
GRAN = 512

_DT_SIZE = {F32: 4, BF16: 2, mybir.dt.float16: 2, mybir.dt.int32: 4, mybir.dt.uint8: 1}


class Sched:
    COMPUTE = ("tensor", "vector", "scalar", "gpsimd", "sync")

    def __init__(self, nc):
        self.nc = nc
        self.ops = []
        self.last_w = {}
        self.readers = {}
        self._mloc_cache = {}

    def _granules(self, ap):
        h = ap.tensor
        sp = str(h.space)
        if "SBUF" not in sp and "PSUM" not in sp:
            return {("D", h.name)}
        key = h.name
        if key not in self._mloc_cache:
            m = self.nc.lookup_mloc(h)
            addr = m.addr
            if "PSUM" in sp:
                addr = (m.bank or 0) * 2048 + (addr or 0)
            self._mloc_cache[key] = (addr or 0, "P" if "PSUM" in sp else "S")
        base, spc = self._mloc_cache[key]
        esz = _DT_SIZE[ap.dtype]
        row = 1
        for s in h.shape[1:]:
            row *= s
        off = ap.offset
        if not isinstance(off, int):
            off = 0
            lo_e, hi_e = 0, row - 1
        else:
            off = off % row if row else 0
            lo_e = hi_e = off
            pairs = list(ap.ap)[1:]
            for st, ct in pairs:
                if st >= 0:
                    hi_e += st * (ct - 1)
                else:
                    lo_e += st * (ct - 1)
        lo = base + lo_e * esz
        hi = base + hi_e * esz + esz - 1
        return {(spc, g) for g in range(lo // GRAN, hi // GRAN + 1)}

    def _dep_ops(self, reads, writes):
        raw = set()
        war = set()
        rg = set()
        wg = set()
        for ap in reads:
            rg |= self._granules(ap)
        for ap in writes:
            wg |= self._granules(ap)
        for g in rg:
            w = self.last_w.get(g)
            if w is not None:
                raw.add(w)
        for g in wg:
            w = self.last_w.get(g)
            if w is not None:
                war.add(w)
            for r in self.readers.get(g, ()):
                war.add(r)
        return raw, war, rg, wg

    def _record(self, opid, rg, wg):
        for g in rg:
            self.readers.setdefault(g, []).append(opid)
        for g in wg:
            self.last_w[g] = opid
            self.readers[g] = []

    def add(self, engine, fn, reads=(), writes=(), kind="c", tag=""):
        opid = len(self.ops)
        raw, war, rg, wg = self._dep_ops(reads, writes)
        real = set()
        for d in war:
            o = self.ops[d]
            if kind == "c" and o["kind"] == "c" and o["engine"] == engine:
                continue
            real.add(d)
        for d in raw:
            if d == opid:
                continue
            real.add(d)
        self.ops.append(dict(engine=engine, fn=fn, kind=kind, deps=real, tag=tag,
                             need_inc=False))
        self._record(opid, rg, wg)
        return opid

    def dma(self, engine, out, in_, tag="", **kw):
        def fn(eng, out=out, in_=in_, kw=kw):
            return eng.dma_start(out=out, in_=in_, **kw)
        return self.add(engine, fn, reads=[in_], writes=[out], kind="dma", tag=tag)

    def idma(self, out, in_, idxap, tag=""):
        off = in_.offset
        in0 = bass.AP(in_.tensor, 0, [list(p) for p in in_.ap])

        def fn(eng, out=out, in0=in0, idxap=idxap, off=off):
            return eng.indirect_dma_start(
                out, None, in0, bass.IndirectOffsetOnAxis(ap=idxap, axis=0),
                element_offset=off)
        return self.add("gpsimd", fn, reads=[in_, idxap], writes=[out], kind="dma",
                        tag=tag)

    def cc(self, kind, op, replica_groups, in_ap, out_ap, tag=""):
        def fn(eng, kind=kind, op=op, rg=replica_groups, i=in_ap, o=out_ap):
            return eng.collective_compute(kind, op, replica_groups=rg, ins=[i], outs=[o])
        return self.add("gpsimd", fn, reads=[in_ap], writes=[out_ap], kind="cc", tag=tag)

    def build(self, final_waits=True):
        nc = self.nc
        for o in self.ops:
            if o["kind"] in ("dma", "cc"):
                o["need_inc"] = True
            for d in o["deps"]:
                self.ops[d]["need_inc"] = True
        if final_waits:
            pass
        num = {}
        for i, o in enumerate(self.ops):
            q = (o["engine"], "c") if o["kind"] == "c" else (o["engine"], o["kind"])
            o["queue"] = q
            if o["need_inc"]:
                num[q] = num.get(q, 0) + 1
                o["inc_no"] = num[q]
        sems = {}
        for q in num:
            sems[q] = nc.ctx.enter_context(nc.semaphore(f"sem_{q[0]}_{q[1]}"))
        by_engine = {}
        for i, o in enumerate(self.ops):
            by_engine.setdefault(o["engine"], []).append(i)

        def emit_for(eng_name):
            def body(eng):
                for i in by_engine.get(eng_name, ()):
                    o = self.ops[i]
                    wv = {}
                    for d in o["deps"]:
                        od = self.ops[d]
                        q = od["queue"]
                        v = od["inc_no"] * (16 if q[1] == "dma" else 1)
                        if wv.get(q, 0) < v:
                            wv[q] = v
                    for q, v in wv.items():
                        eng.wait_ge(sems[q], v)
                    ins = o["fn"](eng)
                    if o["need_inc"]:
                        amt = 16 if o["kind"] == "dma" else 1
                        ins.then_inc(sems[o["queue"]], amt)
            return body

        with nc.Block() as blk:
            for name in ("tensor", "vector", "scalar", "gpsimd", "sync"):
                if name in by_engine:
                    getattr(blk, name)(emit_for(name))
        return sems

    def barrier_read(self, engine, aps, tag="final"):
        def fn(eng):
            return eng.nop()
        return self.add(engine, fn, reads=list(aps), writes=(), kind="c", tag=tag)



from contextlib import ExitStack


def _newprog():
    nc = bass.Bass("TRN2", target_bir_lowering=False)
    S = Sched(nc)
    return nc, S


def _mk_consts(nc, pfx=""):
    return dict(
        ident=nc.alloc_sbuf_tensor(pfx + "ident", [P, P], BF16),
        ones_bf=nc.alloc_sbuf_tensor(pfx + "ones_bf", [P, 1], BF16),
        ones_row_f=nc.alloc_sbuf_tensor(pfx + "ones_row_f", [1, P], F32),
        ones_row_bf=nc.alloc_sbuf_tensor(pfx + "ones_row_bf", [1, P], BF16),
        iscr=nc.alloc_sbuf_tensor(pfx + "iscr", [P, P], F32),
        pscr=nc.alloc_sbuf_tensor(pfx + "pscr", [P, 512], F32),
    )


def AL(nc, es, name, shape, dtype):
    if es is None:
        return nc.alloc_sbuf_tensor(name, shape, dtype)
    return es.enter_context(nc.sbuf_tensor(name, shape, dtype))


GROUPS4 = [[0, 1, 2, 3], [4, 5, 6, 7]]
GROUPS8 = [list(range(8))]

def f32c(x):
    return np.ascontiguousarray(x, dtype=np.float32)


def chunk_col(vec):
    return np.ascontiguousarray(np.asarray(vec, np.float32).reshape(NCH, P).T)


def emit_consts(S, C):
    S.add("gpsimd", lambda g: g.iota(C["iscr"][:, :], [[1, P]], base=0,
                                     channel_multiplier=-1,
                                     allow_small_or_imprecise_dtypes=True),
          writes=[C["iscr"][:, :]], tag="iota")
    S.add("gpsimd", lambda g: g.memset(C["ones_bf"][:, :], 1.0), writes=[C["ones_bf"][:, :]])
    S.add("gpsimd", lambda g: g.memset(C["ones_row_f"][:, :], 1.0), writes=[C["ones_row_f"][:, :]])
    S.add("gpsimd", lambda g: g.memset(C["ones_row_bf"][:, :], 1.0), writes=[C["ones_row_bf"][:, :]])
    S.add("vector", lambda v: v.tensor_single_scalar(C["ident"][:, :], C["iscr"][:, :],
                                                     0.0, ALU.is_equal),
          reads=[C["iscr"][:, :]], writes=[C["ident"][:, :]], tag="ident")


def emit_row_bcast(S, C, src_row, dst_tile, ps, n, f32=True):
    ones = C["ones_row_f"]
    for p0 in range(0, n, 512):
        w = min(512, n - p0)
        S.add("tensor",
              lambda t, p0=p0, w=w: t.matmul(ps[:, p0:p0 + w], ones[:, :],
                                             src_row[:, p0:p0 + w], start=True, stop=True),
              reads=[ones[:, :], src_row[:, p0:p0 + w]], writes=[ps[:, p0:p0 + w]],
              tag="bcast_mm")
    S.add("vector", lambda v: v.tensor_copy(dst_tile[:, 0:n], ps[:, 0:n]),
          reads=[ps[:, 0:n]], writes=[dst_tile[:, 0:n]], tag="bcast_cp")


def emit_ln(S, C, src, dst, scr, gain_col=None, combined_rms=False, tag="ln"):
    nc = S.nc
    st_g = nc.psum_tensor(f"{tag}_stps", [1, 2 * TOK], F32)
    st_ps = st_g.__enter__()
    sums = st_ps[:, 0:TOK]; sumsq = st_ps[:, TOK:2 * TOK]
    for c in range(NCH):
        S.add("scalar", lambda s, c=c: s.square(scr["sq"][:, c % 2, :], src[:, c, :]),
              reads=[src[:, c, :]], writes=[scr["sq"][:, c % 2, :]], tag=f"{tag}_sq{c}")
        S.add("vector", lambda v, c=c: v.tensor_copy(scr["xb"][:, c % 2, :], src[:, c, :]),
              reads=[src[:, c, :]], writes=[scr["xb"][:, c % 2, :]], tag=f"{tag}_xb{c}")
        def stat_mm(t, c=c):
            for piece in range(TOK // 512):
                sl = slice(piece * 512, piece * 512 + 512)
                t.matmul(sums[:, sl], C["ones_bf"][:, :], scr["xb"][:, c % 2, sl],
                         start=(c == 0), stop=(c == NCH - 1))
                ins = t.matmul(sumsq[:, sl], C["ones_bf"][:, :],
                               scr["sq"][:, c % 2, sl],
                               start=(c == 0), stop=(c == NCH - 1))
            return ins
        S.add("tensor", stat_mm,
              reads=[scr["xb"][:, c % 2, :], scr["sq"][:, c % 2, :],
                     C["ones_bf"][:, :]],
              writes=[sums[:, :], sumsq[:, :]], tag=f"{tag}_mm{c}")
    mean = scr["mrow"][:, :]
    work = scr["wrow"][:, :]

    S.add("vector", lambda v: v.tensor_scalar_mul(mean, sums[:, :], 1.0 / D),
          reads=[sums[:, :]], writes=[mean], tag=f"{tag}_mean")
    S.add("vector", lambda v: v.tensor_scalar_mul(work, sumsq[:, :], 1.0 / D),
          reads=[sumsq[:, :]], writes=[work], tag=f"{tag}_ms")
    st_g.__exit__(None, None, None)
    S.add("vector", lambda v: v.tensor_mul(scr["t1"][0:1, 0:TOK], mean, mean),
          reads=[mean], writes=[scr["t1"][0:1, 0:TOK]], tag=f"{tag}_m2")
    S.add("vector", lambda v: v.tensor_sub(work, work, scr["t1"][0:1, 0:TOK]),
          reads=[work, scr["t1"][0:1, 0:TOK]], writes=[work], tag=f"{tag}_var")
    if combined_rms:
        k1, k2 = 1.0 + EPS_RMS, EPS_RMS * EPS_LN
    else:
        k1, k2 = 1.0, EPS_LN
    S.add("vector", lambda v: v.tensor_scalar(work, work, k1, k2, ALU.mult, ALU.add),
          reads=[work], writes=[work], tag=f"{tag}_vare")
    S.add("scalar", lambda s: s.activation(work, work, AF.Sqrt),
          reads=[work], writes=[work], tag=f"{tag}_sd")
    S.add("vector", lambda v: v.reciprocal(work, work),
          reads=[work], writes=[work], tag=f"{tag}_al")
    bc_g = nc.psum_tensor(f"{tag}_bcps", [P, TOK], F32)
    bc_ps = bc_g.__enter__()
    for rowsrc, dstt in ((mean, scr["mb"]), (work, scr["ab"])):
        def bmm(t, rowsrc=rowsrc):
            for p0 in range(0, TOK, 512):
                ins = t.matmul(bc_ps[:, p0:p0 + 512], C["ones_row_f"][:, :],
                               rowsrc[:, p0:p0 + 512], start=True, stop=True)
            return ins
        S.add("tensor", bmm, reads=[C["ones_row_f"][:, :], rowsrc],
              writes=[bc_ps[:, :]], tag=f"{tag}_bmm")
        S.add("vector", lambda v, dstt=dstt: v.tensor_copy(dstt[:, :], bc_ps[:, :]),
              reads=[bc_ps[:, :]], writes=[dstt[:, :]], tag=f"{tag}_bcp")
    bc_g.__exit__(None, None, None)
    for c in range(NCH):
        t1 = scr["t1"]
        S.add("vector", lambda v, c=c: v.tensor_sub(t1[:, :], src[:, c, :], scr["mb"][:, :]),
              reads=[src[:, c, :], scr["mb"][:, :]], writes=[t1[:, :]], tag=f"{tag}_n1")
        if gain_col is None:
            S.add("vector", lambda v, c=c: v.tensor_mul(dst[:, c, :], t1[:, :], scr["ab"][:, :]),
                  reads=[t1[:, :], scr["ab"][:, :]], writes=[dst[:, c, :]], tag=f"{tag}_n2")
        else:
            S.add("vector", lambda v, c=c: v.tensor_mul(t1[:, :], t1[:, :], scr["ab"][:, :]),
                  reads=[t1[:, :], scr["ab"][:, :]], writes=[t1[:, :]], tag=f"{tag}_n2")
            S.add("vector",
                  lambda v, c=c: v.tensor_scalar_mul(dst[:, c, :], t1[:, :],
                                                     gain_col[:, c:c + 1]),
                  reads=[t1[:, :], gain_col[:, c:c + 1]], writes=[dst[:, c, :]],
                  tag=f"{tag}_n3")


def emit_proj_fm(S, C, wch, src, dst, ps, act=None, bias_col=None, scale=1.0,
                 accum_into=None, nchunks_in=NCH, tag="proj", evac_engine="scalar",
                 out_f32_add=None, x_res=None):
    width = src(0).shape[-1]
    npc = (width + 511) // 512
    for o in range(NCH):
        for piece in range(npc):
            w = min(512, width - piece * 512)
            sl = slice(piece * 512, piece * 512 + w)
            pb = (o * npc + piece) % 2
            psl = ps[:, pb * 512:pb * 512 + w]
            rd = [C["ones_bf"][:, :]]
            for i in range(nchunks_in):
                S.add("tensor",
                      lambda t, i=i, o=o, sl=sl, psl=psl: t.matmul(
                          psl, wch(i, o), src(i)[:, sl],
                          start=(i == 0), stop=(i == nchunks_in - 1)),
                      reads=[wch(i, o), src(i)[:, sl]], writes=[psl],
                      tag=f"{tag}_mm{o}_{piece}")
            if out_f32_add is not None:
                attn_ap, xres_ap = out_f32_add(o)
                t1 = C["pscr"]
                S.add("vector",
                      lambda v, psl=psl, a=attn_ap[:, sl], sl=sl: v.tensor_add(
                          t1[:, 0:a.shape[-1]], psl, a),
                      reads=[psl, attn_ap[:, sl]], writes=[t1[:, 0:w]], tag=f"{tag}_e1")
                S.add("vector",
                      lambda v, x=xres_ap[:, sl], sl=sl: v.tensor_add(
                          x, x, t1[:, 0:w]),
                      reads=[t1[:, 0:w], xres_ap[:, sl]], writes=[xres_ap[:, sl]],
                      tag=f"{tag}_e2")
            else:
                dsl = dst(o)[:, sl]
                if act is not None or bias_col is not None or scale != 1.0:
                    b = bias_col(o) if bias_col is not None else 0.0
                    fn = act if act is not None else AF.Identity
                    S.add(evac_engine,
                          lambda s, psl=psl, dsl=dsl, b=b, fn=fn: s.activation(
                              dsl, psl, fn,
                              bias=b if not isinstance(b, float) else b,
                              scale=scale),
                          reads=[psl] + ([b] if not isinstance(b, float) else []),
                          writes=[dsl], tag=f"{tag}_ev")
                else:
                    S.add("vector", lambda v, psl=psl, dsl=dsl: v.tensor_copy(dsl, psl),
                          reads=[psl], writes=[dsl], tag=f"{tag}_ev")


def emit_l0_attn_tiles(S, C, B):
    nc = S.nc
    xn, wqkv, attn_fm, x_res = B["xn"], B["wqkv"], B["attn_fm"], B["x_res"]
    qkps, tps = B["qkps"], B["tps"]
    pqk, vt = B["pqk"], B["vt"]
    eluA, eluB = B["eluA"], B["eluB"]
    gs, gg = B["gs"], B["gg"]
    acc, tmp1k = B["acc"], B["tmp1k"]
    sqt, r16 = B["sqt"], B["r16"]
    a2, a2g, t4 = B["a2"], B["a2g"], B["t4"]
    attn_tm = B["attn_tm"]

    def emit_transpose(st):
        atm = attn_tm[:, st % 2, :]

        def trg(t, atm=atm):
            for c in range(NCH):
                ins = t.transpose(tps[:, c * P:(c + 1) * P],
                                  atm[:, c * P:(c + 1) * P], C["ident"][:, :])
            return ins
        S.add("tensor", trg, reads=[atm[:, :], C["ident"][:, :]],
              writes=[tps[:, :]], tag=f"l0a_tp{st}")
        S.add("vector", lambda v, st=st: v.tensor_copy(
                  attn_fm[:, :, st * P:st * P + P],
                  tps[:, :].rearrange("p (c t) -> p c t", t=P)),
              reads=[tps[:, :]], writes=[attn_fm[:, :, st * P:st * P + P]],
              tag=f"l0a_tcp{st}")

    for st in range(NT):
        ssl = slice(st * P, st * P + P)
        for piece in range(6):
            psl = qkps[:, piece * 512:(piece + 1) * 512]

            def qkvmm(t, piece=piece, psl=psl, ssl=ssl):
                for f in range(NCH):
                    ins = t.matmul(psl, xn[:, f, ssl],
                                   wqkv[:, f, piece * 512:(piece + 1) * 512],
                                   start=(f == 0), stop=(f == NCH - 1))
                return ins
            S.add("tensor", qkvmm,
                  reads=[xn[:, :, ssl], wqkv[:, :, piece * 512:(piece + 1) * 512]],
                  writes=[psl], tag=f"l0a_qkv{st}_{piece}")
        S.add("vector", lambda v: v.tensor_scalar_min(eluA[:, :], qkps[:, 0:2048], 0.0),
              reads=[qkps[:, 0:2048]], writes=[eluA[:, :]], tag=f"l0a_min{st}")
        S.add("scalar", lambda s: s.activation(eluB[:, :], eluA[:, :], AF.Exp),
              reads=[eluA[:, :]], writes=[eluB[:, :]], tag=f"l0a_exp{st}")
        S.add("vector", lambda v: v.scalar_tensor_tensor(pqk[:, :], eluB[:, :], -1.0,
                                                         qkps[:, 0:2048], ALU.add, ALU.max),
              reads=[eluB[:, :], qkps[:, 0:2048]], writes=[pqk[:, :]], tag=f"l0a_elu{st}")
        S.add("vector", lambda v: v.tensor_copy(vt[:, :], qkps[:, 2048:3072]),
              reads=[qkps[:, 2048:3072]], writes=[vt[:, :]], tag=f"l0a_v{st}")
        if st == 0 and "dbg" in B:
            S.dma("sync", B["dbg"]["pqk"].ap(), pqk[:, :], tag="dbg_pqk")
            S.dma("sync", B["dbg"]["vt"].ap(), vt[:, :], tag="dbg_vt")
        pq = pqk[:, 0:1024].rearrange("p (h e) -> p h e", e=64)
        pk = pqk[:, 1024:2048]
        for half, gslice in ((0, gg[:, 0:256]), (1, gg[:, 256:512])):
            for hp in range(16):
                k_ = pk[:, hp * 64 + 32 * half: hp * 64 + 32 * half + 32]
                kb = k_.unsqueeze(1).broadcast_to([P, 16, 32])
                qv = pq[:, :, 32 * half:32 * half + 32]
                gsl = gs[:, (hp % 2) * 512:(hp % 2) * 512 + 512]
                prod = gsl.rearrange("p (h e) -> p h e", e=32)
                S.add("gpsimd", lambda e, prod=prod, qv=qv, kb=kb: e.tensor_tensor(
                          prod, qv, kb, ALU.mult),
                      reads=[pqk[:, :]], writes=[gsl],
                      tag=f"l0a_g{half}m{st}_{hp}")
                out = gslice.rearrange("p (h x) -> p h x", x=16)[:, :, hp]
                S.add("vector", lambda e, out=out, prod=prod: e.tensor_reduce(
                          out, prod, AX.X, ALU.add),
                      reads=[gsl], writes=[out],
                      tag=f"l0a_g{half}r{st}_{hp}")
        S.add("vector", lambda v: v.scalar_tensor_tensor(gg[:, 0:256], gg[:, 256:512],
                                                         B["neglmb"][:, :], gg[:, 0:256],
                                                         ALU.mult, ALU.add),
              reads=[gg[:, :], B["neglmb"][:, :]], writes=[gg[:, 0:256]], tag=f"l0a_h{st}")
        S.add("vector", lambda v: v.tensor_scalar_mul(gg[:, 0:256], gg[:, 0:256], L0_SCALE),
              reads=[gg[:, 0:256]], writes=[gg[:, 0:256]], tag=f"l0a_hs{st}")
        if st == 0 and "dbg" in B:
            S.dma("sync", B["dbg"]["gg"].ap(), gg[:, :], tag="dbg_gg")
        accv = acc[:, :].rearrange("p (h d) -> p h d", d=64)
        for hp in range(16):
            hcol = gg[:, 0:256].rearrange("p (h x) -> p h x", x=16)[:, :, hp]
            hb = hcol.unsqueeze(2).broadcast_to([P, 16, 64])
            vb = vt[:, hp * 64:(hp + 1) * 64].unsqueeze(1).broadcast_to([P, 16, 64])
            if hp == 0:
                S.add("vector", lambda v, hb=hb, vb=vb, accv=accv: v.tensor_tensor(
                          accv, hb, vb, ALU.mult),
                      reads=[gg[:, 0:256], vt[:, :]], writes=[acc[:, :]],
                      tag=f"l0a_acc{st}_0")
            else:
                S.add("gpsimd" if hp % 2 else "vector",
                      lambda e, hb=hb, vb=vb: e.tensor_tensor(
                          tmp1k[:, :].rearrange("p (h d) -> p h d", d=64), hb, vb, ALU.mult),
                      reads=[gg[:, 0:256], vt[:, :]], writes=[tmp1k[:, :]],
                      tag=f"l0a_accm{st}_{hp}")
                S.add("vector", lambda v: v.tensor_add(acc[:, :], acc[:, :], tmp1k[:, :]),
                      reads=[acc[:, :], tmp1k[:, :]], writes=[acc[:, :]],
                      tag=f"l0a_acca{st}_{hp}")
        if st == 0 and "dbg" in B:
            S.dma("sync", B["dbg"]["acc"].ap(), acc[:, :], tag="dbg_acc")
        S.add("scalar", lambda s: s.square(sqt[:, :], acc[:, :]),
              reads=[acc[:, :]], writes=[sqt[:, :]], tag=f"l0a_sq{st}")
        S.add("vector", lambda v: v.tensor_reduce(r16[:, 0:16],
                                                  sqt[:, :].rearrange("p (h d) -> p h d", d=64),
                                                  AX.X, ALU.add),
              reads=[sqt[:, :]], writes=[r16[:, 0:16]], tag=f"l0a_ms{st}")
        fac = (1.0 - LAMBDA_INIT) ** 2
        S.add("vector", lambda v: v.tensor_scalar(r16[:, 16:32], r16[:, 0:16],
                                                  1.0 / (64.0 * fac), EPS_RMS / fac,
                                                  ALU.mult, ALU.add),
              reads=[r16[:, 0:16]], writes=[r16[:, 16:32]], tag=f"l0a_msf{st}")
        S.add("scalar", lambda s: s.sqrt(r16[:, 32:48], r16[:, 16:32]),
              reads=[r16[:, 16:32]], writes=[r16[:, 32:48]], tag=f"l0a_msq{st}")
        S.add("vector", lambda v: v.reciprocal(r16[:, 48:64], r16[:, 32:48]),
              reads=[r16[:, 32:48]], writes=[r16[:, 48:64]], tag=f"l0a_mr{st}")
        r2b = r16[:, 48:64].unsqueeze(2).broadcast_to([P, 16, 64])
        S.add("vector", lambda v: v.tensor_tensor(a2[:, :].rearrange("p (h d) -> p h d", d=64),
                                                  accv, r2b, ALU.mult),
              reads=[acc[:, :], r16[:, 48:64]], writes=[a2[:, :]], tag=f"l0a_a2{st}")
        S.add("vector", lambda v: v.tensor_mul(a2g[:, :], a2[:, :], B["g2t"][:, :]),
              reads=[a2[:, :], B["g2t"][:, :]], writes=[a2g[:, :]], tag=f"l0a_a2g{st}")
        S.add("scalar", lambda s: s.square(sqt[:, :], a2g[:, :]),
              reads=[a2g[:, :]], writes=[sqt[:, :]], tag=f"l0a_sq3{st}")
        S.add("vector", lambda v: v.tensor_reduce(r16[:, 64:65], sqt[:, :], AX.X, ALU.add),
              reads=[sqt[:, :]], writes=[r16[:, 64:65]], tag=f"l0a_m3{st}")
        S.add("vector", lambda v: v.tensor_scalar(r16[:, 65:66], r16[:, 64:65],
                                                  1.0 / D, EPS_RMS, ALU.mult, ALU.add),
              reads=[r16[:, 64:65]], writes=[r16[:, 65:66]], tag=f"l0a_m3f{st}")
        S.add("scalar", lambda s: s.sqrt(r16[:, 66:67], r16[:, 65:66]),
              reads=[r16[:, 65:66]], writes=[r16[:, 66:67]], tag=f"l0a_m3q{st}")
        S.add("vector", lambda v: v.reciprocal(r16[:, 67:68], r16[:, 66:67]),
              reads=[r16[:, 66:67]], writes=[r16[:, 67:68]], tag=f"l0a_m3r{st}")
        S.add("vector", lambda v: v.tensor_scalar_mul(t4[:, :], a2g[:, :], r16[:, 67:68]),
              reads=[a2g[:, :], r16[:, 67:68]], writes=[t4[:, :]], tag=f"l0a_t3{st}")
        S.add("vector", lambda v: v.tensor_mul(t4[:, :], t4[:, :], B["g3t"][:, :]),
              reads=[t4[:, :], B["g3t"][:, :]], writes=[t4[:, :]], tag=f"l0a_t4{st}")
        atm = attn_tm[:, st % 2, :]
        S.add("vector", lambda v, atm=atm: v.tensor_add(atm, t4[:, :], a2g[:, :]),
              reads=[t4[:, :], a2g[:, :]], writes=[atm], tag=f"l0a_atm{st}")
        if st == 0 and "dbg" in B:
            S.dma("sync", B["dbg"]["atm"].ap(), atm[:, :], tag="dbg_atm")
            S.dma("sync", B["dbg"]["r16"].ap(), r16[:, :], tag="dbg_r16")
            S.dma("sync", B["dbg"]["a2"].ap(), a2[:, :], tag="dbg_a2")
            S.dma("sync", B["dbg"]["a2g"].ap(), a2g[:, :], tag="dbg_a2g")
            S.dma("sync", B["dbg"]["t4"].ap(), t4[:, :], tag="dbg_t4")
        if st > 0:
            emit_transpose(st - 1)
        if st == NT - 1:
            emit_transpose(st)


def emit_l0_attn_mlp(S, C, B):
    attn_fm = B["attn_fm"]; x_res = B["x_res"]
    w12 = B["w12"]
    hbuf = B["hbuf"]
    ps2 = B["ps2"]
    for tp in range(4):
        tsl = slice(tp * 512, tp * 512 + 512)
        for o in range(NCH):
            pb = ps2[:, (o % 2) * 512:(o % 2) * 512 + 512]

            def w1mm(t, o=o, pb=pb, tsl=tsl):
                for i in range(NCH):
                    ins = t.matmul(pb, w12[:, i, o * P:(o + 1) * P],
                                   attn_fm[:, i, tsl],
                                   start=(i == 0), stop=(i == NCH - 1))
                return ins
            S.add("tensor", w1mm,
                  reads=[w12[:, :, 0:1024], attn_fm[:, :, tsl]],
                  writes=[pb], tag=f"l0a_w1_{tp}_{o}")
            S.add("scalar", lambda s, o=o, pb=pb: s.activation(hbuf[:, o, :], pb, AF.Gelu),
                  reads=[pb], writes=[hbuf[:, o, :]], tag=f"l0a_w1e_{tp}_{o}")
        for o in range(NCH):
            pb = ps2[:, (o % 2) * 512:(o % 2) * 512 + 512]

            def w2mm(t, o=o, pb=pb):
                for i in range(NCH):
                    ins = t.matmul(pb, w12[:, i, 1024 + o * P:1024 + (o + 1) * P],
                                   hbuf[:, i, :],
                                   start=(i == 0), stop=(i == NCH - 1))
                return ins
            S.add("tensor", w2mm,
                  reads=[w12[:, :, 1024:2048], hbuf[:, :, :]],
                  writes=[pb], tag=f"l0a_w2_{tp}_{o}")
            t1 = C["pscr"]
            S.add("vector", lambda v, o=o, pb=pb, tsl=tsl: v.tensor_add(
                      t1[:, :], pb, attn_fm[:, o, tsl]),
                  reads=[pb, attn_fm[:, o, tsl]], writes=[t1[:, :]], tag=f"l0a_w2a_{tp}_{o}")
            S.add("vector", lambda v, o=o, tsl=tsl: v.tensor_add(
                      x_res[:, o, tsl], x_res[:, o, tsl], t1[:, :]),
                  reads=[t1[:, :], x_res[:, o, tsl]], writes=[x_res[:, o, tsl]],
                  tag=f"l0a_w2r_{tp}_{o}")


def build_A(FC=None):
    if FC is None:
        nc, S = _newprog()
        es = None
        x_d = nc.dram_tensor("x_fm", [D, TOK], F32, kind="ExternalInput")
        out_d = nc.dram_tensor("x1_fm", [D, TOK], F32, kind="ExternalOutput")
        C = _mk_consts(nc)
        x_res = nc.alloc_sbuf_tensor("x_res", [P, NCH, TOK], F32)
        emit_consts(S, C)
        for c in range(NCH):
            S.dma("sync", x_res[:, c, :], x_d.ap()[c * P:(c + 1) * P, :], tag="ld_x")
    else:
        nc, S, C, x_res, es = FC["nc"], FC["S"], FC["C"], FC["x_res"], FC["es"]
    wqkv_d = nc.dram_tensor("wqkvT", [D, 3072], F32, kind="ExternalInput")
    w1_d = nc.dram_tensor("w1T", [D, D], F32, kind="ExternalInput")
    w2_d = nc.dram_tensor("w2T", [D, D], F32, kind="ExternalInput")
    g1_d = nc.dram_tensor("g1c", [P, NCH], F32, kind="ExternalInput")
    g2_d = nc.dram_tensor("g2row", [1, D], F32, kind="ExternalInput")
    g3_d = nc.dram_tensor("g3row", [1, D], F32, kind="ExternalInput")
    lam_d = nc.dram_tensor("lam", [1, 4], F32, kind="ExternalInput")
    dbg = None
    if DEBUG_A:
        dbg = dict(
            pqk=nc.dram_tensor("d_pqk", [P, 2048], BF16, kind="ExternalOutput"),
            vt=nc.dram_tensor("d_vt", [P, 1024], BF16, kind="ExternalOutput"),
            gg=nc.dram_tensor("d_gg", [P, 512], F32, kind="ExternalOutput"),
            acc=nc.dram_tensor("d_acc", [P, 1024], F32, kind="ExternalOutput"),
            atm=nc.dram_tensor("d_atm", [P, 1024], BF16, kind="ExternalOutput"),
            r16=nc.dram_tensor("d_r16", [P, 68], F32, kind="ExternalOutput"),
            a2=nc.dram_tensor("d_a2", [P, 1024], BF16, kind="ExternalOutput"),
            a2g=nc.dram_tensor("d_a2g", [P, 1024], BF16, kind="ExternalOutput"),
            t4=nc.dram_tensor("d_t4", [P, 1024], BF16, kind="ExternalOutput"),
            xn=nc.dram_tensor("d_xn", [D, TOK], BF16, kind="ExternalOutput"),
        )

    xn = FC["outb"] if FC is not None else AL(nc, es, "A_xn", [P, NCH, TOK], BF16)
    wqkv = AL(nc, es, "A_wqkv", [P, NCH, 3072], BF16)
    SCR = AL(nc, es, "A_SCR", [P, 14336], BF16)
    SCF = SCR.bitcast(F32)
    SCRF = AL(nc, es, "A_SCRF", [P, TOK], F32)
    mrow = SCF[0:1, 0:TOK]
    wrow = SCF[0:1, TOK:2 * TOK]
    r16 = AL(nc, es, "A_r16", [P, 68], F32)
    g1c = AL(nc, es, "A_g1c", [P, NCH], F32)
    g2t = AL(nc, es, "A_g2t", [P, D], BF16)
    g3t = AL(nc, es, "A_g3t", [P, D], BF16)
    grow = SCR[0:1, 12288:14336]
    lam = AL(nc, es, "A_lam", [1, 4], F32)
    lsc = AL(nc, es, "A_lsc", [1, 8], F32)
    neglmb = AL(nc, es, "A_neglmb", [P, 1], F32)

    lnscr = dict(
        xb=SCR[:, 0:4096].rearrange("p (b t) -> p b t", t=TOK),
        sq=SCR[:, 4096:8192].rearrange("p (b t) -> p b t", t=TOK),
        mrow=mrow, wrow=wrow,
        mb=SCR[:, 8192:10240], ab=SCR[:, 10240:12288],
        t1=SCRF,
    )
    Bt = dict(
        x_res=x_res, xn=xn, attn_fm=xn, wqkv=wqkv, w12=wqkv,
        pqk=SCR[:, 0:2048],
        eluA=SCR[:, 2048:4096],
        eluB=SCR[:, 4096:6144],
        vt=SCR[:, 6144:7168],
        gs=SCR[:, 4096:5120],
        tmp1k=SCR[:, 5120:6144],
        gg=SCF[:, 3584:4096],
        acc=SCF[:, 1024:2048],
        sqt=SCR[:, 4096:5120],
        a2=SCR[:, 6144:7168],
        a2g=SCR[:, 0:1024],
        t4=SCR[:, 1024:2048],
        attn_tm=SCR[:, 8192:10240].rearrange("p (b t) -> p b t", t=1024),
        hbuf=SCR[:, 10240:14336].rearrange("p (c t) -> p c t", t=512),
        g2t=g2t, g3t=g3t, neglmb=neglmb, r16=r16,
    )

    for c in range(NCH):
        S.dma("gpsimd", wqkv[:, c, :], wqkv_d.ap()[c * P:(c + 1) * P, :], tag="ld_w")
    S.dma("sync", g1c[:, :], g1_d.ap(), tag="ld_g1")
    S.dma("gpsimd", grow[:, 0:D], g2_d.ap(), tag="ld_g2")
    S.dma("gpsimd", grow[:, D:2 * D], g3_d.ap(), tag="ld_g3")
    S.dma("sync", lam[:, :], lam_d.ap(), tag="ld_lam")

    S.add("vector", lambda v: v.tensor_mul(lsc[:, 0:1], lam[:, 0:1], lam[:, 1:2]),
          reads=[lam[:, :]], writes=[lsc[:, 0:1]])
    S.add("vector", lambda v: v.tensor_mul(lsc[:, 1:2], lam[:, 2:3], lam[:, 3:4]),
          reads=[lam[:, :]], writes=[lsc[:, 1:2]])
    S.add("scalar", lambda s: s.activation(lsc[:, 2:4], lsc[:, 0:2], AF.Exp),
          reads=[lsc[:, 0:2]], writes=[lsc[:, 2:4]])
    S.add("vector", lambda v: v.tensor_sub(lsc[:, 4:5], lsc[:, 3:4], lsc[:, 2:3]),
          reads=[lsc[:, 2:4]], writes=[lsc[:, 4:5]])
    S.add("vector", lambda v: v.tensor_scalar_add(lsc[:, 5:6], lsc[:, 4:5], -LAMBDA_INIT),
          reads=[lsc[:, 4:5]], writes=[lsc[:, 5:6]])
    with nc.psum_tensor("bc1", [P, 1], F32) as bc1:
        S.add("tensor", lambda t: t.matmul(bc1[:, :], C["ones_row_f"][:, :], lsc[:, 5:6],
                                           start=True, stop=True),
              reads=[C["ones_row_f"][:, :], lsc[:, 5:6]], writes=[bc1[:, :]])
        S.add("vector", lambda v: v.tensor_copy(neglmb[:, :], bc1[:, :]),
              reads=[bc1[:, :]], writes=[neglmb[:, :]])
    with nc.psum_tensor("bc2", [P, 1024], F32) as bc2:
        for src_, dstt in ((grow[:, 0:D], g2t), (grow[:, D:2 * D], g3t)):
            for p0 in range(0, D, 512):
                S.add("tensor", lambda t, p0=p0, src_=src_: t.matmul(
                          bc2[:, p0:p0 + 512], C["ones_row_bf"][:, :], src_[:, p0:p0 + 512],
                          start=True, stop=True),
                      reads=[C["ones_row_bf"][:, :], src_], writes=[bc2[:, p0:p0 + 512]])
            S.add("vector", lambda v, dstt=dstt: v.tensor_copy(dstt[:, :], bc2[:, :]),
                  reads=[bc2[:, :]], writes=[dstt[:, :]])

    emit_ln(S, C, x_res, xn, lnscr, gain_col=g1c, combined_rms=True, tag="ln1")
    if DEBUG_A:
        Bt["dbg"] = dbg
        for c in range(NCH):
            S.dma("sync", dbg["xn"].ap()[c * P:(c + 1) * P, :], xn[:, c, :], tag="dbg_xn")

    with (nc.psum_tensor("qkps", [P, 3072], F32) as qkps,
          nc.psum_tensor("tps", [P, 1024], BF16) as tps):
        Bt["qkps"] = qkps
        Bt["tps"] = tps
        emit_l0_attn_tiles(S, C, Bt)
    for c in range(NCH):
        S.dma("gpsimd", wqkv[:, c, 0:1024], w1_d.ap()[c * P:(c + 1) * P, :], tag="ld_w1")
        S.dma("gpsimd", wqkv[:, c, 1024:2048], w2_d.ap()[c * P:(c + 1) * P, :], tag="ld_w2")

    with nc.psum_tensor("ps2", [P, 1024], F32) as ps2:
        Bt["ps2"] = ps2
        emit_l0_attn_mlp(S, C, Bt)

    if FC is None:
        for c in range(NCH):
            S.dma("sync", out_d.ap()[c * P:(c + 1) * P, :], x_res[:, c, :], tag="st_x")
        aps = [out_d.ap()] + ([v.ap() for v in dbg.values()] if DEBUG_A else [])
        S.barrier_read("sync", aps)
        S.build()
    return nc


def pack_A(inputs, core):
    x = np.asarray(inputs["L0_attn_x"], np.float32).reshape(NCORE * TOK, D)
    sl = x[core * TOK:(core + 1) * TOK]
    return {
        "x_fm": np.ascontiguousarray(sl.T),
        "wqkvT": np.ascontiguousarray(np.concatenate(
            [np.asarray(inputs["L0_attn_Wq"]).T, np.asarray(inputs["L0_attn_Wk"]).T,
             np.asarray(inputs["L0_attn_Wv"]).T], axis=1).astype(np.float32)),
        "w1T": np.ascontiguousarray(np.asarray(inputs["L0_attn_W1"], np.float32).T),
        "w2T": np.ascontiguousarray(np.asarray(inputs["L0_attn_W2"], np.float32).T),
        "g1c": chunk_col(inputs["L0_attn_g1"]),
        "g2row": np.ascontiguousarray(np.tile(np.asarray(inputs["L0_attn_g2"], np.float32),
                                              16)[None, :]),
        "g3row": f32c(np.asarray(inputs["L0_attn_g3"]))[None, :],
        "lam": f32c(np.asarray(inputs["L0_attn_lambda_params"]))[None, :],
    }


def run_A(inputs):
    nc = build_A()
    maps = [pack_A(inputs, c) for c in range(NCORE)]
    res = run_bass_kernel_spmd(nc, maps, core_ids=list(range(NCORE)))
    out = np.empty((NCORE * TOK, D), np.float32)
    for c in range(NCORE):
        out[c * TOK:(c + 1) * TOK] = res.results[c]["x1_fm"].T
    return out.reshape(2, NCORE * TOK // 2, D)



def emit_tok_norms(S, C, src, width, stat_row, sqbuf, per_chunk, tag):
    nc = S.nc
    HW = 512
    nh = (width + HW - 1) // HW
    ps_g = nc.psum_tensor(f"{tag}_nps", [1, HW], F32)
    ps = ps_g.__enter__()
    tmpc = stat_row[:, 8:9]
    first_red = [True]

    def red(tgt, w):
        if first_red[0]:
            S.add("vector", lambda v, tgt=tgt, w=w: v.tensor_reduce(
                      tgt, ps[:, 0:w], AX.X, ALU.max),
                  reads=[ps[:, 0:w]], writes=[tgt], tag=f"{tag}_mx")
        else:
            S.add("vector", lambda v, w=w: v.tensor_reduce(tmpc, ps[:, 0:w],
                                                           AX.X, ALU.max),
                  reads=[ps[:, 0:w]], writes=[tmpc], tag=f"{tag}_mxt")
            S.add("vector", lambda v, tgt=tgt: v.tensor_max(tgt, tgt, tmpc),
                  reads=[tgt, tmpc], writes=[tgt], tag=f"{tag}_mxc")
        first_red[0] = False

    if per_chunk:
        for c in range(NCH):
            first_red[0] = True
            tgt = stat_row[:, c:c + 1]
            for h in range(nh):
                w = min(HW, width - h * HW)
                hsl = slice(h * HW, h * HW + w)
                S.add("scalar", lambda s, c=c, hsl=hsl, w=w: s.square(
                          sqbuf[:, 0:w], src[:, c, hsl]),
                      reads=[src[:, c, hsl]], writes=[sqbuf[:, 0:w]], tag=f"{tag}_sq")
                S.add("tensor", lambda t, w=w: t.matmul(
                          ps[:, 0:w], C["ones_bf"][:, :], sqbuf[:, 0:w],
                          start=True, stop=True),
                      reads=[C["ones_bf"][:, :], sqbuf[:, 0:w]], writes=[ps[:, 0:w]],
                      tag=f"{tag}_mm")
                red(tgt, w)
    else:
        tgt = stat_row[:, 0:1]
        for h in range(nh):
            w = min(HW, width - h * HW)
            hsl = slice(h * HW, h * HW + w)
            for c in range(NCH):
                S.add("scalar", lambda s, c=c, hsl=hsl, w=w: s.square(
                          sqbuf[:, 0:w], src[:, c, hsl]),
                      reads=[src[:, c, hsl]], writes=[sqbuf[:, 0:w]], tag=f"{tag}_sq")
                S.add("tensor", lambda t, w=w, c=c: t.matmul(
                          ps[:, 0:w], C["ones_bf"][:, :], sqbuf[:, 0:w],
                          start=(c == 0), stop=(c == NCH - 1)),
                      reads=[C["ones_bf"][:, :], sqbuf[:, 0:w]], writes=[ps[:, 0:w]],
                      tag=f"{tag}_mm")
            red(tgt, w)
    ps_g.__exit__(None, None, None)


def emit_shift(S, C, qn, kn, n, negc_col, rowscr, tag):
    nc = S.nc
    a = rowscr[:, 0:n]; b = rowscr[:, n:2 * n]
    S.add("scalar", lambda s: s.activation(a, qn, AF.Sqrt), reads=[qn], writes=[a],
          tag=f"{tag}_s1")
    S.add("scalar", lambda s: s.activation(b, kn, AF.Sqrt), reads=[kn], writes=[b],
          tag=f"{tag}_s2")
    S.add("vector", lambda v: v.tensor_mul(a, a, b), reads=[a, b], writes=[a],
          tag=f"{tag}_m")
    S.add("vector", lambda v: v.tensor_scalar_add(a, a, -60.0), reads=[a], writes=[a],
          tag=f"{tag}_sub")
    S.add("vector", lambda v: v.tensor_scalar(a, a, -1.0, 0.0, ALU.mult, ALU.min),
          reads=[a], writes=[a], tag=f"{tag}_neg")
    ps_g = nc.psum_tensor(f"{tag}_bps", [P, max(n, 1)], F32)
    ps = ps_g.__enter__()
    S.add("tensor", lambda t: t.matmul(ps[:, 0:n], C["ones_row_f"][:, :], a,
                                       start=True, stop=True),
          reads=[C["ones_row_f"][:, :], a], writes=[ps[:, 0:n]], tag=f"{tag}_bc")
    S.add("vector", lambda v: v.tensor_copy(negc_col[:, 0:n], ps[:, 0:n]),
          reads=[ps[:, 0:n]], writes=[negc_col[:, 0:n]], tag=f"{tag}_cp")
    ps_g.__exit__(None, None, None)


def emit_fff(S, C, F, tag, out_scale=1.0, load_w2=None):
    nc = S.nc
    xn, H, out = F["xn"], F["H"], F["out"]
    for o in range(NCH):
        for piece in range(4):
            sl = slice(piece * 512, piece * 512 + 512)
            pb = F["ps"][:, ((o * 4 + piece) % 2) * 512:((o * 4 + piece) % 2) * 512 + 512]

            def hmm(t, o=o, sl=sl, pb=pb, F=F):
                for i in range(NCH):
                    ins = t.matmul(pb, F["w1"][:, i, o * P:(o + 1) * P], xn[:, i, sl],
                                   start=(i == 0), stop=(i == NCH - 1))
                return ins
            S.add("tensor", hmm,
                  reads=[F["w1"][:, :, :], xn[:, :, sl]],
                  writes=[pb], tag=f"{tag}_h{o}_{piece}")
            S.add("scalar", lambda s, o=o, sl=sl, pb=pb: s.activation(
                      H[:, o, sl], pb, AF.Relu, bias=F["b1c"][:, o:o + 1]),
                  reads=[pb, F["b1c"][:, o:o + 1]], writes=[H[:, o, sl]],
                  tag=f"{tag}_he{o}_{piece}")
    for st in range(NT):
        ssl = slice(st * P, st * P + P)
        def nsmm(t, ssl=ssl, F=F):
            for i in range(NCH):
                ins = t.matmul(F["sps"][:, :], xn[:, i, ssl], F["nwt"][:, i, :],
                               start=(i == 0), stop=(i == NCH - 1))
            return ins
        S.add("tensor", nsmm, reads=[xn[:, :, ssl], F["nwt"][:, :, :]],
              writes=[F["sps"][:, :]], tag=f"{tag}_ns{st}")
        S.add("vector", lambda v, st=st: v.tensor_add(
                  F["sco"][:, st * 16:st * 16 + 16], F["sps"][:, :], F["nbb"][:, :]),
              reads=[F["sps"][:, :], F["nbb"][:, :]],
              writes=[F["sco"][:, st * 16:st * 16 + 16]], tag=f"{tag}_nse{st}")
    emit_fff_route(S, F, tag)
    for st in range(NT):
        S.add("tensor", lambda t, st=st: t.transpose(
                  F["tps"][:, :], F["mb16"][:, st * 16:st * 16 + 16], C["ident"][:, :]),
              reads=[F["mb16"][:, st * 16:st * 16 + 16], C["ident"][:, :]],
              writes=[F["tps"][:, :]], tag=f"{tag}_mt{st}")
        S.add("vector", lambda v, st=st: v.tensor_copy(
                  F["m16"][:, st * P:st * P + P], F["tps"][:, :]),
              reads=[F["tps"][:, :]], writes=[F["m16"][:, st * P:st * P + P]],
              tag=f"{tag}_mtc{st}")
    for c in range(NCH):
        for piece in range(4):
            sl = slice(piece * 512, piece * 512 + 512)
            S.add("tensor", lambda t, c=c, sl=sl: t.matmul(
                      F["mps"][:, :], F["bsel"][:, c, :], F["m16"][:, sl],
                      start=True, stop=True),
                  reads=[F["bsel"][:, c, :], F["m16"][:, sl]], writes=[F["mps"][:, :]],
                  tag=f"{tag}_mk{c}_{piece}")
            S.add("vector", lambda v, c=c, sl=sl: v.tensor_mul(
                      H[:, c, sl], H[:, c, sl], F["mps"][:, :]),
                  reads=[H[:, c, sl], F["mps"][:, :]], writes=[H[:, c, sl]],
                  tag=f"{tag}_mka{c}_{piece}")
    if load_w2 is not None:
        load_w2()
    for o in range(NCH):
        for piece in range(4):
            sl = slice(piece * 512, piece * 512 + 512)
            pb = F["ps"][:, ((o * 4 + piece) % 2) * 512:((o * 4 + piece) % 2) * 512 + 512]

            def omm(t, o=o, sl=sl, pb=pb, F=F):
                for i in range(NCH):
                    t.matmul(pb, F["w2"][:, i, o * P:(o + 1) * P], H[:, i, sl],
                             start=(i == 0), stop=False)
                return t.matmul(pb, F["b2"][:, o * P:(o + 1) * P], F["m16"][:, sl],
                                start=False, stop=True)
            S.add("tensor", omm,
                  reads=[F["w2"][:, :, :], H[:, :, sl],
                         F["b2"][:, :], F["m16"][:, sl]],
                  writes=[pb], tag=f"{tag}_o{o}_{piece}")
            S.add("scalar", lambda s, o=o, sl=sl, pb=pb: s.activation(
                      out[:, o, sl], pb, AF.Identity, scale=out_scale),
                  reads=[pb], writes=[out[:, o, sl]], tag=f"{tag}_oe{o}_{piece}")


def emit_fff_route(S, F, tag):
    sco, rt, mb = F["sco"], F["rt"], F["mb16"]

    def sv(off, w):
        return sco[:, :].rearrange("p (t n) -> p t n", n=16)[:, :, off:off + w]

    def rv(off, w):
        return rt[:, off * 16:(off + w) * 16].rearrange("p (x t) -> p t x", t=16) \
            if False else rt[:, :].rearrange("p (a t) -> p t a", t=16)[:, :, off:off + w]

    V = "vector"
    A = lambda fn, r, w, tg: S.add(V, fn, reads=r, writes=w, tag=f"{tag}_{tg}")
    R = rv
    allr = [rt[:, :]]; alls = [sco[:, :]]
    A(lambda v: v.tensor_single_scalar(R(0, 1), sv(0, 1), 0.0, ALU.is_ge), alls, allr, "b1")
    A(lambda v: v.tensor_scalar(R(1, 1), R(0, 1), -1.0, 1.0, ALU.mult, ALU.add), allr, allr, "nb1")
    A(lambda v: v.tensor_sub(R(2, 1), sv(2, 1), sv(1, 1)), alls, allr, "d1")
    A(lambda v: v.tensor_mul(R(2, 1), R(2, 1), R(0, 1)), allr, allr, "d2")
    A(lambda v: v.tensor_add(R(2, 1), R(2, 1), sv(1, 1)), allr + alls, allr, "d3")
    A(lambda v: v.tensor_single_scalar(R(3, 1), R(2, 1), 0.0, ALU.is_ge), allr, allr, "b2")
    A(lambda v: v.tensor_scalar(R(4, 1), R(3, 1), -1.0, 1.0, ALU.mult, ALU.add), allr, allr, "nb2")
    A(lambda v: v.tensor_mul(R(9, 1), R(1, 1), R(4, 1)), allr, allr, "p200")
    A(lambda v: v.tensor_mul(R(10, 1), R(1, 1), R(3, 1)), allr, allr, "p201")
    A(lambda v: v.tensor_mul(R(11, 1), R(0, 1), R(4, 1)), allr, allr, "p210")
    A(lambda v: v.tensor_mul(R(12, 1), R(0, 1), R(3, 1)), allr, allr, "p211")
    A(lambda v: v.tensor_mul(R(21, 4), R(9, 4), sv(3, 4)), allr + alls, allr, "s2m")
    A(lambda v: v.tensor_reduce(R(2, 1).squeeze(2), R(21, 4), AX.X, ALU.add), allr, allr, "s2r")
    A(lambda v: v.tensor_single_scalar(R(5, 1), R(2, 1), 0.0, ALU.is_ge), allr, allr, "b3")
    A(lambda v: v.tensor_scalar(R(6, 1), R(5, 1), -1.0, 1.0, ALU.mult, ALU.add), allr, allr, "nb3")
    def p3view(b):
        return rt[:, :].rearrange("p (a t) -> p t a", t=16)[:, :, 13 + b:21 + b:2]
    A(lambda v: v.tensor_mul(p3view(0), R(9, 4), R(6, 1).broadcast_to([P, 16, 4])),
      allr, allr, "p3e")
    A(lambda v: v.tensor_mul(p3view(1), R(9, 4), R(5, 1).broadcast_to([P, 16, 4])),
      allr, allr, "p3o")
    A(lambda v: v.tensor_mul(R(21, 8), R(13, 8), sv(7, 8)), allr + alls, allr, "s3m")
    A(lambda v: v.tensor_reduce(R(2, 1).squeeze(2), R(21, 8), AX.X, ALU.add), allr, allr, "s3r")
    A(lambda v: v.tensor_single_scalar(R(7, 1), R(2, 1), 0.0, ALU.is_ge), allr, allr, "b4")
    A(lambda v: v.tensor_scalar(R(8, 1), R(7, 1), -1.0, 1.0, ALU.mult, ALU.add), allr, allr, "nb4")
    def mview(b):
        return mb[:, :].rearrange("p (t l) -> p t l", l=16)[:, :, b:16:2]
    A(lambda v: v.tensor_mul(mview(0), R(13, 8), R(8, 1).broadcast_to([P, 16, 8])),
      allr, [mb[:, :]], "me")
    A(lambda v: v.tensor_mul(mview(1), R(13, 8), R(7, 1).broadcast_to([P, 16, 8])),
      allr, [mb[:, :]], "mo")



def build_B1(FC=None):
    if FC is None:
        nc, S = _newprog()
        es = None
        x_d = nc.dram_tensor("x_fm", [D, TOK], F32, kind="ExternalInput")
        C = _mk_consts(nc)
        x_res = nc.alloc_sbuf_tensor("x_res", [P, NCH, TOK], F32)
        emit_consts(S, C)
        for c in range(NCH):
            S.dma("sync", x_res[:, c, :], x_d.ap()[c * P:(c + 1) * P, :], tag="ld_x")
        okind = "ExternalOutput"
    else:
        nc, S, C, x_res, es = FC["nc"], FC["S"], FC["C"], FC["x_res"], FC["es"]
        okind = "Internal"
    wd = {}
    for t in ("q", "k", "v"):
        wd[t] = dict(
            w1=nc.dram_tensor(f"{t}_w1", [D, D], F32, kind="ExternalInput"),
            w2=nc.dram_tensor(f"{t}_w2", [D, D], F32, kind="ExternalInput"),
            nwt=nc.dram_tensor(f"{t}_nwt", [D, 16], F32, kind="ExternalInput"),
            nb=nc.dram_tensor(f"{t}_nb", [1, 16], F32, kind="ExternalInput"),
            b1c=nc.dram_tensor(f"{t}_b1c", [P, NCH], F32, kind="ExternalInput"),
            b2=nc.dram_tensor(f"{t}_b2", [16, D], F32, kind="ExternalInput"),
        )
    q_o = nc.dram_tensor("q_fm_o", [D, TOK], BF16, kind=okind)
    qn_o = nc.dram_tensor("qn_o", [1, 16], F32, kind=okind)
    k_o = nc.dram_tensor("k_o", [1032, TOK], BF16, kind=okind)
    v_o = nc.dram_tensor("v_o", [TOK, 1032], BF16, kind=okind)

    if FC is None:
        xv = x_res.bitcast(BF16)
        xn = xv[:, :, 0:TOK]
        outb = nc.alloc_sbuf_tensor("outb", [P, NCH, TOK], BF16)
    else:
        xn = AL(nc, es, "B1_xn", [P, NCH, TOK], BF16)
        outb = FC["outb"]
    H = AL(nc, es, "B1_H", [P, NCH, TOK], BF16)
    Hf = H.bitcast(F32).rearrange("p c w -> p (c w)")
    wbuf = AL(nc, es, "B1_wbuf", [P, NCH, 1024], BF16)
    SCRF = AL(nc, es, "B1_SCRF", [P, TOK], F32)
    nwt = AL(nc, es, "B1_nwt_sb", [P, NCH, 16], BF16)
    nbb = AL(nc, es, "B1_nbb", [P, 16], F32)
    b1c = AL(nc, es, "B1_b1c_sb", [P, NCH], F32)
    b2 = AL(nc, es, "B1_b2_sb", [16, D], BF16)
    nbrow = AL(nc, es, "B1_nbrow", [1, 16], F32)
    sco = AL(nc, es, "B1_sco", [P, 256], F32)
    rt = AL(nc, es, "B1_rt", [P, 480], F32)
    mb16 = AL(nc, es, "B1_mb16", [P, 256], BF16)
    m16 = AL(nc, es, "B1_m16", [16, TOK], BF16)
    bsel = AL(nc, es, "B1_bsel", [16, NCH, P], BF16)
    bselscr = SCRF[0:16, 0:NCH * P]
    vstage = AL(nc, es, "B1_vstage", [P, 1032], BF16)
    sqstat = AL(nc, es, "B1_sqstat", [P, 512], BF16)
    statrow = AL(nc, es, "B1_statrow", [1, 16], F32)
    krow = SCRF.bitcast(BF16)[0:1, 0:TOK]

    S.add("gpsimd", lambda g: g.iota(bselscr[:, :], [[-2, NCH], [-1, 2], [0, 64]],
                                     base=0, channel_multiplier=1,
                                     allow_small_or_imprecise_dtypes=True),
          writes=[bselscr[:, :]], tag="bsel_i")
    S.add("vector", lambda v: v.tensor_single_scalar(
              bsel[:, :, :].rearrange("l c p -> l (c p)"), bselscr[:, :], 0.0,
              ALU.is_equal),
          reads=[bselscr[:, :]], writes=[bsel[:, :, :]], tag="bsel_e")

    lnscr = dict(
        xb=H[:, 0:2, :], sq=H[:, 2:4, :],
        mrow=Hf[0:1, 0:TOK], wrow=Hf[0:1, TOK:2 * TOK],
        mb=H[:, 4, :], ab=H[:, 5, :], t1=SCRF[:, :],
    )
    emit_ln(S, C, x_res, xn, lnscr, tag="ln2")

    with (nc.psum_tensor("fps", [P, 1024], F32) as fps,
          nc.psum_tensor("mps", [P, 512], F32) as mps,
          nc.psum_tensor("spsn", [P, 16], F32) as spsn,
          nc.psum_tensor("tpsm", [16, P], BF16) as tpsm,
          nc.psum_tensor("tpsv", [P, 1024], BF16) as tpsv,
          nc.psum_tensor("nbps", [P, 16], F32) as nbps):
        F = dict(xn=xn, H=H, sco=sco, rt=rt, mb16=mb16, m16=m16, bsel=bsel,
                 nwt=nwt, nbb=nbb, b1c=b1c, b2=b2, ps=fps, mps=mps, sps=spsn,
                 tps=tpsm,
                 w1=wbuf, w2=wbuf)
        for t, is_q in (("k", False), ("v", False), ("q", True)):
            d = wd[t]
            for c in range(NCH):
                S.dma("gpsimd", wbuf[:, c, :], d["w1"].ap()[c * P:(c + 1) * P, :],
                      tag=f"ld_{t}w1")
                S.dma("gpsimd", nwt[:, c, :], d["nwt"].ap()[c * P:(c + 1) * P, :],
                      tag=f"ld_{t}nw")

            def _ldw2(d=d, t=t):
                for c in range(NCH):
                    S.dma("gpsimd", wbuf[:, c, :], d["w2"].ap()[c * P:(c + 1) * P, :],
                          tag=f"ld_{t}w2")
            S.dma("sync", b1c[:, :], d["b1c"].ap(), tag=f"ld_{t}b1")
            S.dma("gpsimd", b2[:, :], d["b2"].ap(), tag=f"ld_{t}b2")
            S.dma("sync", nbrow[:, :], d["nb"].ap(), tag=f"ld_{t}nb")
            S.add("tensor", lambda tt: tt.matmul(nbps[:, :], C["ones_row_f"][:, :],
                                                 nbrow[:, :], start=True, stop=True),
                  reads=[C["ones_row_f"][:, :], nbrow[:, :]], writes=[nbps[:, :]],
                  tag=f"{t}_nbb")
            S.add("vector", lambda v: v.tensor_copy(nbb[:, :], nbps[:, :]),
                  reads=[nbps[:, :]], writes=[nbb[:, :]], tag=f"{t}_nbbc")
            F["out"] = outb
            emit_fff(S, C, F, tag=f"fff_{t}", out_scale=(1.0 / 32.0 if is_q else 1.0),
                     load_w2=_ldw2)
            if t == "k":
                emit_tok_norms(S, C, outb, TOK, statrow[:, 0:16], sqstat, False, "kn")
                S.add("vector", lambda v: v.memset(krow[:, :], 0.0),
                      writes=[krow[:, :]], tag="krow0")
                S.add("vector", lambda v: v.tensor_copy(krow[:, 0:1], statrow[:, 0:1]),
                      reads=[statrow[:, 0:1]], writes=[krow[:, 0:1]], tag="krowc")
                for c in range(NCH):
                    S.dma("sync", k_o.ap()[c * P:(c + 1) * P, :], outb[:, c, :],
                          tag="st_k")
                S.dma("sync", k_o.ap()[1024:1025, :], krow[0:1, :], tag="st_kst")
            elif t == "v":
                S.add("vector", lambda v: v.memset(vstage[:, 1024:1032], 0.0),
                      writes=[vstage[:, 1024:1032]], tag="vst0")
                S.add("vector", lambda v: v.memset(vstage[:, 1024:1025], 1.0),
                      writes=[vstage[:, 1024:1025]], tag="vst1")
                for st in range(NT):
                    for c in range(NCH):
                        S.add("tensor", lambda tt, st=st, c=c: tt.transpose(
                                  tpsv[:, c * P:(c + 1) * P],
                                  outb[:, c, st * P:(st + 1) * P], C["ident"][:, :]),
                              reads=[outb[:, c, st * P:(st + 1) * P], C["ident"][:, :]],
                              writes=[tpsv[:, c * P:(c + 1) * P]], tag=f"vtp{st}_{c}")
                    S.add("vector", lambda v, st=st: v.tensor_copy(
                              vstage[:, 0:1024], tpsv[:, :]),
                          reads=[tpsv[:, :]], writes=[vstage[:, 0:1024]],
                          tag=f"vtc{st}")
                    S.dma("sync", v_o.ap()[st * P:(st + 1) * P, :], vstage[:, :],
                          tag=f"st_v{st}")
            else:
                emit_tok_norms(S, C, outb, TOK, statrow[:, 0:16], sqstat, False, "qn")
                S.dma("sync", qn_o.ap(), statrow[:, :], tag="st_qn")
                for c in range(NCH):
                    S.dma("sync", q_o.ap()[c * P:(c + 1) * P, :], outb[:, c, :],
                          tag="st_q")

    if FC is None:
        S.barrier_read("sync", [q_o.ap(), qn_o.ap(), k_o.ap(), v_o.ap()])
        S.build()
        return nc
    kg = nc.dram_tensor("B1_kg", [8 * 1032, TOK], BF16, kind="Internal")
    vg = nc.dram_tensor("B1_vg", [8 * TOK, 1032], BF16, kind="Internal")
    if not NO_CC:
        S.cc("AllGather", ALU.bypass, GROUPS8, k_o.ap().opt(), kg.ap().opt(), tag="ccK0")
        S.cc("AllGather", ALU.bypass, GROUPS8, v_o.ap().opt(), vg.ap().opt(), tag="ccV0")
    FC["io"].update(B1_q=q_o, B1_qn=qn_o, B1_kg=kg, B1_vg=vg)
    return nc


def pack_B1(inputs, x1, core):
    sl = x1.reshape(NCORE * TOK, D)[core * TOK:(core + 1) * TOK]
    m = {"x_fm": np.ascontiguousarray(sl.T.astype(np.float32))}
    for t in ("q", "k", "v"):
        w1 = np.asarray(inputs[f"L0_ffn_{t}_w1"], np.float32)
        w2 = np.asarray(inputs[f"L0_ffn_{t}_w2"], np.float32)
        nw = np.asarray(inputs[f"L0_ffn_{t}_node_w"], np.float32)
        nb = np.asarray(inputs[f"L0_ffn_{t}_node_b"], np.float32)
        b1 = np.asarray(inputs[f"L0_ffn_{t}_b1"], np.float32)
        b2 = np.asarray(inputs[f"L0_ffn_{t}_b2"], np.float32)
        m[f"{t}_w1"] = np.ascontiguousarray(w1.transpose(1, 0, 2).reshape(D, D))
        m[f"{t}_w2"] = np.ascontiguousarray(w2.reshape(D, D))
        nwt = np.zeros((D, 16), np.float32); nwt[:, 0:15] = nw.T
        m[f"{t}_nwt"] = nwt
        nbp = np.zeros((1, 16), np.float32); nbp[0, 0:15] = nb
        m[f"{t}_nb"] = nbp
        m[f"{t}_b1c"] = chunk_col(b1.reshape(D))
        m[f"{t}_b2"] = np.ascontiguousarray(b2)
    return m


def build_B2(FC=None):
    if FC is None:
        nc, S = _newprog()
        es = None
        x_d = nc.dram_tensor("x_fm", [D, TOK], F32, kind="ExternalInput")
        q_d = nc.dram_tensor("q_fm", [D, TOK], BF16, kind="ExternalInput")
        qn_d = nc.dram_tensor("qn", [1, 16], F32, kind="ExternalInput")
        k_d = nc.dram_tensor("kfull", [4 * 1032, TOK], BF16, kind="ExternalInput")
        v_d = nc.dram_tensor("vfull", [4 * TOK, 1032], BF16, kind="ExternalInput")
        out_d = nc.dram_tensor("x2_fm", [D, TOK], F32, kind="ExternalOutput")
        C = _mk_consts(nc)
        x_res = nc.alloc_sbuf_tensor("x_res", [P, NCH, TOK], F32)
        emit_consts(S, C)
    else:
        nc, S, C, x_res, es = FC["nc"], FC["S"], FC["C"], FC["x_res"], FC["es"]
        q_d = FC["io"]["B1_q"]; qn_d = FC["io"]["B1_qn"]
        k_d = FC["io"]["B1_kg"]; v_d = FC["io"]["B1_vg"]
    if DEBUG_B2:
        dbg_l = nc.dram_tensor("dbg_l", [1, 1024], F32, kind="ExternalOutput")
        dbg_ctx = nc.dram_tensor("dbg_ctx", [P, 1024], F32, kind="ExternalOutput")
        dbg_lb = nc.dram_tensor("dbg_lb", [P, 1024], F32, kind="ExternalOutput")
        dbg_t1 = nc.dram_tensor("dbg_t1", [P, 1024], F32, kind="ExternalOutput")
        dbg_xr = nc.dram_tensor("dbg_xr", [P, 1024], F32, kind="ExternalOutput")
        dbg_x0 = nc.dram_tensor("dbg_x0", [P, 1024], F32, kind="ExternalOutput")

    q_fm = FC["outb"] if FC is not None else nc.alloc_sbuf_tensor("q_sb", [P, NCH, TOK], BF16)
    ctx = AL(nc, es, "B2_ctx", [P, NCH, 1024], F32)
    K_sb = AL(nc, es, "B2_K_sb", [P, NCH, 1024], BF16)
    V_sb = AL(nc, es, "B2_V_sb", [P, NCH, 1032], BF16)
    Pt = AL(nc, es, "B2_Pt", [P, NCH, 1024], BF16)
    rows = AL(nc, es, "B2_rows", [1, 64], F32)
    negc = AL(nc, es, "B2_negc", [P, 8], F32)
    lrow = AL(nc, es, "B2_lrow", [1, 1024], F32)
    linvB = AL(nc, es, "B2_linvB", [P, 1024], F32)
    t1f = AL(nc, es, "B2_t1f", [P, 1024], F32)

    for c in range(NCH):
        if FC is None:
            S.dma("sync", x_res[:, c, :], x_d.ap()[c * P:(c + 1) * P, :], tag="ld_x")
            S.dma("sync", q_fm[:, c, :], q_d.ap()[c * P:(c + 1) * P, :], tag="ld_q")
    if DEBUG_B2:
        S.dma("sync", dbg_x0.ap(), x_res[:, 0, 0:1024], tag="dbg_x0")
    S.dma("sync", rows[0:1, 0:16], qn_d.ap(), tag="ld_qn")
    nrk = 4 if FC is None else 8
    kst = bass.AP(k_d, 1024 * TOK, [[1, 1], [1032 * TOK, nrk]])
    S.dma("gpsimd", rows[0:1, 16:16 + nrk], kst, tag="ld_kst",
          allow_slow_non_contiguous=True)
    S.add("vector", lambda v: v.tensor_reduce(rows[:, 24:25], rows[:, 16:16 + nrk],
                                              AX.X, ALU.max),
          reads=[rows[:, 16:16 + nrk]], writes=[rows[:, 24:25]], tag="kmax")
    emit_shift(S, C, rows[:, 0:1], rows[:, 24:25], 1, negc, rows[:, 32:64], "shift")

    with (nc.psum_tensor("sps", [P, 1024], F32) as sps,
          nc.psum_tensor("sps2", [P, 1024], F32) as sps2,
          nc.psum_tensor("lps", [1, 1024], F32) as lps,
          nc.psum_tensor("cps", [P, 1024], F32) as cps):
        SPS = (sps, sps2)
        for qh in range(2):
            qsl = slice(qh * 1024, qh * 1024 + 1024)
            ng = 0
            pend = [None]

            def load_group(g, qh=qh):
                rank = g // 2
                kcol = (g % 2) * 1024
                for c in range(NCH):
                    if FC is None:
                        kap = k_d.ap()[1032 * rank + c * P:1032 * rank + (c + 1) * P,
                                       kcol:kcol + 1024]
                        vap = v_d.ap()[TOK * rank + kcol + c * P:
                                       TOK * rank + kcol + (c + 1) * P, :]
                        S.dma("sync", K_sb[:, c, :], kap, tag=f"ldK{qh}_{g}")
                        S.dma("sync", V_sb[:, c, :], vap, tag=f"ldV{qh}_{g}")
                    else:
                        kap = bass.AP(k_d, (2 * (1032 * rank + c * P)
                                            + kcol // 1024) * 1024,
                                      [[2048, P], [1, 1024]])
                        vap = bass.AP(v_d, (TOK * rank + kcol + c * P) * 1032,
                                      [[1032, P], [1, 1032]])
                        S.idma(K_sb[:, c, :], kap, FC["bk2"][:, :], tag=f"ldK{qh}_{g}")
                        S.idma(V_sb[:, c, :], vap, FC["bv"][:, :], tag=f"ldV{qh}_{g}")

            def mk_pv(g, qh=qh):
                def emit():
                    for c in range(NCH):
                        def pvmm(tt, c=c):
                            for piece in range(2):
                                psl = cps[:, piece * 512:piece * 512 + 512]
                                for t in range(NCH):
                                    ins = tt.matmul(
                                        psl, V_sb[:, t, c * P:(c + 1) * P],
                                        Pt[:, t, piece * 512:piece * 512 + 512],
                                        start=(t == 0), stop=(t == NCH - 1))
                            return ins
                        S.add("tensor", pvmm,
                              reads=[V_sb[:, :, c * P:(c + 1) * P], Pt[:, :, :]],
                              writes=[cps[:, :]], tag=f"pv{qh}_{g}_{c}")
                        if g == 0:
                            S.add("vector", lambda v, c=c: v.tensor_copy(
                                      ctx[:, c, :], cps[:, :]),
                                  reads=[cps[:, :]], writes=[ctx[:, c, :]],
                                  tag=f"fl{qh}_{g}_{c}")
                        else:
                            S.add("vector", lambda v, c=c: v.tensor_add(
                                      ctx[:, c, :], ctx[:, c, :], cps[:, :]),
                                  reads=[cps[:, :], ctx[:, c, :]],
                                  writes=[ctx[:, c, :]], tag=f"fl{qh}_{g}_{c}")
                return emit

            def emit_qk(g, t, qh=qh):
                sp = SPS[t % 2]

                def qkmm(tt, t=t, qh=qh, sp=sp):
                    for piece in range(2):
                        psl = sp[:, piece * 512:piece * 512 + 512]
                        for c in range(NCH):
                            ins = tt.matmul(
                                psl, K_sb[:, c, t * P:(t + 1) * P],
                                q_fm[:, c, qh * 1024 + piece * 512:
                                     qh * 1024 + piece * 512 + 512],
                                start=(c == 0), stop=(c == NCH - 1))
                    return ins
                S.add("tensor", qkmm,
                      reads=[K_sb[:, :, t * P:(t + 1) * P], q_fm[:, :, qsl]],
                      writes=[sp[:, :]], tag=f"qk{qh}_{g}_{t}")

            def emit_expl(g, t, ng_, qh=qh):
                sp = SPS[t % 2]
                S.add("scalar", lambda s, t=t, sp=sp: s.activation(
                          Pt[:, t, :], sp[:, :], AF.Exp, bias=negc[:, 0:1]),
                      reads=[sp[:, :], negc[:, 0:1]], writes=[Pt[:, t, :]],
                      tag=f"exp{qh}_{g}_{t}")
                def lmm(tt, t=t, ng_=ng_):
                    for piece in range(2):
                        ins = tt.matmul(lps[:, piece * 512:piece * 512 + 512],
                                        C["ones_bf"][:, :],
                                        Pt[:, t, piece * 512:piece * 512 + 512],
                                        start=(ng_ == 0), stop=(ng_ == 63))
                    return ins
                S.add("tensor", lmm, reads=[C["ones_bf"][:, :], Pt[:, t, :]],
                      writes=[lps[:, :]], tag=f"l{qh}_{g}_{t}")

            load_group(0)
            for g in range(8):
                if g + 1 < 8:
                    load_group(g + 1)
                emit_qk(g, 0)
                emit_qk(g, 1)
                if pend[0] is not None:
                    pend[0]()
                emit_expl(g, 0, ng)
                for t in range(2, NCH):
                    emit_qk(g, t)
                    emit_expl(g, t - 1, ng + t - 1)
                emit_expl(g, NCH - 1, ng + NCH - 1)
                ng += NCH
                pend[0] = mk_pv(g)
            pend[0]()
            pend[0] = None
            S.add("vector", lambda v: v.tensor_scalar_add(lrow[:, :], lps[:, :], 1e-30),
                  reads=[lps[:, :]], writes=[lrow[:, :]], tag=f"lc{qh}")
            S.add("vector", lambda v: v.reciprocal(lrow[:, :], lrow[:, :]),
                  reads=[lrow[:, :]], writes=[lrow[:, :]], tag=f"li{qh}")
            for p0 in (0, 512):
                S.add("tensor", lambda t, p0=p0: t.matmul(
                          cps[:, p0:p0 + 512], C["ones_row_f"][:, :],
                          lrow[:, p0:p0 + 512], start=True, stop=True),
                      reads=[C["ones_row_f"][:, :], lrow[:, :]],
                      writes=[cps[:, p0:p0 + 512]], tag=f"lb{qh}")
            S.add("vector", lambda v: v.tensor_copy(linvB[:, :], cps[:, :]),
                  reads=[cps[:, :]], writes=[linvB[:, :]], tag=f"lbc{qh}")
            if qh == 0 and DEBUG_B2:
                S.dma("sync", dbg_lb.ap(), linvB[:, :], tag="dbg_lb")
            if qh == 0 and DEBUG_B2:
                S.dma("sync", dbg_l.ap(), lrow[:, :], tag="dbg_l")
                S.dma("sync", dbg_ctx.ap(), ctx[:, 0, :], tag="dbg_ctx")
            for c in range(NCH):
                S.add("vector", lambda v, c=c: v.tensor_mul(t1f[:, :], ctx[:, c, :],
                                                            linvB[:, :]),
                      reads=[ctx[:, c, :], linvB[:, :]], writes=[t1f[:, :]],
                      tag=f"nm{qh}_{c}")
                if qh == 0 and c == 0 and DEBUG_B2:
                    S.dma("sync", dbg_t1.ap(), t1f[:, :], tag="dbg_t1")
                S.add("vector", lambda v, c=c, qsl=qsl: v.tensor_add(
                          x_res[:, c, qsl], x_res[:, c, qsl], t1f[:, :]),
                      reads=[t1f[:, :], x_res[:, c, qsl]], writes=[x_res[:, c, qsl]],
                      tag=f"ra{qh}_{c}")
                if qh == 0 and c == 0 and DEBUG_B2:
                    S.dma("sync", dbg_xr.ap(), x_res[:, 0, 0:1024], tag="dbg_xr")
    if FC is None:
        for c in range(NCH):
            S.dma("sync", out_d.ap()[c * P:(c + 1) * P, :], x_res[:, c, :], tag="st_x")
        S.barrier_read("sync", [out_d.ap()])
        S.build()
    return nc


def emit_proj_generic(S, C, wbuf, src_fn, dst_fn, ps, nchunks_in, width, act, bias_col,
                      scale, tag):
    npc = (width + 511) // 512
    for o in range(NCH):
        for piece in range(npc):
            w = min(512, width - piece * 512)
            sl = slice(piece * 512, piece * 512 + w)
            pb = ps[:, ((o * npc + piece) % 2) * 512:((o * npc + piece) % 2) * 512 + w]
            ni = nchunks_in

            def mmgroup(t, o=o, sl=sl, pb=pb, src_fn=src_fn, ni=ni, wbuf=wbuf):
                for i in range(ni):
                    ins = t.matmul(pb, wbuf[:, i, o * P:(o + 1) * P],
                                   src_fn(i)[:, sl],
                                   start=(i == 0), stop=(i == ni - 1))
                return ins
            S.add("tensor", mmgroup,
                  reads=[wbuf[:, i, o * P:(o + 1) * P] for i in range(ni)] +
                        [src_fn(i)[:, sl] for i in range(ni)],
                  writes=[pb], tag=f"{tag}_mm{o}_{piece}")
            b = bias_col(o) if bias_col is not None else 0.0
            dsl = dst_fn(o)[:, sl]
            S.add("scalar", lambda s, dsl=dsl, pb=pb, b=b, act=act, scale=scale:
                  s.activation(dsl, pb, act, bias=b, scale=scale),
                  reads=[pb] + ([b] if not isinstance(b, float) else []),
                  writes=[dsl], tag=f"{tag}_ev{o}_{piece}")


def build_C1(FC=None):
    if FC is None:
        nc, S = _newprog()
        es = None
        x_d = nc.dram_tensor("x_fm", [D, TOK], F32, kind="ExternalInput")
        C = _mk_consts(nc)
        x_res = nc.alloc_sbuf_tensor("x_res", [P, NCH, TOK], F32)
        emit_consts(S, C)
        for c in range(NCH):
            S.dma("sync", x_res[:, c, :], x_d.ap()[c * P:(c + 1) * P, :], tag="ld_x")
        okind = "ExternalOutput"
    else:
        nc, S, C, x_res, es = FC["nc"], FC["S"], FC["C"], FC["x_res"], FC["es"]
        okind = "Internal"
    past_d = nc.dram_tensor("past_fm", [D, 512], F32, kind="ExternalInput")
    wds = {}
    for t in ("q", "k", "v", "rk", "rv"):
        wds[t] = (nc.dram_tensor(f"W{t}", [D, D], F32, kind="ExternalInput"),
                  nc.dram_tensor(f"b{t}c", [P, NCH], F32, kind="ExternalInput"))
    q_o = nc.dram_tensor("C1_q_o", [D, TOK], BF16, kind=okind)
    qn_o = nc.dram_tensor("C1_qn_o", [1, 16], F32, kind=okind)
    k_o = nc.dram_tensor("C1_k_o", [1032, TOK], BF16, kind=okind)
    v_o = nc.dram_tensor("C1_v_o", [TOK, 1040], BF16, kind=okind)
    rk_o = nc.dram_tensor("C1_rk_o", [1032, 512], BF16, kind=okind)
    rv_o = nc.dram_tensor("C1_rv_o", [512, 1040], BF16, kind=okind)

    if FC is None:
        xv = x_res.bitcast(BF16)
        xn = xv[:, :, 0:TOK]
        outb = nc.alloc_sbuf_tensor("outb", [P, NCH, TOK], BF16)
    else:
        xn = AL(nc, es, "C1_xn", [P, NCH, TOK], BF16)
        outb = FC["outb"]
    H = AL(nc, es, "C1_H", [P, NCH, TOK], BF16)
    Hf = H.bitcast(F32).rearrange("p c w -> p (c w)")
    wbuf = AL(nc, es, "C1_wbuf", [P, NCH, 1024], BF16)
    SCRF = AL(nc, es, "C1_SCRF", [P, TOK], F32)
    past_sb = AL(nc, es, "C1_past", [P, NCH, 512], BF16)
    bcol = AL(nc, es, "C1_bcol", [P, NCH], F32)
    statrow = AL(nc, es, "C1_statrow", [1, 16], F32)
    sqstat = AL(nc, es, "C1_sqstat", [P, 512], BF16)
    krow = AL(nc, es, "C1_krow", [1, TOK], BF16)
    vstage = AL(nc, es, "C1_vstage", [P, 1040], BF16)

    for c in range(NCH):
        S.dma("gpsimd", past_sb[:, c, :], past_d.ap()[c * P:(c + 1) * P, :], tag="ld_p")
    lnscr = dict(xb=H[:, 0:2, :], sq=H[:, 2:4, :],
                 mrow=Hf[0:1, 0:TOK], wrow=Hf[0:1, TOK:2 * TOK],
                 mb=H[:, 4, :], ab=H[:, 5, :], t1=SCRF[:, :])
    emit_ln(S, C, x_res, xn, lnscr, tag="ln3")

    def stage_v(src, width, v_out, nt, tag):
        S.add("vector", lambda v: v.memset(vstage[:, :].rearrange(
                  "p (h w) -> p h w", w=65)[:, :, 64:65], 1.0),
              writes=[vstage[:, :]], tag=f"{tag}_ones")
        with nc.psum_tensor(f"{tag}_tps", [P, 1024], BF16) as tps:
            for st in range(nt):
                for c in range(NCH):
                    S.add("tensor", lambda tt, st=st, c=c: tt.transpose(
                              tps[:, c * P:(c + 1) * P], src[:, c, st * P:(st + 1) * P],
                              C["ident"][:, :]),
                          reads=[src[:, c, st * P:(st + 1) * P], C["ident"][:, :]],
                          writes=[tps[:, c * P:(c + 1) * P]], tag=f"{tag}_t{st}_{c}")
                S.add("vector", lambda v, st=st: v.tensor_copy(
                          vstage[:, :].rearrange("p (h w) -> p h w", w=65)[:, :, 0:64],
                          tps[:, :].rearrange("p (h w) -> p h w", w=64)),
                      reads=[tps[:, :]], writes=[vstage[:, :]], tag=f"{tag}_c{st}")
                S.dma("sync", v_out.ap()[st * P:(st + 1) * P, :], vstage[:, :],
                      tag=f"{tag}_st{st}")

    with nc.psum_tensor("C1_pps", [P, 1024], F32) as pps:
        for t, src_fn, n_in, width, outdst in (
                ("k", lambda i: xn[:, i, :], NCH, TOK, None),
                ("v", lambda i: xn[:, i, :], NCH, TOK, None),
                ("rk", lambda i: past_sb[:, i, :], NCH, 512, None),
                ("rv", lambda i: past_sb[:, i, :], NCH, 512, None),
                ("q", lambda i: xn[:, i, :], NCH, TOK, None)):
            w_d, b_d = wds[t]
            for c in range(NCH):
                S.dma("gpsimd", wbuf[:, c, :], w_d.ap()[c * P:(c + 1) * P, :],
                      tag=f"ld_w{t}")
            S.dma("sync", bcol[:, :], b_d.ap(), tag=f"ld_b{t}")
            scale = 1.0 / 8.0 if t in ("q",) else 1.0
            if t == "q":
                S.add("vector", lambda v: v.tensor_scalar_mul(bcol[:, :], bcol[:, :],
                                                              1.0 / 8.0),
                      reads=[bcol[:, :]], writes=[bcol[:, :]], tag="bq8")
            dstbuf = outb
            emit_proj_generic(S, C, wbuf, src_fn,
                              lambda o: dstbuf[:, o, 0:width], pps, n_in, width,
                              AF.Identity, lambda o: bcol[:, o:o + 1], scale,
                              tag=f"pj_{t}")
            if t == "q":
                emit_tok_norms(S, C, outb, TOK, statrow[:, 0:16], sqstat, True, "c1qn")
                S.dma("sync", qn_o.ap(), statrow[:, :], tag="st_qn")
                for c in range(NCH):
                    S.dma("sync", q_o.ap()[c * P:(c + 1) * P, :], outb[:, c, :],
                          tag="st_q")
            elif t in ("k", "rk"):
                wd_ = TOK if t == "k" else 512
                out_t = k_o if t == "k" else rk_o
                emit_tok_norms(S, C, outb[:, :, 0:wd_] if wd_ != TOK else outb,
                               wd_, statrow[:, 0:16], sqstat, True, f"c1{t}n")
                S.add("vector", lambda v, wd_=wd_: v.memset(krow[:, 0:wd_], 0.0),
                      writes=[krow[:, :]], tag=f"{t}row0")
                S.add("vector", lambda v: v.tensor_copy(krow[:, 0:8], statrow[:, 0:8]),
                      reads=[statrow[:, 0:8]], writes=[krow[:, 0:8]], tag=f"{t}rowc")
                for c in range(NCH):
                    S.dma("sync", out_t.ap()[c * P:(c + 1) * P, :], outb[:, c, 0:wd_],
                          tag=f"st_{t}")
                S.dma("sync", out_t.ap()[1024:1025, :], krow[0:1, 0:wd_], tag=f"st_{t}s")
            elif t == "v":
                stage_v(outb, TOK, v_o, NT, "vst")
            elif t == "rv":
                stage_v(outb[:, :, 0:512].rearrange("p c w -> p c w"), 512, rv_o, 4,
                        "rvst")

    if FC is None:
        S.barrier_read("sync", [q_o.ap(), qn_o.ap(), k_o.ap(), v_o.ap(), rk_o.ap(),
                                rv_o.ap()])
        S.build()
        return nc
    kg = nc.dram_tensor("C1_kg", [8 * 1032, TOK], BF16, kind="Internal")
    vg = nc.dram_tensor("C1_vg", [8 * TOK, 1040], BF16, kind="Internal")
    rkg = nc.dram_tensor("C1_rkg", [8 * 1032, 512], BF16, kind="Internal")
    rvg = nc.dram_tensor("C1_rvg", [8 * 512, 1040], BF16, kind="Internal")
    S.cc("AllGather", ALU.bypass, GROUPS8, k_o.ap().opt(), kg.ap().opt(), tag="ccK1")
    S.cc("AllGather", ALU.bypass, GROUPS8, v_o.ap().opt(), vg.ap().opt(), tag="ccV1")
    S.cc("AllGather", ALU.bypass, GROUPS8, rk_o.ap().opt(), rkg.ap().opt(), tag="ccRK")
    S.cc("AllGather", ALU.bypass, GROUPS8, rv_o.ap().opt(), rvg.ap().opt(), tag="ccRV")
    FC["io"].update(C1_q=q_o, C1_qn=qn_o, C1_kg=kg, C1_vg=vg, C1_rkg=rkg, C1_rvg=rvg)
    return nc


def pack_C1(inputs, x2, core):
    sl = x2.reshape(NCORE * TOK, D)[core * TOK:(core + 1) * TOK]
    past = np.asarray(inputs["L1_attn_past_states"], np.float32).reshape(4096, D)
    psl = past[core * 512:(core + 1) * 512]
    m = {"x_fm": np.ascontiguousarray(sl.T.astype(np.float32)),
         "past_fm": np.ascontiguousarray(psl.T)}
    for t, wk, bk in (("q", "Wq", "bq"), ("k", "Wk", "bk"), ("v", "Wv", "bv"),
                      ("rk", "Wrk", "brk"), ("rv", "Wrv", "brv")):
        m[f"W{t}"] = np.ascontiguousarray(np.asarray(inputs[f"L1_attn_{wk}"], np.float32))
        m[f"b{t}c"] = chunk_col(inputs[f"L1_attn_{bk}"])
    return m


def build_C2(FC=None):
    if FC is None:
        nc, S = _newprog()
        es = None
        x_d = nc.dram_tensor("x_fm", [D, TOK], F32, kind="ExternalInput")
        q_d = nc.dram_tensor("q_fm", [D, TOK], BF16, kind="ExternalInput")
        qn_d = nc.dram_tensor("qn", [1, 16], F32, kind="ExternalInput")
        k_d = nc.dram_tensor("kfull", [4 * 1032, TOK], BF16, kind="ExternalInput")
        v_d = nc.dram_tensor("vfull", [4 * TOK, 1040], BF16, kind="ExternalInput")
        rk_d = nc.dram_tensor("rkfull", [8 * 1032, 512], BF16, kind="ExternalInput")
        rv_d = nc.dram_tensor("rvfull", [8 * 512, 1040], BF16, kind="ExternalInput")
        out_d = nc.dram_tensor("x3_fm", [D, TOK], F32, kind="ExternalOutput")
        C = _mk_consts(nc)
        emit_consts(S, C)
        x_res = None
    else:
        nc, S, C, x_res, es = FC["nc"], FC["S"], FC["C"], FC["x_res"], FC["es"]
        q_d = FC["io"]["C1_q"]; qn_d = FC["io"]["C1_qn"]
        k_d = FC["io"]["C1_kg"]; v_d = FC["io"]["C1_vg"]
        rk_d = FC["io"]["C1_rkg"]; rv_d = FC["io"]["C1_rvg"]
    wo_d = nc.dram_tensor("Wo", [D, D], F32, kind="ExternalInput")
    boc_d = nc.dram_tensor("boc", [P, NCH], F32, kind="ExternalInput")
    wrq_d = nc.dram_tensor("Wrq", [D, D], F32, kind="ExternalInput")
    brqc_d = nc.dram_tensor("brqc", [P, NCH], F32, kind="ExternalInput")
    wgA_d = nc.dram_tensor("WgA", [D, D], F32, kind="ExternalInput")
    wgB_d = nc.dram_tensor("WgB", [D, D], F32, kind="ExternalInput")
    bgc_d = nc.dram_tensor("bgc", [P, NCH], F32, kind="ExternalInput")

    q_fm = FC["outb"] if FC is not None else nc.alloc_sbuf_tensor("q_sb", [P, NCH, TOK], BF16)
    nctx = AL(nc, es, "C2_nctx", [P, NCH, TOK], BF16)
    out_fm = AL(nc, es, "C2_out_fm", [P, NCH, TOK], BF16)
    rec_fm = q_fm
    gate_fm = nctx
    war = AL(nc, es, "C2_war", [P, 16384], BF16)
    wbuf = war[:, 0:8192].rearrange("p (c w) -> p c w", w=1024)
    wbuf2 = war[:, 8192:16384].rearrange("p (c w) -> p c w", w=1024)
    K_sb = war[:, 0:2048].rearrange("p (s w) -> p s w", w=1024)
    V_sb = war[:, 2048:4128].rearrange("p (s t w) -> p s t w", t=8, w=130)
    Pt = war[:, 4128:8224].rearrange("p (s w) -> p s w", w=2048)
    shF = war[:, 8224:12320].bitcast(F32)
    ctxAn = war[0:64, 12320:13344]
    ctxBn = war[0:64, 13344:14368]
    onesF = AL(nc, es, "C2_onesF", [P, P], F32)
    Ilo = AL(nc, es, "C2_Ilo", [64, P], BF16)
    Ihi = AL(nc, es, "C2_Ihi", [64, P], BF16)
    ishft = AL(nc, es, "C2_ishft", [64, P], F32)
    rows = AL(nc, es, "C2_rows_sb", [1, 192], F32)
    negc_s = AL(nc, es, "C2_negc_s", [P, 8], F32)
    negc_r = AL(nc, es, "C2_negc_r", [P, 8], F32)
    bcol = AL(nc, es, "C2_bcol", [P, NCH], F32)
    bcol2 = AL(nc, es, "C2_bcol2", [P, NCH], F32)
    sqstat = AL(nc, es, "C2_sqstat", [P, 512], BF16)
    statrow = AL(nc, es, "C2_statrow", [1, 16], F32)
    t1f = AL(nc, es, "C2_t1f", [P, 512], F32)
    t2f = AL(nc, es, "C2_t2f", [P, 512], BF16)

    S.add("gpsimd", lambda g: g.memset(onesF[:, :], 1.0), writes=[onesF[:, :]])
    S.add("gpsimd", lambda g: g.iota(ishft[:, :], [[1, P]], base=0,
                                     channel_multiplier=-1,
                                     allow_small_or_imprecise_dtypes=True),
          writes=[ishft[:, :]], tag="ilo_i")
    S.add("vector", lambda v: v.tensor_single_scalar(Ilo[:, :], ishft[:, :], 0.0,
                                                     ALU.is_equal),
          reads=[ishft[:, :]], writes=[Ilo[:, :]], tag="ilo")
    S.add("vector", lambda v: v.tensor_single_scalar(Ihi[:, :], ishft[:, :], 64.0,
                                                     ALU.is_equal),
          reads=[ishft[:, :]], writes=[Ihi[:, :]], tag="ihi")

    if FC is None:
        xs = AL(nc, es, "C2_xs", [P, 2, 512], F32)
        ys = AL(nc, es, "C2_ys", [P, 2, 512], F32)
    for c in range(NCH):
        S.dma("sync", q_fm[:, c, :], q_d.ap()[c * P:(c + 1) * P, :], tag="ld_q")
    if DEBUG_B2:
        S.dma("sync", dbg_x0.ap(), x_res[:, 0, 0:1024], tag="dbg_x0")
    S.dma("sync", rows[0:1, 0:16], qn_d.ap(), tag="ld_qn")
    nrk = 4 if FC is None else 8
    kst = bass.AP(k_d, 1024 * TOK, [[1, 1], [1032 * TOK, nrk], [1, 8]])
    S.dma("gpsimd", rows[0:1, 16:16 + 8 * nrk], kst, tag="ld_kst",
          allow_slow_non_contiguous=True)
    S.add("vector", lambda v: v.tensor_reduce(
              rows[:, 80:88],
              rows[:, 16:16 + 8 * nrk].rearrange("o (r c) -> o c r", c=8),
              AX.X, ALU.max),
          reads=[rows[:, 16:16 + 8 * nrk]], writes=[rows[:, 80:88]], tag="knmax")
    emit_shift(S, C, rows[:, 0:8], rows[:, 80:88], 8, negc_s, rows[:, 96:160], "shs")

    def attn_pass(tag, q_src, negc, ranks, kwidth, kd, vd, ngroups_per_rank, ntile,
                  epilogue):
        with (nc.psum_tensor(f"{tag}_sps", [P, 2048], F32) as sps,
              nc.psum_tensor(f"{tag}_cA", [65, 1024], F32) as cA,
              nc.psum_tensor(f"{tag}_cB", [65, 1024], F32) as cB):
            for qh in range(2):
                for cp in range(NCH):
                    groups = [(rk, gg) for rk in ranks
                              for gg in range(ngroups_per_rank)]
                    nt_total = len(groups) * ntile
                    klen = kwidth if ngroups_per_rank == 1 else 1024
                    vw = kwidth if ngroups_per_rank == 1 else TOK

                    def load_group(gi, qh=qh, cp=cp, klen=klen, vw=vw, groups=groups):
                        rk, gg = groups[gi]
                        slot = gi % 2
                        kcol = gg * 1024
                        if FC is None or tag != "sa":
                            kap = kd.ap()[1032 * rk + cp * P:1032 * rk + (cp + 1) * P,
                                          kcol:kcol + klen]
                            S.dma("sync", K_sb[:, slot, 0:klen], kap,
                                  tag=f"{tag}K{qh}_{cp}_{gi}")
                        else:
                            kap = bass.AP(kd, (2 * (1032 * rk + cp * P)
                                               + kcol // 1024) * 1024,
                                          [[2048, P], [1, 1024]])
                            S.idma(K_sb[:, slot, 0:klen], kap, FC["bk2"][:, :],
                                   tag=f"{tag}K{qh}_{cp}_{gi}")
                        for t in range(ntile):
                            if FC is None or tag != "sa":
                                vap = vd.ap()[vw * rk + kcol + t * P:
                                              vw * rk + kcol + (t + 1) * P,
                                              130 * cp:130 * cp + 130]
                                S.dma("sync", V_sb[:, slot, t, :], vap,
                                      tag=f"{tag}V{qh}_{cp}_{gi}_{t}")
                            else:
                                vap = bass.AP(
                                    vd, ((TOK * rk + kcol + t * P) * 8 + cp) * 130,
                                    [[8 * 130, P], [1, 130]])
                                S.idma(V_sb[:, slot, t, :], vap, FC["bv8"][:, :],
                                       tag=f"{tag}V{qh}_{cp}_{gi}_{t}")

                    load_group(0)
                    pend = [None]
                    it = 0
                    for gi in range(len(groups)):
                        slot = gi % 2
                        if gi + 1 < len(groups):
                            load_group(gi + 1)
                        for t in range(ntile):
                            ps_ = it % 2

                            def qkmmA(tt, t=t, slot=slot, qh=qh, cp=cp):
                                for piece in range(2):
                                    qpc = slice(qh * 1024 + piece * 512,
                                                qh * 1024 + piece * 512 + 512)
                                    ins = tt.matmul(
                                        sps[:, piece * 512:piece * 512 + 512],
                                        K_sb[0:64, slot, t * P:(t + 1) * P],
                                        q_src[0:64, cp, qpc],
                                        start=True, stop=True)
                                return ins

                            def qkmmB(tt, t=t, slot=slot, qh=qh, cp=cp):
                                for piece in range(2):
                                    qpc = slice(qh * 1024 + piece * 512,
                                                qh * 1024 + piece * 512 + 512)
                                    ins = tt.matmul(
                                        sps[:, 1024 + piece * 512:
                                            1024 + piece * 512 + 512],
                                        K_sb[64:128, slot, t * P:(t + 1) * P],
                                        q_src[64:128, cp, qpc],
                                        start=True, stop=True,
                                        tile_position=(64, 0))
                                return ins
                            S.add("tensor", qkmmA,
                                  reads=[K_sb[0:64, slot, t * P:(t + 1) * P],
                                         q_src[0:64, cp,
                                               qh * 1024:qh * 1024 + 1024]],
                                  writes=[sps[:, 0:1024]], tag=f"{tag}qkA")
                            S.add("tensor", qkmmB,
                                  reads=[K_sb[64:128, slot, t * P:(t + 1) * P],
                                         q_src[64:128, cp,
                                               qh * 1024:qh * 1024 + 1024]],
                                  writes=[sps[:, 1024:2048]], tag=f"{tag}qkB")
                            if pend[0] is not None:
                                pend[0]()
                            S.add("scalar", lambda s, cp=cp, ps_=ps_: s.activation(
                                      Pt[:, ps_, 0:1024], sps[:, 0:1024], AF.Exp,
                                      bias=negc[:, cp:cp + 1]),
                                  reads=[sps[:, 0:1024], negc[:, cp:cp + 1]],
                                  writes=[Pt[:, ps_, 0:1024]], tag=f"{tag}expA")
                            S.add("scalar", lambda s, cp=cp, ps_=ps_: s.activation(
                                      Pt[:, ps_, 1024:2048], sps[:, 1024:2048], AF.Exp,
                                      bias=negc[:, cp:cp + 1]),
                                  reads=[sps[:, 1024:2048], negc[:, cp:cp + 1]],
                                  writes=[Pt[:, ps_, 1024:2048]], tag=f"{tag}expB")

                            def mk_pv(it=it, ps_=ps_, slot=slot, t=t,
                                      nt_total=nt_total):
                                def emit():
                                    def pvmm(tt):
                                        for piece in range(2):
                                            tt.matmul(
                                                cA[:, piece * 512:piece * 512 + 512],
                                                V_sb[:, slot, t, 0:65],
                                                Pt[:, ps_, piece * 512:
                                                   piece * 512 + 512],
                                                start=(it == 0),
                                                stop=(it == nt_total - 1))
                                        for piece in range(2):
                                            ins = tt.matmul(
                                                cB[:, piece * 512:piece * 512 + 512],
                                                V_sb[:, slot, t, 65:130],
                                                Pt[:, ps_, 1024 + piece * 512:
                                                   1024 + piece * 512 + 512],
                                                start=(it == 0),
                                                stop=(it == nt_total - 1))
                                        return ins
                                    S.add("tensor", pvmm,
                                          reads=[V_sb[:, slot, t, :],
                                                 Pt[:, ps_, :]],
                                          writes=[cA[:, :], cB[:, :]],
                                          tag=f"{tag}pv")
                                return emit
                            pend[0] = mk_pv()
                            it += 1
                    pend[0]()
                    pend[0] = None
                    lA = shF[64:65, 0:1024]
                    lB = shF[64:65, 1024:2048]
                    S.add("vector", lambda v: v.tensor_scalar_add(lA, cA[64:65, :],
                                                                  1e-30),
                          reads=[cA[:, :]], writes=[lA], tag=f"{tag}la")
                    S.add("vector", lambda v: v.tensor_scalar_add(lB, cB[64:65, :],
                                                                  1e-30),
                          reads=[cB[:, :]], writes=[lB], tag=f"{tag}lb")
                    S.add("vector", lambda v: v.reciprocal(lA, lA), reads=[lA],
                          writes=[lA], tag=f"{tag}lra")
                    S.add("vector", lambda v: v.reciprocal(lB, lB), reads=[lB],
                          writes=[lB], tag=f"{tag}lrb")
                    def bcmm(t):
                        for p0 in (0, 512):
                            t.matmul(sps[0:64, p0:p0 + 512], onesF[64:65, 0:64],
                                     lA[:, p0:p0 + 512], start=True, stop=True)
                            ins = t.matmul(sps[0:64, 1024 + p0:1024 + p0 + 512],
                                           onesF[64:65, 0:64], lB[:, p0:p0 + 512],
                                           start=True, stop=True)
                        return ins
                    S.add("tensor", bcmm, reads=[onesF[:, :], lA, lB],
                          writes=[sps[:, :]], tag=f"{tag}bc")
                    S.add("vector", lambda v: v.tensor_copy(shF[0:64, :],
                                                            sps[0:64, :]),
                          reads=[sps[0:64, :]], writes=[shF[0:64, :]], tag=f"{tag}lvc")
                    S.add("vector", lambda v: v.tensor_mul(ctxAn[:, :], cA[0:64, :],
                                                           shF[0:64, 0:1024]),
                          reads=[cA[:, :], shF[0:64, :]], writes=[ctxAn[:, :]],
                          tag=f"{tag}na")
                    S.add("vector", lambda v: v.tensor_mul(ctxBn[:, :], cB[0:64, :],
                                                           shF[0:64, 1024:2048]),
                          reads=[cB[:, :], shF[0:64, :]], writes=[ctxBn[:, :]],
                          tag=f"{tag}nb")
                    def pkmm(t):
                        for p0 in (0, 512):
                            t.matmul(sps[:, p0:p0 + 512], Ilo[:, :],
                                     ctxAn[:, p0:p0 + 512], start=True, stop=False)
                            ins = t.matmul(sps[:, p0:p0 + 512], Ihi[:, :],
                                           ctxBn[:, p0:p0 + 512], start=False,
                                           stop=True)
                        return ins
                    S.add("tensor", pkmm,
                          reads=[Ilo[:, :], Ihi[:, :], ctxAn[:, :], ctxBn[:, :]],
                          writes=[sps[:, 0:1024]], tag=f"{tag}pk")
                    epilogue(qh, cp, sps)
    def self_epi(qh, cp, sps):
        S.add("vector", lambda v, qh=qh, cp=cp: v.tensor_copy(
                  nctx[:, cp, qh * 1024:qh * 1024 + 1024], sps[:, 0:1024]),
              reads=[sps[:, 0:1024]], writes=[nctx[:, cp, qh * 1024:qh * 1024 + 1024]],
              tag=f"sepi{qh}_{cp}")
    attn_pass("sa", q_fm, negc_s, [0, 1, 2, 3], TOK, k_d, v_d, 2, 8, self_epi)

    for c in range(NCH):
        S.dma("gpsimd", wbuf[:, c, :], wo_d.ap()[c * P:(c + 1) * P, :], tag="ld_wo")
    S.dma("sync", bcol[:, :], boc_d.ap(), tag="ld_bo")
    with nc.psum_tensor("C2_pps", [P, 1024], F32) as pps:
        emit_proj_generic(S, C, wbuf, lambda i: nctx[:, i, :],
                          lambda o: out_fm[:, o, :], pps, NCH, TOK, AF.Identity,
                          lambda o: bcol[:, o:o + 1], 1.0, tag="wo")
        for c in range(NCH):
            S.dma("gpsimd", wbuf[:, c, :], wrq_d.ap()[c * P:(c + 1) * P, :],
                  tag="ld_wrq")
        S.dma("sync", bcol2[:, :], brqc_d.ap(), tag="ld_brq")
        S.add("vector", lambda v: v.tensor_scalar_mul(bcol2[:, :], bcol2[:, :],
                                                      1.0 / 8.0),
              reads=[bcol2[:, :]], writes=[bcol2[:, :]], tag="brq8")
        emit_proj_generic(S, C, wbuf, lambda i: out_fm[:, i, :],
                          lambda o: nctx[:, o, :], pps, NCH, TOK, AF.Identity,
                          lambda o: bcol2[:, o:o + 1], 1.0 / 8.0, tag="wrq")
    emit_tok_norms(S, C, nctx, TOK, statrow[:, 0:16], sqstat, True, "rqn")
    rkst = bass.AP(rk_d, 1024 * 512, [[1, 1], [1032 * 512, 8], [1, 8]])
    S.dma("gpsimd", rows[0:1, 16:80], rkst, tag="ld_rkst",
          allow_slow_non_contiguous=True)
    S.add("vector", lambda v: v.tensor_reduce(
              rows[:, 80:88], rows[:, 16:80].rearrange("o (r c) -> o c r", c=8),
              AX.X, ALU.max),
          reads=[rows[:, 16:80]], writes=[rows[:, 80:88]], tag="rknmax")
    emit_shift(S, C, statrow[:, 0:8], rows[:, 80:88], 8, negc_r, rows[:, 96:160],
               "shr")

    for p2 in range(2):
        def rec_epi(qh, cp, sps, p2=p2):
            if p2 == 0:
                S.add("vector", lambda v, qh=qh, cp=cp: v.tensor_scalar_mul(
                          rec_fm[:, cp, qh * 1024:qh * 1024 + 1024], sps[:, 0:1024],
                          0.5),
                      reads=[sps[:, 0:1024]],
                      writes=[rec_fm[:, cp, qh * 1024:qh * 1024 + 1024]],
                      tag=f"repi{p2}_{qh}_{cp}")
            else:
                S.add("vector", lambda v, qh=qh, cp=cp: v.scalar_tensor_tensor(
                          rec_fm[:, cp, qh * 1024:qh * 1024 + 1024], sps[:, 0:1024],
                          0.5, rec_fm[:, cp, qh * 1024:qh * 1024 + 1024],
                          ALU.mult, ALU.add),
                      reads=[sps[:, 0:1024],
                             rec_fm[:, cp, qh * 1024:qh * 1024 + 1024]],
                      writes=[rec_fm[:, cp, qh * 1024:qh * 1024 + 1024]],
                      tag=f"repi{p2}_{qh}_{cp}")
        attn_pass(f"ra{p2}", nctx, negc_r, [4 * p2 + i for i in range(4)], 512,
                  rk_d, rv_d, 1, 4, rec_epi)

    for c in range(NCH):
        S.dma("gpsimd", wbuf[:, c, :], wgA_d.ap()[c * P:(c + 1) * P, :], tag="ld_wga")
        S.dma("gpsimd", wbuf2[:, c, :], wgB_d.ap()[c * P:(c + 1) * P, :], tag="ld_wgb")
    S.dma("sync", bcol[:, :], bgc_d.ap(), tag="ld_bg")
    with nc.psum_tensor("C2_gps", [P, 1024], F32) as gps:
        for o in range(NCH):
            for piece in range(4):
                sl = slice(piece * 512, piece * 512 + 512)
                pb = gps[:, ((o * 4 + piece) % 2) * 512:((o * 4 + piece) % 2) * 512 + 512]
                def gmm(t, o=o, sl=sl, pb=pb):
                    for i in range(NCH):
                        t.matmul(pb, wbuf[:, i, o * P:(o + 1) * P],
                                 out_fm[:, i, sl], start=(i == 0), stop=False)
                    for i in range(NCH):
                        ins = t.matmul(pb, wbuf2[:, i, o * P:(o + 1) * P],
                                       rec_fm[:, i, sl],
                                       start=False, stop=(i == NCH - 1))
                    return ins
                S.add("tensor", gmm,
                      reads=[wbuf[:, :, :], wbuf2[:, :, :], out_fm[:, :, sl],
                             rec_fm[:, :, sl]],
                      writes=[pb], tag=f"g_{o}_{piece}")
                S.add("scalar", lambda s, o=o, sl=sl, pb=pb: s.activation(
                          gate_fm[:, o, sl], pb, AF.Sigmoid, bias=bcol[:, o:o + 1]),
                      reads=[pb, bcol[:, o:o + 1]], writes=[gate_fm[:, o, sl]],
                      tag=f"ge_{o}_{piece}")
    for c in range(NCH):
        for piece in range(4):
            sl = slice(piece * 512, piece * 512 + 512)
            S.add("vector", lambda v, c=c, sl=sl: v.tensor_sub(
                      t2f[:, :], rec_fm[:, c, sl], out_fm[:, c, sl]),
                  reads=[rec_fm[:, c, sl], out_fm[:, c, sl]], writes=[t2f[:, :]],
                  tag=f"cb1_{c}_{piece}")
            S.add("vector", lambda v, c=c, sl=sl: v.tensor_mul(
                      t2f[:, :], t2f[:, :], gate_fm[:, c, sl]),
                  reads=[t2f[:, :], gate_fm[:, c, sl]], writes=[t2f[:, :]],
                  tag=f"cb2_{c}_{piece}")
            S.add("vector", lambda v, c=c, sl=sl: v.tensor_add(
                      t1f[:, :], t2f[:, :], out_fm[:, c, sl]),
                  reads=[t2f[:, :], out_fm[:, c, sl]], writes=[t1f[:, :]],
                  tag=f"cb3_{c}_{piece}")
            if FC is None:
                bufi = (c * 4 + piece) % 2
                S.dma("sync", xs[:, bufi, :],
                      x_d.ap()[c * P:(c + 1) * P, sl], tag=f"ld_x{c}_{piece}")
                S.add("vector", lambda v, bufi=bufi: v.tensor_add(
                          ys[:, bufi, :], t1f[:, :], xs[:, bufi, :]),
                      reads=[t1f[:, :], xs[:, bufi, :]], writes=[ys[:, bufi, :]],
                      tag=f"cb4_{c}_{piece}")
                S.dma("sync", out_d.ap()[c * P:(c + 1) * P, sl], ys[:, bufi, :],
                      tag=f"st_x{c}_{piece}")
            else:
                S.add("vector", lambda v, c=c, sl=sl: v.tensor_add(
                          x_res[:, c, sl], x_res[:, c, sl], t1f[:, :]),
                      reads=[t1f[:, :], x_res[:, c, sl]], writes=[x_res[:, c, sl]],
                      tag=f"cb4_{c}_{piece}")
    if FC is None:
        S.barrier_read("sync", [out_d.ap()])
        S.build()
    return nc
def pack_C2(inputs, x2, core):
    sl = x2.reshape(NCORE * TOK, D)[core * TOK:(core + 1) * TOK]
    wg = np.asarray(inputs["L1_attn_Wg"], np.float32)
    return {"x_fm": np.ascontiguousarray(sl.T.astype(np.float32)),
            "Wo": np.ascontiguousarray(np.asarray(inputs["L1_attn_Wo"], np.float32)),
            "boc": chunk_col(inputs["L1_attn_bo"]),
            "Wrq": np.ascontiguousarray(np.asarray(inputs["L1_attn_Wrq"], np.float32)),
            "brqc": chunk_col(inputs["L1_attn_brq"]),
            "WgA": np.ascontiguousarray(wg[0:1024]),
            "WgB": np.ascontiguousarray(wg[1024:2048]),
            "bgc": chunk_col(inputs["L1_attn_bg"])}


def build_D(FC=None):
    if FC is None:
        nc, S = _newprog()
        es = None
        x_d = nc.dram_tensor("x_fm", [D, TOK], F32, kind="ExternalInput")
        C = _mk_consts(nc)
        x_res = nc.alloc_sbuf_tensor("x_res", [P, NCH, TOK], F32)
        emit_consts(S, C)
        for c in range(NCH):
            S.dma("sync", x_res[:, c, :], x_d.ap()[c * P:(c + 1) * P, :], tag="ld_x")
        H = nc.alloc_sbuf_tensor("Hscr", [P, NCH, TOK], BF16)
    else:
        nc, S, C, x_res, es = FC["nc"], FC["S"], FC["C"], FC["x_res"], FC["es"]
        H = FC["outb"]
    w_d = nc.dram_tensor("wT", [D, D], F32, kind="ExternalInput")
    al_d = nc.dram_tensor("alpha", [1, 1], F32, kind="ExternalInput")
    bc_d = nc.dram_tensor("biasc", [P, NCH], F32, kind="ExternalInput")
    out_d = nc.dram_tensor("y_fm", [D, TOK], F32, kind="ExternalOutput")

    xn = AL(nc, es, "D_xn", [P, NCH, TOK], BF16)
    Hf = H.bitcast(F32).rearrange("p c w -> p (c w)")
    SCRF = AL(nc, es, "D_SCRF", [P, TOK], F32)
    wfst = AL(nc, es, "D_wfst", [P, 2, 1024], F32)
    tern = AL(nc, es, "D_tern", [P, NCH, 1024], BF16)
    bcol = AL(nc, es, "D_bcol", [P, NCH], F32)
    sums8 = AL(nc, es, "D_sums8", [P, NCH], F32)
    srow = AL(nc, es, "D_srow", [1, 16], F32)
    deltaB = AL(nc, es, "D_deltaB", [P, 2], F32)
    absw = AL(nc, es, "D_absw", [P, 1024], F32)
    sgn = AL(nc, es, "D_sgn", [P, 1024], BF16)
    msk = AL(nc, es, "D_msk", [P, 1024], BF16)
    alr = AL(nc, es, "D_alr", [1, 1], F32)
    ones_col_f = AL(nc, es, "D_ones_col", [P, 1], F32)

    S.add("gpsimd", lambda g: g.memset(ones_col_f[:, :], 1.0),
          writes=[ones_col_f[:, :]])
    for c in range(NCH):
        S.dma("sync", wfst[:, c % 2, :], w_d.ap()[c * P:(c + 1) * P, :], tag="ld_w")
        S.add("scalar", lambda s, c=c: s.activation(absw[:, :], wfst[:, c % 2, :],
                                                    AF.Abs),
              reads=[wfst[:, c % 2, :]], writes=[absw[:, :]], tag=f"q_abs{c}")
        S.add("vector", lambda v, c=c: v.tensor_reduce(sums8[:, c:c + 1], absw[:, :],
                                                       AX.X, ALU.add),
              reads=[absw[:, :]], writes=[sums8[:, c:c + 1]], tag=f"q_sum{c}")
    S.dma("sync", bcol[:, :], bc_d.ap(), tag="ld_b")
    S.dma("sync", alr[:, :], al_d.ap(), tag="ld_al")

    lnscr = dict(xb=H[:, 0:2, :], sq=H[:, 2:4, :],
                 mrow=Hf[0:1, 0:TOK], wrow=Hf[0:1, TOK:2 * TOK],
                 mb=H[:, 4, :], ab=H[:, 5, :], t1=SCRF[:, :])
    emit_ln(S, C, x_res, xn, lnscr, tag="ln4")

    with nc.psum_tensor("qps", [1, 16], F32) as qps:
        S.add("tensor", lambda t: t.matmul(qps[:, 0:8], ones_col_f[:, :],
                                           sums8[:, :], start=True, stop=True),
              reads=[ones_col_f[:, :], sums8[:, :]], writes=[qps[:, 0:8]],
              tag="q_mm")
        S.add("vector", lambda v: v.tensor_reduce(srow[:, 0:1], qps[:, 0:8],
                                                  AX.X, ALU.add),
              reads=[qps[:, 0:8]], writes=[srow[:, 0:1]], tag="q_tot")
    S.add("vector", lambda v: v.tensor_scalar_mul(srow[:, 1:2], srow[:, 0:1],
                                                  0.7 / (1024.0 * 1024.0)),
          reads=[srow[:, 0:1]], writes=[srow[:, 1:2]], tag="q_del")
    with nc.psum_tensor("qbp", [P, 2], F32) as qbp:
        S.add("tensor", lambda t: t.matmul(qbp[:, 0:1], C["ones_row_f"][:, :],
                                           srow[:, 1:2], start=True, stop=True),
              reads=[C["ones_row_f"][:, :], srow[:, 1:2]], writes=[qbp[:, 0:1]],
              tag="q_b1")
        S.add("tensor", lambda t: t.matmul(qbp[:, 1:2], C["ones_row_f"][:, :],
                                           alr[:, :], start=True, stop=True),
              reads=[C["ones_row_f"][:, :], alr[:, :]], writes=[qbp[:, 1:2]],
              tag="q_b2")
        S.add("vector", lambda v: v.tensor_copy(deltaB[:, :], qbp[:, :]),
              reads=[qbp[:, :]], writes=[deltaB[:, :]], tag="q_bc")
    for c in range(NCH):
        S.dma("sync", wfst[:, c % 2, :], w_d.ap()[c * P:(c + 1) * P, :], tag="ld_w2")
        S.add("scalar", lambda s, c=c: s.sign(sgn[:, :], wfst[:, c % 2, :]),
              reads=[wfst[:, c % 2, :]], writes=[sgn[:, :]], tag=f"q_sgn{c}")
        S.add("scalar", lambda s, c=c: s.activation(absw[:, :], wfst[:, c % 2, :],
                                                    AF.Abs),
              reads=[wfst[:, c % 2, :]], writes=[absw[:, :]], tag=f"q_ab2{c}")
        S.add("vector", lambda v, c=c: v.tensor_scalar(msk[:, :], absw[:, :],
                                                       deltaB[:, 0:1], None,
                                                       ALU.is_gt),
              reads=[absw[:, :], deltaB[:, 0:1]], writes=[msk[:, :]], tag=f"q_m{c}")
        S.add("vector", lambda v, c=c: v.tensor_mul(tern[:, c, :], msk[:, :],
                                                    sgn[:, :]),
              reads=[msk[:, :], sgn[:, :]], writes=[tern[:, c, :]], tag=f"q_t{c}")
    with nc.psum_tensor("dps", [P, 1024], F32) as dps:
        for o in range(NCH):
            for piece in range(4):
                sl = slice(piece * 512, piece * 512 + 512)
                pb = dps[:, ((o * 4 + piece) % 2) * 512:((o * 4 + piece) % 2) * 512 + 512]
                def dmm(t, o=o, sl=sl, pb=pb):
                    for i in range(NCH):
                        ins = t.matmul(pb, tern[:, i, o * P:(o + 1) * P],
                                       xn[:, i, sl],
                                       start=(i == 0), stop=(i == NCH - 1))
                    return ins
                S.add("tensor", dmm, reads=[tern[:, :, :], xn[:, :, sl]],
                      writes=[pb], tag=f"d_mm{o}_{piece}")
                S.add("vector", lambda v, o=o, sl=sl, pb=pb: v.scalar_tensor_tensor(
                          SCRF[:, 0:512], pb, deltaB[:, 1:2],
                          bcol[:, o:o + 1].broadcast_to([P, 512]), ALU.mult, ALU.add),
                      reads=[pb, deltaB[:, 1:2], bcol[:, o:o + 1]],
                      writes=[SCRF[:, 0:512]], tag=f"d_e{o}_{piece}")
                S.add("vector", lambda v, o=o, sl=sl: v.tensor_add(
                          x_res[:, o, sl], x_res[:, o, sl], SCRF[:, 0:512]),
                      reads=[SCRF[:, 0:512], x_res[:, o, sl]],
                      writes=[x_res[:, o, sl]], tag=f"d_r{o}_{piece}")
    for c in range(NCH):
        S.dma("sync", out_d.ap()[c * P:(c + 1) * P, :], x_res[:, c, :], tag="st_y")
    S.barrier_read("sync", [out_d.ap()])
    if FC is None:
        S.build()
    return nc


def pack_D(inputs, x3, core):
    sl = x3.reshape(NCORE * TOK, D)[core * TOK:(core + 1) * TOK]
    return {"x_fm": np.ascontiguousarray(sl.T.astype(np.float32)),
            "wT": np.ascontiguousarray(np.asarray(inputs["L1_ffn_weight"],
                                                  np.float32).T),
            "alpha": np.asarray(inputs["L1_ffn_alpha"], np.float32).reshape(1, 1),
            "biasc": chunk_col(inputs["L1_ffn_bias"])}


def build_full(n_phases=6):
    nc, S = _newprog()
    C = _mk_consts(nc)
    x_d = nc.dram_tensor("x_fm", [D, TOK], F32, kind="ExternalInput")
    x_res = nc.alloc_sbuf_tensor("x_res", [P, NCH, TOK], F32)
    outb = nc.alloc_sbuf_tensor("outb", [P, NCH, TOK], BF16)
    emit_consts(S, C)
    bk2_d = nc.dram_tensor("bk2_idx", [P, 1], mybir.dt.int32, kind="ExternalInput")
    bv_d = nc.dram_tensor("bv_idx", [P, 1], mybir.dt.int32, kind="ExternalInput")
    bv8_d = nc.dram_tensor("bv8_idx", [P, 1], mybir.dt.int32, kind="ExternalInput")
    bk2 = nc.alloc_sbuf_tensor("bk2_sb", [P, 1], mybir.dt.int32)
    bv = nc.alloc_sbuf_tensor("bv_sb", [P, 1], mybir.dt.int32)
    bv8 = nc.alloc_sbuf_tensor("bv8_sb", [P, 1], mybir.dt.int32)
    S.dma("sync", bk2[:, :], bk2_d.ap(), tag="ld_bk2")
    S.dma("sync", bv[:, :], bv_d.ap(), tag="ld_bv")
    S.dma("sync", bv8[:, :], bv8_d.ap(), tag="ld_bv8")
    for c in range(NCH):
        S.dma("sync", x_res[:, c, :], x_d.ap()[c * P:(c + 1) * P, :], tag="ld_x")
    FC = dict(nc=nc, S=S, C=C, x_res=x_res, outb=outb, io={}, es=None,
              bk2=bk2, bv=bv, bv8=bv8)
    phases = (build_A, build_B1, build_B2, build_C1, build_C2, build_D)[:n_phases]
    for phase in phases:
        with ExitStack() as es:
            FC["es"] = es
            phase(FC)
    if n_phases < 6:
        yd = nc.dram_tensor("y_fm", [D, TOK], F32, kind="ExternalOutput")
        for c in range(NCH):
            S.dma("sync", yd.ap()[c * P:(c + 1) * P, :], x_res[:, c, :], tag="st_y")
        S.barrier_read("sync", [yd.ap()])
    S.build()
    return nc


def pack_full(inputs, core):
    m = pack_A(inputs, core)
    b = core // 4
    m["bk2_idx"] = (b * 2 * 4 * 1032 + 2 * np.arange(P, dtype=np.int32)).reshape(P, 1)
    m["bv_idx"] = (b * 4 * TOK + np.arange(P, dtype=np.int32)).reshape(P, 1)
    m["bv8_idx"] = (b * 4 * TOK * 8 + 8 * np.arange(P, dtype=np.int32)).reshape(P, 1)
    xdummy = np.zeros((NCORE * TOK, D), np.float32)
    for pk in (lambda: pack_B1(inputs, xdummy, core),
               lambda: pack_C1(inputs, xdummy, core),
               lambda: pack_C2(inputs, xdummy, core),
               lambda: pack_D(inputs, xdummy, core)):
        d = pk()
        d.pop("x_fm", None)
        m.update(d)
    return m


def kernel_fused(**inputs):
    nc = build_full()
    maps = [pack_full(inputs, c) for c in range(NCORE)]
    res = run_bass_kernel_spmd(nc, maps, core_ids=list(range(NCORE)))
    out = np.empty((NCORE * TOK, D), np.float32)
    for c in range(NCORE):
        out[c * TOK:(c + 1) * TOK] = res.results[c]["y_fm"].T
    return out.reshape(2, NCORE * TOK // 2, D)


def _run(nc, maps):
    res = run_bass_kernel_spmd(nc, maps, core_ids=list(range(NCORE)))
    return res.results


def _asm_x(results, key):
    out = np.empty((NCORE * TOK, D), np.float32)
    for c in range(NCORE):
        out[c * TOK:(c + 1) * TOK] = results[c][key].T
    return out


FUSED = True


def kernel(**inputs):
    if FUSED:
        return kernel_fused(**inputs)
    resA = _run(build_A(), [pack_A(inputs, c) for c in range(NCORE)])
    x1 = _asm_x(resA, "x1_fm")
    resB1 = _run(build_B1(), [pack_B1(inputs, x1, c) for c in range(NCORE)])
    kfull = np.concatenate([resB1[c]["k_o"] for c in range(NCORE)], axis=0)
    vfull = np.concatenate([resB1[c]["v_o"] for c in range(NCORE)], axis=0)
    mapsB2 = []
    for c in range(NCORE):
        b = c // 4
        mapsB2.append({
            "x_fm": np.ascontiguousarray(
                x1[c * TOK:(c + 1) * TOK].T.astype(np.float32)),
            "q_fm": np.ascontiguousarray(resB1[c]["q_fm_o"]),
            "qn": resB1[c]["qn_o"],
            "kfull": np.ascontiguousarray(kfull[b * 4 * 1032:(b + 1) * 4 * 1032]),
            "vfull": np.ascontiguousarray(vfull[b * 4 * TOK:(b + 1) * 4 * TOK]),
        })
    resB2 = _run(build_B2(), mapsB2)
    x2 = _asm_x(resB2, "x2_fm")
    resC1 = _run(build_C1(), [pack_C1(inputs, x2, c) for c in range(NCORE)])
    kfull = np.concatenate([resC1[c]["k_o"] for c in range(NCORE)], axis=0)
    vfull = np.concatenate([resC1[c]["v_o"] for c in range(NCORE)], axis=0)
    rkfull = np.concatenate([resC1[c]["rk_o"] for c in range(NCORE)], axis=0)
    rvfull = np.concatenate([resC1[c]["rv_o"] for c in range(NCORE)], axis=0)
    mapsC2 = []
    for c in range(NCORE):
        b = c // 4
        m = pack_C2(inputs, x2, c)
        m.update({
            "q_fm": np.ascontiguousarray(resC1[c]["q_fm_o"]),
            "qn": resC1[c]["qn_o"],
            "kfull": np.ascontiguousarray(kfull[b * 4 * 1032:(b + 1) * 4 * 1032]),
            "vfull": np.ascontiguousarray(vfull[b * 4 * TOK:(b + 1) * 4 * TOK]),
            "rkfull": np.ascontiguousarray(rkfull),
            "rvfull": np.ascontiguousarray(rvfull),
        })
        mapsC2.append(m)
    resC2 = _run(build_C2(), mapsC2)
    x3 = _asm_x(resC2, "x3_fm")
    resD = _run(build_D(), [pack_D(inputs, x3, c) for c in range(NCORE)])
    y = _asm_x(resD, "y_fm")
    return y.reshape(2, NCORE * TOK // 2, D)


# version 11
# speedup vs baseline: 1.1629x; 1.1629x over previous
import numpy as np
import ml_dtypes
import concourse.bass as bass
import concourse.mybir as mybir
from concourse.bass_utils import run_bass_kernel_spmd

F32 = mybir.dt.float32
BF16 = mybir.dt.bfloat16
AF = mybir.ActivationFunctionType
ALU = mybir.AluOpType
AX = mybir.AxisListType

P = 128
TOK = 2048
NT = TOK // P
NCH = 8
D = 1024
NCORE = 8
EPS_RMS = float(np.finfo(np.float32).eps)
EPS_LN = 1e-5
LAMBDA_INIT = 0.8 - 0.6 * float(np.exp(-0.3 * 0.0))
L0_SCALE = 1.0 / float(np.sqrt(512.0))
BF = ml_dtypes.bfloat16
DEBUG_A = False
DEBUG_B2 = False
B2_LOADS_ONLY = False
NO_CC = False
GRAN = 512

_DT_SIZE = {F32: 4, BF16: 2, mybir.dt.float16: 2, mybir.dt.int32: 4, mybir.dt.uint8: 1}


class Sched:
    COMPUTE = ("tensor", "vector", "scalar", "gpsimd", "sync")

    def __init__(self, nc):
        self.nc = nc
        self.ops = []
        self.last_w = {}
        self.readers = {}
        self._mloc_cache = {}

    def _granules(self, ap):
        h = ap.tensor
        sp = str(h.space)
        if "SBUF" not in sp and "PSUM" not in sp:
            return {("D", h.name)}
        key = h.name
        if key not in self._mloc_cache:
            m = self.nc.lookup_mloc(h)
            addr = m.addr
            if "PSUM" in sp:
                addr = (m.bank or 0) * 2048 + (addr or 0)
            self._mloc_cache[key] = (addr or 0, "P" if "PSUM" in sp else "S")
        base, spc = self._mloc_cache[key]
        esz = _DT_SIZE[ap.dtype]
        row = 1
        for s in h.shape[1:]:
            row *= s
        off = ap.offset
        if not isinstance(off, int):
            off = 0
            lo_e, hi_e = 0, row - 1
        else:
            off = off % row if row else 0
            lo_e = hi_e = off
            pairs = list(ap.ap)[1:]
            for st, ct in pairs:
                if st >= 0:
                    hi_e += st * (ct - 1)
                else:
                    lo_e += st * (ct - 1)
        lo = base + lo_e * esz
        hi = base + hi_e * esz + esz - 1
        return {(spc, g) for g in range(lo // GRAN, hi // GRAN + 1)}

    def _dep_ops(self, reads, writes):
        raw = set()
        war = set()
        rg = set()
        wg = set()
        for ap in reads:
            rg |= self._granules(ap)
        for ap in writes:
            wg |= self._granules(ap)
        for g in rg:
            w = self.last_w.get(g)
            if w is not None:
                raw.add(w)
        for g in wg:
            w = self.last_w.get(g)
            if w is not None:
                war.add(w)
            for r in self.readers.get(g, ()):
                war.add(r)
        return raw, war, rg, wg

    def _record(self, opid, rg, wg):
        for g in rg:
            self.readers.setdefault(g, []).append(opid)
        for g in wg:
            self.last_w[g] = opid
            self.readers[g] = []

    def add(self, engine, fn, reads=(), writes=(), kind="c", tag=""):
        opid = len(self.ops)
        raw, war, rg, wg = self._dep_ops(reads, writes)
        real = set()
        for d in war:
            o = self.ops[d]
            if kind == "c" and o["kind"] == "c" and o["engine"] == engine:
                continue
            real.add(d)
        for d in raw:
            if d == opid:
                continue
            real.add(d)
        self.ops.append(dict(engine=engine, fn=fn, kind=kind, deps=real, tag=tag,
                             need_inc=False))
        self._record(opid, rg, wg)
        return opid

    def dma(self, engine, out, in_, tag="", **kw):
        def fn(eng, out=out, in_=in_, kw=kw):
            return eng.dma_start(out=out, in_=in_, **kw)
        return self.add(engine, fn, reads=[in_], writes=[out], kind="dma", tag=tag)

    def idma(self, out, in_, idxap, tag=""):
        off = in_.offset
        in0 = bass.AP(in_.tensor, 0, [list(p) for p in in_.ap])

        def fn(eng, out=out, in0=in0, idxap=idxap, off=off):
            return eng.indirect_dma_start(
                out, None, in0, bass.IndirectOffsetOnAxis(ap=idxap, axis=0),
                element_offset=off)
        return self.add("gpsimd", fn, reads=[in_, idxap], writes=[out], kind="dma",
                        tag=tag)

    def cc(self, kind, op, replica_groups, in_ap, out_ap, tag=""):
        def fn(eng, kind=kind, op=op, rg=replica_groups, i=in_ap, o=out_ap):
            return eng.collective_compute(kind, op, replica_groups=rg, ins=[i], outs=[o])
        return self.add("gpsimd", fn, reads=[in_ap], writes=[out_ap], kind="cc", tag=tag)

    def build(self, final_waits=True):
        nc = self.nc
        for o in self.ops:
            if o["kind"] in ("dma", "cc"):
                o["need_inc"] = True
            for d in o["deps"]:
                self.ops[d]["need_inc"] = True
        if final_waits:
            pass
        num = {}
        for i, o in enumerate(self.ops):
            q = (o["engine"], "c") if o["kind"] == "c" else (o["engine"], o["kind"])
            o["queue"] = q
            if o["need_inc"]:
                num[q] = num.get(q, 0) + 1
                o["inc_no"] = num[q]
        sems = {}
        for q in num:
            sems[q] = nc.ctx.enter_context(nc.semaphore(f"sem_{q[0]}_{q[1]}"))
        by_engine = {}
        for i, o in enumerate(self.ops):
            by_engine.setdefault(o["engine"], []).append(i)

        def emit_for(eng_name):
            def body(eng):
                for i in by_engine.get(eng_name, ()):
                    o = self.ops[i]
                    wv = {}
                    for d in o["deps"]:
                        od = self.ops[d]
                        q = od["queue"]
                        v = od["inc_no"] * (16 if q[1] == "dma" else 1)
                        if wv.get(q, 0) < v:
                            wv[q] = v
                    for q, v in wv.items():
                        eng.wait_ge(sems[q], v)
                    ins = o["fn"](eng)
                    if o["need_inc"]:
                        amt = 16 if o["kind"] == "dma" else 1
                        ins.then_inc(sems[o["queue"]], amt)
            return body

        with nc.Block() as blk:
            for name in ("tensor", "vector", "scalar", "gpsimd", "sync"):
                if name in by_engine:
                    getattr(blk, name)(emit_for(name))
        return sems

    def barrier_read(self, engine, aps, tag="final"):
        def fn(eng):
            return eng.nop()
        return self.add(engine, fn, reads=list(aps), writes=(), kind="c", tag=tag)



from contextlib import ExitStack


def _newprog():
    nc = bass.Bass("TRN2", target_bir_lowering=False)
    S = Sched(nc)
    return nc, S


def _mk_consts(nc, pfx=""):
    return dict(
        ident=nc.alloc_sbuf_tensor(pfx + "ident", [P, P], BF16),
        ones_bf=nc.alloc_sbuf_tensor(pfx + "ones_bf", [P, 1], BF16),
        ones_row_f=nc.alloc_sbuf_tensor(pfx + "ones_row_f", [1, P], F32),
        ones_row_bf=nc.alloc_sbuf_tensor(pfx + "ones_row_bf", [1, P], BF16),
        iscr=nc.alloc_sbuf_tensor(pfx + "iscr", [P, P], F32),
        pscr=nc.alloc_sbuf_tensor(pfx + "pscr", [P, 512], F32),
    )


def AL(nc, es, name, shape, dtype):
    if es is None:
        return nc.alloc_sbuf_tensor(name, shape, dtype)
    return es.enter_context(nc.sbuf_tensor(name, shape, dtype))


GROUPS4 = [[0, 1, 2, 3], [4, 5, 6, 7]]
GROUPS8 = [list(range(8))]

def f32c(x):
    return np.ascontiguousarray(x, dtype=np.float32)


def chunk_col(vec):
    return np.ascontiguousarray(np.asarray(vec, np.float32).reshape(NCH, P).T)


def emit_consts(S, C):
    S.add("gpsimd", lambda g: g.iota(C["iscr"][:, :], [[1, P]], base=0,
                                     channel_multiplier=-1,
                                     allow_small_or_imprecise_dtypes=True),
          writes=[C["iscr"][:, :]], tag="iota")
    S.add("gpsimd", lambda g: g.memset(C["ones_bf"][:, :], 1.0), writes=[C["ones_bf"][:, :]])
    S.add("gpsimd", lambda g: g.memset(C["ones_row_f"][:, :], 1.0), writes=[C["ones_row_f"][:, :]])
    S.add("gpsimd", lambda g: g.memset(C["ones_row_bf"][:, :], 1.0), writes=[C["ones_row_bf"][:, :]])
    S.add("vector", lambda v: v.tensor_single_scalar(C["ident"][:, :], C["iscr"][:, :],
                                                     0.0, ALU.is_equal),
          reads=[C["iscr"][:, :]], writes=[C["ident"][:, :]], tag="ident")


def emit_row_bcast(S, C, src_row, dst_tile, ps, n, f32=True):
    ones = C["ones_row_f"]
    for p0 in range(0, n, 512):
        w = min(512, n - p0)
        S.add("tensor",
              lambda t, p0=p0, w=w: t.matmul(ps[:, p0:p0 + w], ones[:, :],
                                             src_row[:, p0:p0 + w], start=True, stop=True),
              reads=[ones[:, :], src_row[:, p0:p0 + w]], writes=[ps[:, p0:p0 + w]],
              tag="bcast_mm")
    S.add("vector", lambda v: v.tensor_copy(dst_tile[:, 0:n], ps[:, 0:n]),
          reads=[ps[:, 0:n]], writes=[dst_tile[:, 0:n]], tag="bcast_cp")


def emit_ln(S, C, src, dst, scr, gain_col=None, combined_rms=False, tag="ln"):
    nc = S.nc
    st_g = nc.psum_tensor(f"{tag}_stps", [1, 2 * TOK], F32)
    st_ps = st_g.__enter__()
    sums = st_ps[:, 0:TOK]; sumsq = st_ps[:, TOK:2 * TOK]
    for c in range(NCH):
        S.add("scalar", lambda s, c=c: s.square(scr["sq"][:, c % 2, :], src[:, c, :]),
              reads=[src[:, c, :]], writes=[scr["sq"][:, c % 2, :]], tag=f"{tag}_sq{c}")
        S.add("vector", lambda v, c=c: v.tensor_copy(scr["xb"][:, c % 2, :], src[:, c, :]),
              reads=[src[:, c, :]], writes=[scr["xb"][:, c % 2, :]], tag=f"{tag}_xb{c}")
        def stat_mm(t, c=c):
            for piece in range(TOK // 512):
                sl = slice(piece * 512, piece * 512 + 512)
                t.matmul(sums[:, sl], C["ones_bf"][:, :], scr["xb"][:, c % 2, sl],
                         start=(c == 0), stop=(c == NCH - 1))
                ins = t.matmul(sumsq[:, sl], C["ones_bf"][:, :],
                               scr["sq"][:, c % 2, sl],
                               start=(c == 0), stop=(c == NCH - 1))
            return ins
        S.add("tensor", stat_mm,
              reads=[scr["xb"][:, c % 2, :], scr["sq"][:, c % 2, :],
                     C["ones_bf"][:, :]],
              writes=[sums[:, :], sumsq[:, :]], tag=f"{tag}_mm{c}")
    mean = scr["mrow"][:, :]
    work = scr["wrow"][:, :]

    S.add("vector", lambda v: v.tensor_scalar_mul(mean, sums[:, :], 1.0 / D),
          reads=[sums[:, :]], writes=[mean], tag=f"{tag}_mean")
    S.add("vector", lambda v: v.tensor_scalar_mul(work, sumsq[:, :], 1.0 / D),
          reads=[sumsq[:, :]], writes=[work], tag=f"{tag}_ms")
    st_g.__exit__(None, None, None)
    S.add("vector", lambda v: v.tensor_mul(scr["t1"][0:1, 0:TOK], mean, mean),
          reads=[mean], writes=[scr["t1"][0:1, 0:TOK]], tag=f"{tag}_m2")
    S.add("vector", lambda v: v.tensor_sub(work, work, scr["t1"][0:1, 0:TOK]),
          reads=[work, scr["t1"][0:1, 0:TOK]], writes=[work], tag=f"{tag}_var")
    if combined_rms:
        k1, k2 = 1.0 + EPS_RMS, EPS_RMS * EPS_LN
    else:
        k1, k2 = 1.0, EPS_LN
    S.add("vector", lambda v: v.tensor_scalar(work, work, k1, k2, ALU.mult, ALU.add),
          reads=[work], writes=[work], tag=f"{tag}_vare")
    S.add("scalar", lambda s: s.activation(work, work, AF.Sqrt),
          reads=[work], writes=[work], tag=f"{tag}_sd")
    S.add("vector", lambda v: v.reciprocal(work, work),
          reads=[work], writes=[work], tag=f"{tag}_al")
    bc_g = nc.psum_tensor(f"{tag}_bcps", [P, TOK], F32)
    bc_ps = bc_g.__enter__()
    for rowsrc, dstt in ((mean, scr["mb"]), (work, scr["ab"])):
        def bmm(t, rowsrc=rowsrc):
            for p0 in range(0, TOK, 512):
                ins = t.matmul(bc_ps[:, p0:p0 + 512], C["ones_row_f"][:, :],
                               rowsrc[:, p0:p0 + 512], start=True, stop=True)
            return ins
        S.add("tensor", bmm, reads=[C["ones_row_f"][:, :], rowsrc],
              writes=[bc_ps[:, :]], tag=f"{tag}_bmm")
        S.add("vector", lambda v, dstt=dstt: v.tensor_copy(dstt[:, :], bc_ps[:, :]),
              reads=[bc_ps[:, :]], writes=[dstt[:, :]], tag=f"{tag}_bcp")
    bc_g.__exit__(None, None, None)
    for c in range(NCH):
        t1 = scr["t1"]
        S.add("vector", lambda v, c=c: v.tensor_sub(t1[:, :], src[:, c, :], scr["mb"][:, :]),
              reads=[src[:, c, :], scr["mb"][:, :]], writes=[t1[:, :]], tag=f"{tag}_n1")
        if gain_col is None:
            S.add("vector", lambda v, c=c: v.tensor_mul(dst[:, c, :], t1[:, :], scr["ab"][:, :]),
                  reads=[t1[:, :], scr["ab"][:, :]], writes=[dst[:, c, :]], tag=f"{tag}_n2")
        else:
            S.add("vector", lambda v, c=c: v.tensor_mul(t1[:, :], t1[:, :], scr["ab"][:, :]),
                  reads=[t1[:, :], scr["ab"][:, :]], writes=[t1[:, :]], tag=f"{tag}_n2")
            S.add("vector",
                  lambda v, c=c: v.tensor_scalar_mul(dst[:, c, :], t1[:, :],
                                                     gain_col[:, c:c + 1]),
                  reads=[t1[:, :], gain_col[:, c:c + 1]], writes=[dst[:, c, :]],
                  tag=f"{tag}_n3")


def emit_proj_fm(S, C, wch, src, dst, ps, act=None, bias_col=None, scale=1.0,
                 accum_into=None, nchunks_in=NCH, tag="proj", evac_engine="scalar",
                 out_f32_add=None, x_res=None):
    width = src(0).shape[-1]
    npc = (width + 511) // 512
    for o in range(NCH):
        for piece in range(npc):
            w = min(512, width - piece * 512)
            sl = slice(piece * 512, piece * 512 + w)
            pb = (o * npc + piece) % 2
            psl = ps[:, pb * 512:pb * 512 + w]
            rd = [C["ones_bf"][:, :]]
            for i in range(nchunks_in):
                S.add("tensor",
                      lambda t, i=i, o=o, sl=sl, psl=psl: t.matmul(
                          psl, wch(i, o), src(i)[:, sl],
                          start=(i == 0), stop=(i == nchunks_in - 1)),
                      reads=[wch(i, o), src(i)[:, sl]], writes=[psl],
                      tag=f"{tag}_mm{o}_{piece}")
            if out_f32_add is not None:
                attn_ap, xres_ap = out_f32_add(o)
                t1 = C["pscr"]
                S.add("vector",
                      lambda v, psl=psl, a=attn_ap[:, sl], sl=sl: v.tensor_add(
                          t1[:, 0:a.shape[-1]], psl, a),
                      reads=[psl, attn_ap[:, sl]], writes=[t1[:, 0:w]], tag=f"{tag}_e1")
                S.add("vector",
                      lambda v, x=xres_ap[:, sl], sl=sl: v.tensor_add(
                          x, x, t1[:, 0:w]),
                      reads=[t1[:, 0:w], xres_ap[:, sl]], writes=[xres_ap[:, sl]],
                      tag=f"{tag}_e2")
            else:
                dsl = dst(o)[:, sl]
                if act is not None or bias_col is not None or scale != 1.0:
                    b = bias_col(o) if bias_col is not None else 0.0
                    fn = act if act is not None else AF.Identity
                    S.add(evac_engine,
                          lambda s, psl=psl, dsl=dsl, b=b, fn=fn: s.activation(
                              dsl, psl, fn,
                              bias=b if not isinstance(b, float) else b,
                              scale=scale),
                          reads=[psl] + ([b] if not isinstance(b, float) else []),
                          writes=[dsl], tag=f"{tag}_ev")
                else:
                    S.add("vector", lambda v, psl=psl, dsl=dsl: v.tensor_copy(dsl, psl),
                          reads=[psl], writes=[dsl], tag=f"{tag}_ev")


def emit_l0_attn_tiles(S, C, B):
    nc = S.nc
    xn, wqkv, attn_fm, x_res = B["xn"], B["wqkv"], B["attn_fm"], B["x_res"]
    qkps, tps = B["qkps"], B["tps"]
    pqk, vt = B["pqk"], B["vt"]
    eluA, eluB = B["eluA"], B["eluB"]
    gs, gg = B["gs"], B["gg"]
    acc, tmp1k = B["acc"], B["tmp1k"]
    sqt, r16 = B["sqt"], B["r16"]
    a2, a2g, t4 = B["a2"], B["a2g"], B["t4"]
    attn_tm = B["attn_tm"]

    def emit_transpose(st):
        atm = attn_tm[:, st % 2, :]

        def trg(t, atm=atm):
            for c in range(NCH):
                ins = t.transpose(tps[:, c * P:(c + 1) * P],
                                  atm[:, c * P:(c + 1) * P], C["ident"][:, :])
            return ins
        S.add("tensor", trg, reads=[atm[:, :], C["ident"][:, :]],
              writes=[tps[:, :]], tag=f"l0a_tp{st}")
        S.add("vector", lambda v, st=st: v.tensor_copy(
                  attn_fm[:, :, st * P:st * P + P],
                  tps[:, :].rearrange("p (c t) -> p c t", t=P)),
              reads=[tps[:, :]], writes=[attn_fm[:, :, st * P:st * P + P]],
              tag=f"l0a_tcp{st}")

    for st in range(NT):
        ssl = slice(st * P, st * P + P)
        for piece in range(6):
            psl = qkps[:, piece * 512:(piece + 1) * 512]

            def qkvmm(t, piece=piece, psl=psl, ssl=ssl):
                for f in range(NCH):
                    ins = t.matmul(psl, xn[:, f, ssl],
                                   wqkv[:, f, piece * 512:(piece + 1) * 512],
                                   start=(f == 0), stop=(f == NCH - 1))
                return ins
            S.add("tensor", qkvmm,
                  reads=[xn[:, :, ssl], wqkv[:, :, piece * 512:(piece + 1) * 512]],
                  writes=[psl], tag=f"l0a_qkv{st}_{piece}")
        S.add("vector", lambda v: v.tensor_scalar_min(eluA[:, :], qkps[:, 0:2048], 0.0),
              reads=[qkps[:, 0:2048]], writes=[eluA[:, :]], tag=f"l0a_min{st}")
        S.add("scalar", lambda s: s.activation(eluB[:, :], eluA[:, :], AF.Exp),
              reads=[eluA[:, :]], writes=[eluB[:, :]], tag=f"l0a_exp{st}")
        S.add("vector", lambda v: v.scalar_tensor_tensor(pqk[:, :], eluB[:, :], -1.0,
                                                         qkps[:, 0:2048], ALU.add, ALU.max),
              reads=[eluB[:, :], qkps[:, 0:2048]], writes=[pqk[:, :]], tag=f"l0a_elu{st}")
        S.add("vector", lambda v: v.tensor_copy(vt[:, :], qkps[:, 2048:3072]),
              reads=[qkps[:, 2048:3072]], writes=[vt[:, :]], tag=f"l0a_v{st}")
        if st == 0 and "dbg" in B:
            S.dma("sync", B["dbg"]["pqk"].ap(), pqk[:, :], tag="dbg_pqk")
            S.dma("sync", B["dbg"]["vt"].ap(), vt[:, :], tag="dbg_vt")
        pq = pqk[:, 0:1024].rearrange("p (h e) -> p h e", e=64)
        pk = pqk[:, 1024:2048]
        for half, gslice in ((0, gg[:, 0:256]), (1, gg[:, 256:512])):
            for hp in range(16):
                k_ = pk[:, hp * 64 + 32 * half: hp * 64 + 32 * half + 32]
                kb = k_.unsqueeze(1).broadcast_to([P, 16, 32])
                qv = pq[:, :, 32 * half:32 * half + 32]
                gsl = gs[:, (hp % 2) * 512:(hp % 2) * 512 + 512]
                prod = gsl.rearrange("p (h e) -> p h e", e=32)
                S.add("gpsimd", lambda e, prod=prod, qv=qv, kb=kb: e.tensor_tensor(
                          prod, qv, kb, ALU.mult),
                      reads=[pqk[:, :]], writes=[gsl],
                      tag=f"l0a_g{half}m{st}_{hp}")
                out = gslice.rearrange("p (h x) -> p h x", x=16)[:, :, hp]
                S.add("vector", lambda e, out=out, prod=prod: e.tensor_reduce(
                          out, prod, AX.X, ALU.add),
                      reads=[gsl], writes=[out],
                      tag=f"l0a_g{half}r{st}_{hp}")
        S.add("vector", lambda v: v.scalar_tensor_tensor(gg[:, 0:256], gg[:, 256:512],
                                                         B["neglmb"][:, :], gg[:, 0:256],
                                                         ALU.mult, ALU.add),
              reads=[gg[:, :], B["neglmb"][:, :]], writes=[gg[:, 0:256]], tag=f"l0a_h{st}")
        S.add("vector", lambda v: v.tensor_scalar_mul(gg[:, 0:256], gg[:, 0:256], L0_SCALE),
              reads=[gg[:, 0:256]], writes=[gg[:, 0:256]], tag=f"l0a_hs{st}")
        if st == 0 and "dbg" in B:
            S.dma("sync", B["dbg"]["gg"].ap(), gg[:, :], tag="dbg_gg")
        accv = acc[:, :].rearrange("p (h d) -> p h d", d=64)
        for hp in range(16):
            hcol = gg[:, 0:256].rearrange("p (h x) -> p h x", x=16)[:, :, hp]
            hb = hcol.unsqueeze(2).broadcast_to([P, 16, 64])
            vb = vt[:, hp * 64:(hp + 1) * 64].unsqueeze(1).broadcast_to([P, 16, 64])
            if hp == 0:
                S.add("vector", lambda v, hb=hb, vb=vb, accv=accv: v.tensor_tensor(
                          accv, hb, vb, ALU.mult),
                      reads=[gg[:, 0:256], vt[:, :]], writes=[acc[:, :]],
                      tag=f"l0a_acc{st}_0")
            else:
                S.add("gpsimd" if hp % 2 else "vector",
                      lambda e, hb=hb, vb=vb: e.tensor_tensor(
                          tmp1k[:, :].rearrange("p (h d) -> p h d", d=64), hb, vb, ALU.mult),
                      reads=[gg[:, 0:256], vt[:, :]], writes=[tmp1k[:, :]],
                      tag=f"l0a_accm{st}_{hp}")
                S.add("vector", lambda v: v.tensor_add(acc[:, :], acc[:, :], tmp1k[:, :]),
                      reads=[acc[:, :], tmp1k[:, :]], writes=[acc[:, :]],
                      tag=f"l0a_acca{st}_{hp}")
        if st == 0 and "dbg" in B:
            S.dma("sync", B["dbg"]["acc"].ap(), acc[:, :], tag="dbg_acc")
        S.add("scalar", lambda s: s.square(sqt[:, :], acc[:, :]),
              reads=[acc[:, :]], writes=[sqt[:, :]], tag=f"l0a_sq{st}")
        S.add("vector", lambda v: v.tensor_reduce(r16[:, 0:16],
                                                  sqt[:, :].rearrange("p (h d) -> p h d", d=64),
                                                  AX.X, ALU.add),
              reads=[sqt[:, :]], writes=[r16[:, 0:16]], tag=f"l0a_ms{st}")
        fac = (1.0 - LAMBDA_INIT) ** 2
        S.add("vector", lambda v: v.tensor_scalar(r16[:, 16:32], r16[:, 0:16],
                                                  1.0 / (64.0 * fac), EPS_RMS / fac,
                                                  ALU.mult, ALU.add),
              reads=[r16[:, 0:16]], writes=[r16[:, 16:32]], tag=f"l0a_msf{st}")
        S.add("scalar", lambda s: s.sqrt(r16[:, 32:48], r16[:, 16:32]),
              reads=[r16[:, 16:32]], writes=[r16[:, 32:48]], tag=f"l0a_msq{st}")
        S.add("vector", lambda v: v.reciprocal(r16[:, 48:64], r16[:, 32:48]),
              reads=[r16[:, 32:48]], writes=[r16[:, 48:64]], tag=f"l0a_mr{st}")
        r2b = r16[:, 48:64].unsqueeze(2).broadcast_to([P, 16, 64])
        S.add("vector", lambda v: v.tensor_tensor(a2[:, :].rearrange("p (h d) -> p h d", d=64),
                                                  accv, r2b, ALU.mult),
              reads=[acc[:, :], r16[:, 48:64]], writes=[a2[:, :]], tag=f"l0a_a2{st}")
        S.add("vector", lambda v: v.tensor_mul(a2g[:, :], a2[:, :], B["g2t"][:, :]),
              reads=[a2[:, :], B["g2t"][:, :]], writes=[a2g[:, :]], tag=f"l0a_a2g{st}")
        S.add("scalar", lambda s: s.square(sqt[:, :], a2g[:, :]),
              reads=[a2g[:, :]], writes=[sqt[:, :]], tag=f"l0a_sq3{st}")
        S.add("vector", lambda v: v.tensor_reduce(r16[:, 64:65], sqt[:, :], AX.X, ALU.add),
              reads=[sqt[:, :]], writes=[r16[:, 64:65]], tag=f"l0a_m3{st}")
        S.add("vector", lambda v: v.tensor_scalar(r16[:, 65:66], r16[:, 64:65],
                                                  1.0 / D, EPS_RMS, ALU.mult, ALU.add),
              reads=[r16[:, 64:65]], writes=[r16[:, 65:66]], tag=f"l0a_m3f{st}")
        S.add("scalar", lambda s: s.sqrt(r16[:, 66:67], r16[:, 65:66]),
              reads=[r16[:, 65:66]], writes=[r16[:, 66:67]], tag=f"l0a_m3q{st}")
        S.add("vector", lambda v: v.reciprocal(r16[:, 67:68], r16[:, 66:67]),
              reads=[r16[:, 66:67]], writes=[r16[:, 67:68]], tag=f"l0a_m3r{st}")
        S.add("vector", lambda v: v.tensor_scalar_mul(t4[:, :], a2g[:, :], r16[:, 67:68]),
              reads=[a2g[:, :], r16[:, 67:68]], writes=[t4[:, :]], tag=f"l0a_t3{st}")
        S.add("vector", lambda v: v.tensor_mul(t4[:, :], t4[:, :], B["g3t"][:, :]),
              reads=[t4[:, :], B["g3t"][:, :]], writes=[t4[:, :]], tag=f"l0a_t4{st}")
        atm = attn_tm[:, st % 2, :]
        S.add("vector", lambda v, atm=atm: v.tensor_add(atm, t4[:, :], a2g[:, :]),
              reads=[t4[:, :], a2g[:, :]], writes=[atm], tag=f"l0a_atm{st}")
        if st == 0 and "dbg" in B:
            S.dma("sync", B["dbg"]["atm"].ap(), atm[:, :], tag="dbg_atm")
            S.dma("sync", B["dbg"]["r16"].ap(), r16[:, :], tag="dbg_r16")
            S.dma("sync", B["dbg"]["a2"].ap(), a2[:, :], tag="dbg_a2")
            S.dma("sync", B["dbg"]["a2g"].ap(), a2g[:, :], tag="dbg_a2g")
            S.dma("sync", B["dbg"]["t4"].ap(), t4[:, :], tag="dbg_t4")
        if st > 0:
            emit_transpose(st - 1)
        if st == NT - 1:
            emit_transpose(st)


def emit_l0_attn_mlp(S, C, B):
    attn_fm = B["attn_fm"]; x_res = B["x_res"]
    w12 = B["w12"]
    hbuf = B["hbuf"]
    ps2 = B["ps2"]
    for tp in range(4):
        tsl = slice(tp * 512, tp * 512 + 512)
        for o in range(NCH):
            pb = ps2[:, (o % 2) * 512:(o % 2) * 512 + 512]

            def w1mm(t, o=o, pb=pb, tsl=tsl):
                for i in range(NCH):
                    ins = t.matmul(pb, w12[:, i, o * P:(o + 1) * P],
                                   attn_fm[:, i, tsl],
                                   start=(i == 0), stop=(i == NCH - 1))
                return ins
            S.add("tensor", w1mm,
                  reads=[w12[:, :, 0:1024], attn_fm[:, :, tsl]],
                  writes=[pb], tag=f"l0a_w1_{tp}_{o}")
            S.add("scalar", lambda s, o=o, pb=pb: s.activation(hbuf[:, o, :], pb, AF.Gelu),
                  reads=[pb], writes=[hbuf[:, o, :]], tag=f"l0a_w1e_{tp}_{o}")
        for o in range(NCH):
            pb = ps2[:, (o % 2) * 512:(o % 2) * 512 + 512]

            def w2mm(t, o=o, pb=pb):
                for i in range(NCH):
                    ins = t.matmul(pb, w12[:, i, 1024 + o * P:1024 + (o + 1) * P],
                                   hbuf[:, i, :],
                                   start=(i == 0), stop=(i == NCH - 1))
                return ins
            S.add("tensor", w2mm,
                  reads=[w12[:, :, 1024:2048], hbuf[:, :, :]],
                  writes=[pb], tag=f"l0a_w2_{tp}_{o}")
            t1 = C["pscr"]
            S.add("vector", lambda v, o=o, pb=pb, tsl=tsl: v.tensor_add(
                      t1[:, :], pb, attn_fm[:, o, tsl]),
                  reads=[pb, attn_fm[:, o, tsl]], writes=[t1[:, :]], tag=f"l0a_w2a_{tp}_{o}")
            S.add("vector", lambda v, o=o, tsl=tsl: v.tensor_add(
                      x_res[:, o, tsl], x_res[:, o, tsl], t1[:, :]),
                  reads=[t1[:, :], x_res[:, o, tsl]], writes=[x_res[:, o, tsl]],
                  tag=f"l0a_w2r_{tp}_{o}")


def build_A(FC=None):
    if FC is None:
        nc, S = _newprog()
        es = None
        x_d = nc.dram_tensor("x_fm", [D, TOK], F32, kind="ExternalInput")
        out_d = nc.dram_tensor("x1_fm", [D, TOK], F32, kind="ExternalOutput")
        C = _mk_consts(nc)
        x_res = nc.alloc_sbuf_tensor("x_res", [P, NCH, TOK], F32)
        emit_consts(S, C)
        for c in range(NCH):
            S.dma("sync", x_res[:, c, :], x_d.ap()[c * P:(c + 1) * P, :], tag="ld_x")
    else:
        nc, S, C, x_res, es = FC["nc"], FC["S"], FC["C"], FC["x_res"], FC["es"]
    wqkv_d = nc.dram_tensor("wqkvT", [D, 3072], F32, kind="ExternalInput")
    w1_d = nc.dram_tensor("w1T", [D, D], F32, kind="ExternalInput")
    w2_d = nc.dram_tensor("w2T", [D, D], F32, kind="ExternalInput")
    g1_d = nc.dram_tensor("g1c", [P, NCH], F32, kind="ExternalInput")
    g2_d = nc.dram_tensor("g2row", [1, D], F32, kind="ExternalInput")
    g3_d = nc.dram_tensor("g3row", [1, D], F32, kind="ExternalInput")
    lam_d = nc.dram_tensor("lam", [1, 4], F32, kind="ExternalInput")
    dbg = None
    if DEBUG_A:
        dbg = dict(
            pqk=nc.dram_tensor("d_pqk", [P, 2048], BF16, kind="ExternalOutput"),
            vt=nc.dram_tensor("d_vt", [P, 1024], BF16, kind="ExternalOutput"),
            gg=nc.dram_tensor("d_gg", [P, 512], F32, kind="ExternalOutput"),
            acc=nc.dram_tensor("d_acc", [P, 1024], F32, kind="ExternalOutput"),
            atm=nc.dram_tensor("d_atm", [P, 1024], BF16, kind="ExternalOutput"),
            r16=nc.dram_tensor("d_r16", [P, 68], F32, kind="ExternalOutput"),
            a2=nc.dram_tensor("d_a2", [P, 1024], BF16, kind="ExternalOutput"),
            a2g=nc.dram_tensor("d_a2g", [P, 1024], BF16, kind="ExternalOutput"),
            t4=nc.dram_tensor("d_t4", [P, 1024], BF16, kind="ExternalOutput"),
            xn=nc.dram_tensor("d_xn", [D, TOK], BF16, kind="ExternalOutput"),
        )

    xn = FC["outb"] if FC is not None else AL(nc, es, "A_xn", [P, NCH, TOK], BF16)
    wqkv = AL(nc, es, "A_wqkv", [P, NCH, 3072], BF16)
    SCR = AL(nc, es, "A_SCR", [P, 14336], BF16)
    SCF = SCR.bitcast(F32)
    SCRF = AL(nc, es, "A_SCRF", [P, TOK], F32)
    mrow = SCF[0:1, 0:TOK]
    wrow = SCF[0:1, TOK:2 * TOK]
    r16 = AL(nc, es, "A_r16", [P, 68], F32)
    g1c = AL(nc, es, "A_g1c", [P, NCH], F32)
    g2t = AL(nc, es, "A_g2t", [P, D], BF16)
    g3t = AL(nc, es, "A_g3t", [P, D], BF16)
    grow = SCR[0:1, 12288:14336]
    lam = AL(nc, es, "A_lam", [1, 4], F32)
    lsc = AL(nc, es, "A_lsc", [1, 8], F32)
    neglmb = AL(nc, es, "A_neglmb", [P, 1], F32)

    lnscr = dict(
        xb=SCR[:, 0:4096].rearrange("p (b t) -> p b t", t=TOK),
        sq=SCR[:, 4096:8192].rearrange("p (b t) -> p b t", t=TOK),
        mrow=mrow, wrow=wrow,
        mb=SCR[:, 8192:10240], ab=SCR[:, 10240:12288],
        t1=SCRF,
    )
    Bt = dict(
        x_res=x_res, xn=xn, attn_fm=xn, wqkv=wqkv, w12=wqkv,
        pqk=SCR[:, 0:2048],
        eluA=SCR[:, 2048:4096],
        eluB=SCR[:, 4096:6144],
        vt=SCR[:, 6144:7168],
        gs=SCR[:, 4096:5120],
        tmp1k=SCR[:, 5120:6144],
        gg=SCF[:, 3584:4096],
        acc=SCF[:, 1024:2048],
        sqt=SCR[:, 4096:5120],
        a2=SCR[:, 6144:7168],
        a2g=SCR[:, 0:1024],
        t4=SCR[:, 1024:2048],
        attn_tm=SCR[:, 8192:10240].rearrange("p (b t) -> p b t", t=1024),
        hbuf=SCR[:, 10240:14336].rearrange("p (c t) -> p c t", t=512),
        g2t=g2t, g3t=g3t, neglmb=neglmb, r16=r16,
    )

    for c in range(NCH):
        S.dma("gpsimd", wqkv[:, c, :], wqkv_d.ap()[c * P:(c + 1) * P, :], tag="ld_w")
    S.dma("sync", g1c[:, :], g1_d.ap(), tag="ld_g1")
    S.dma("gpsimd", grow[:, 0:D], g2_d.ap(), tag="ld_g2")
    S.dma("gpsimd", grow[:, D:2 * D], g3_d.ap(), tag="ld_g3")
    S.dma("sync", lam[:, :], lam_d.ap(), tag="ld_lam")

    S.add("vector", lambda v: v.tensor_mul(lsc[:, 0:1], lam[:, 0:1], lam[:, 1:2]),
          reads=[lam[:, :]], writes=[lsc[:, 0:1]])
    S.add("vector", lambda v: v.tensor_mul(lsc[:, 1:2], lam[:, 2:3], lam[:, 3:4]),
          reads=[lam[:, :]], writes=[lsc[:, 1:2]])
    S.add("scalar", lambda s: s.activation(lsc[:, 2:4], lsc[:, 0:2], AF.Exp),
          reads=[lsc[:, 0:2]], writes=[lsc[:, 2:4]])
    S.add("vector", lambda v: v.tensor_sub(lsc[:, 4:5], lsc[:, 3:4], lsc[:, 2:3]),
          reads=[lsc[:, 2:4]], writes=[lsc[:, 4:5]])
    S.add("vector", lambda v: v.tensor_scalar_add(lsc[:, 5:6], lsc[:, 4:5], -LAMBDA_INIT),
          reads=[lsc[:, 4:5]], writes=[lsc[:, 5:6]])
    with nc.psum_tensor("bc1", [P, 1], F32) as bc1:
        S.add("tensor", lambda t: t.matmul(bc1[:, :], C["ones_row_f"][:, :], lsc[:, 5:6],
                                           start=True, stop=True),
              reads=[C["ones_row_f"][:, :], lsc[:, 5:6]], writes=[bc1[:, :]])
        S.add("vector", lambda v: v.tensor_copy(neglmb[:, :], bc1[:, :]),
              reads=[bc1[:, :]], writes=[neglmb[:, :]])
    with nc.psum_tensor("bc2", [P, 1024], F32) as bc2:
        for src_, dstt in ((grow[:, 0:D], g2t), (grow[:, D:2 * D], g3t)):
            for p0 in range(0, D, 512):
                S.add("tensor", lambda t, p0=p0, src_=src_: t.matmul(
                          bc2[:, p0:p0 + 512], C["ones_row_bf"][:, :], src_[:, p0:p0 + 512],
                          start=True, stop=True),
                      reads=[C["ones_row_bf"][:, :], src_], writes=[bc2[:, p0:p0 + 512]])
            S.add("vector", lambda v, dstt=dstt: v.tensor_copy(dstt[:, :], bc2[:, :]),
                  reads=[bc2[:, :]], writes=[dstt[:, :]])

    emit_ln(S, C, x_res, xn, lnscr, gain_col=g1c, combined_rms=True, tag="ln1")
    if DEBUG_A:
        Bt["dbg"] = dbg
        for c in range(NCH):
            S.dma("sync", dbg["xn"].ap()[c * P:(c + 1) * P, :], xn[:, c, :], tag="dbg_xn")

    with (nc.psum_tensor("qkps", [P, 3072], F32) as qkps,
          nc.psum_tensor("tps", [P, 1024], BF16) as tps):
        Bt["qkps"] = qkps
        Bt["tps"] = tps
        emit_l0_attn_tiles(S, C, Bt)
    for c in range(NCH):
        S.dma("gpsimd", wqkv[:, c, 0:1024], w1_d.ap()[c * P:(c + 1) * P, :], tag="ld_w1")
        S.dma("gpsimd", wqkv[:, c, 1024:2048], w2_d.ap()[c * P:(c + 1) * P, :], tag="ld_w2")

    with nc.psum_tensor("ps2", [P, 1024], F32) as ps2:
        Bt["ps2"] = ps2
        emit_l0_attn_mlp(S, C, Bt)

    if FC is None:
        for c in range(NCH):
            S.dma("sync", out_d.ap()[c * P:(c + 1) * P, :], x_res[:, c, :], tag="st_x")
        aps = [out_d.ap()] + ([v.ap() for v in dbg.values()] if DEBUG_A else [])
        S.barrier_read("sync", aps)
        S.build()
    return nc


def pack_A(inputs, core):
    x = np.asarray(inputs["L0_attn_x"], np.float32).reshape(NCORE * TOK, D)
    sl = x[core * TOK:(core + 1) * TOK]
    return {
        "x_fm": np.ascontiguousarray(sl.T),
        "wqkvT": np.ascontiguousarray(np.concatenate(
            [np.asarray(inputs["L0_attn_Wq"]).T, np.asarray(inputs["L0_attn_Wk"]).T,
             np.asarray(inputs["L0_attn_Wv"]).T], axis=1).astype(np.float32)),
        "w1T": np.ascontiguousarray(np.asarray(inputs["L0_attn_W1"], np.float32).T),
        "w2T": np.ascontiguousarray(np.asarray(inputs["L0_attn_W2"], np.float32).T),
        "g1c": chunk_col(inputs["L0_attn_g1"]),
        "g2row": np.ascontiguousarray(np.tile(np.asarray(inputs["L0_attn_g2"], np.float32),
                                              16)[None, :]),
        "g3row": f32c(np.asarray(inputs["L0_attn_g3"]))[None, :],
        "lam": f32c(np.asarray(inputs["L0_attn_lambda_params"]))[None, :],
    }


def run_A(inputs):
    nc = build_A()
    maps = [pack_A(inputs, c) for c in range(NCORE)]
    res = run_bass_kernel_spmd(nc, maps, core_ids=list(range(NCORE)))
    out = np.empty((NCORE * TOK, D), np.float32)
    for c in range(NCORE):
        out[c * TOK:(c + 1) * TOK] = res.results[c]["x1_fm"].T
    return out.reshape(2, NCORE * TOK // 2, D)



def emit_tok_norms(S, C, src, width, stat_row, sqbuf, per_chunk, tag):
    nc = S.nc
    HW = 512
    nh = (width + HW - 1) // HW
    ps_g = nc.psum_tensor(f"{tag}_nps", [1, HW], F32)
    ps = ps_g.__enter__()
    tmpc = stat_row[:, 8:9]
    first_red = [True]

    def red(tgt, w):
        if first_red[0]:
            S.add("vector", lambda v, tgt=tgt, w=w: v.tensor_reduce(
                      tgt, ps[:, 0:w], AX.X, ALU.max),
                  reads=[ps[:, 0:w]], writes=[tgt], tag=f"{tag}_mx")
        else:
            S.add("vector", lambda v, w=w: v.tensor_reduce(tmpc, ps[:, 0:w],
                                                           AX.X, ALU.max),
                  reads=[ps[:, 0:w]], writes=[tmpc], tag=f"{tag}_mxt")
            S.add("vector", lambda v, tgt=tgt: v.tensor_max(tgt, tgt, tmpc),
                  reads=[tgt, tmpc], writes=[tgt], tag=f"{tag}_mxc")
        first_red[0] = False

    if per_chunk:
        for c in range(NCH):
            first_red[0] = True
            tgt = stat_row[:, c:c + 1]
            for h in range(nh):
                w = min(HW, width - h * HW)
                hsl = slice(h * HW, h * HW + w)
                S.add("scalar", lambda s, c=c, hsl=hsl, w=w: s.square(
                          sqbuf[:, 0:w], src[:, c, hsl]),
                      reads=[src[:, c, hsl]], writes=[sqbuf[:, 0:w]], tag=f"{tag}_sq")
                S.add("tensor", lambda t, w=w: t.matmul(
                          ps[:, 0:w], C["ones_bf"][:, :], sqbuf[:, 0:w],
                          start=True, stop=True),
                      reads=[C["ones_bf"][:, :], sqbuf[:, 0:w]], writes=[ps[:, 0:w]],
                      tag=f"{tag}_mm")
                red(tgt, w)
    else:
        tgt = stat_row[:, 0:1]
        for h in range(nh):
            w = min(HW, width - h * HW)
            hsl = slice(h * HW, h * HW + w)
            for c in range(NCH):
                S.add("scalar", lambda s, c=c, hsl=hsl, w=w: s.square(
                          sqbuf[:, 0:w], src[:, c, hsl]),
                      reads=[src[:, c, hsl]], writes=[sqbuf[:, 0:w]], tag=f"{tag}_sq")
                S.add("tensor", lambda t, w=w, c=c: t.matmul(
                          ps[:, 0:w], C["ones_bf"][:, :], sqbuf[:, 0:w],
                          start=(c == 0), stop=(c == NCH - 1)),
                      reads=[C["ones_bf"][:, :], sqbuf[:, 0:w]], writes=[ps[:, 0:w]],
                      tag=f"{tag}_mm")
            red(tgt, w)
    ps_g.__exit__(None, None, None)


def emit_shift(S, C, qn, kn, n, negc_col, rowscr, tag):
    nc = S.nc
    a = rowscr[:, 0:n]; b = rowscr[:, n:2 * n]
    S.add("scalar", lambda s: s.activation(a, qn, AF.Sqrt), reads=[qn], writes=[a],
          tag=f"{tag}_s1")
    S.add("scalar", lambda s: s.activation(b, kn, AF.Sqrt), reads=[kn], writes=[b],
          tag=f"{tag}_s2")
    S.add("vector", lambda v: v.tensor_mul(a, a, b), reads=[a, b], writes=[a],
          tag=f"{tag}_m")
    S.add("vector", lambda v: v.tensor_scalar_add(a, a, -60.0), reads=[a], writes=[a],
          tag=f"{tag}_sub")
    S.add("vector", lambda v: v.tensor_scalar(a, a, -1.0, 0.0, ALU.mult, ALU.min),
          reads=[a], writes=[a], tag=f"{tag}_neg")
    ps_g = nc.psum_tensor(f"{tag}_bps", [P, max(n, 1)], F32)
    ps = ps_g.__enter__()
    S.add("tensor", lambda t: t.matmul(ps[:, 0:n], C["ones_row_f"][:, :], a,
                                       start=True, stop=True),
          reads=[C["ones_row_f"][:, :], a], writes=[ps[:, 0:n]], tag=f"{tag}_bc")
    S.add("vector", lambda v: v.tensor_copy(negc_col[:, 0:n], ps[:, 0:n]),
          reads=[ps[:, 0:n]], writes=[negc_col[:, 0:n]], tag=f"{tag}_cp")
    ps_g.__exit__(None, None, None)


def emit_fff(S, C, F, tag, out_scale=1.0, load_w2=None):
    nc = S.nc
    xn, H, out = F["xn"], F["H"], F["out"]
    for o in range(NCH):
        for piece in range(4):
            sl = slice(piece * 512, piece * 512 + 512)
            pb = F["ps"][:, ((o * 4 + piece) % 2) * 512:((o * 4 + piece) % 2) * 512 + 512]

            def hmm(t, o=o, sl=sl, pb=pb, F=F):
                for i in range(NCH):
                    ins = t.matmul(pb, F["w1"][:, i, o * P:(o + 1) * P], xn[:, i, sl],
                                   start=(i == 0), stop=(i == NCH - 1))
                return ins
            S.add("tensor", hmm,
                  reads=[F["w1"][:, :, :], xn[:, :, sl]],
                  writes=[pb], tag=f"{tag}_h{o}_{piece}")
            S.add("scalar", lambda s, o=o, sl=sl, pb=pb: s.activation(
                      H[:, o, sl], pb, AF.Relu, bias=F["b1c"][:, o:o + 1]),
                  reads=[pb, F["b1c"][:, o:o + 1]], writes=[H[:, o, sl]],
                  tag=f"{tag}_he{o}_{piece}")
    for st in range(NT):
        ssl = slice(st * P, st * P + P)
        def nsmm(t, ssl=ssl, F=F):
            for i in range(NCH):
                ins = t.matmul(F["sps"][:, :], xn[:, i, ssl], F["nwt"][:, i, :],
                               start=(i == 0), stop=(i == NCH - 1))
            return ins
        S.add("tensor", nsmm, reads=[xn[:, :, ssl], F["nwt"][:, :, :]],
              writes=[F["sps"][:, :]], tag=f"{tag}_ns{st}")
        S.add("vector", lambda v, st=st: v.tensor_add(
                  F["sco"][:, st * 16:st * 16 + 16], F["sps"][:, :], F["nbb"][:, :]),
              reads=[F["sps"][:, :], F["nbb"][:, :]],
              writes=[F["sco"][:, st * 16:st * 16 + 16]], tag=f"{tag}_nse{st}")
    emit_fff_route(S, F, tag)
    for st in range(NT):
        S.add("tensor", lambda t, st=st: t.transpose(
                  F["tps"][:, :], F["mb16"][:, st * 16:st * 16 + 16], C["ident"][:, :]),
              reads=[F["mb16"][:, st * 16:st * 16 + 16], C["ident"][:, :]],
              writes=[F["tps"][:, :]], tag=f"{tag}_mt{st}")
        S.add("vector", lambda v, st=st: v.tensor_copy(
                  F["m16"][:, st * P:st * P + P], F["tps"][:, :]),
              reads=[F["tps"][:, :]], writes=[F["m16"][:, st * P:st * P + P]],
              tag=f"{tag}_mtc{st}")
    for c in range(NCH):
        for piece in range(4):
            sl = slice(piece * 512, piece * 512 + 512)
            S.add("tensor", lambda t, c=c, sl=sl: t.matmul(
                      F["mps"][:, :], F["bsel"][:, c, :], F["m16"][:, sl],
                      start=True, stop=True),
                  reads=[F["bsel"][:, c, :], F["m16"][:, sl]], writes=[F["mps"][:, :]],
                  tag=f"{tag}_mk{c}_{piece}")
            S.add("vector", lambda v, c=c, sl=sl: v.tensor_mul(
                      H[:, c, sl], H[:, c, sl], F["mps"][:, :]),
                  reads=[H[:, c, sl], F["mps"][:, :]], writes=[H[:, c, sl]],
                  tag=f"{tag}_mka{c}_{piece}")
    if load_w2 is not None:
        load_w2()
    for o in range(NCH):
        for piece in range(4):
            sl = slice(piece * 512, piece * 512 + 512)
            pb = F["ps"][:, ((o * 4 + piece) % 2) * 512:((o * 4 + piece) % 2) * 512 + 512]

            def omm(t, o=o, sl=sl, pb=pb, F=F):
                for i in range(NCH):
                    t.matmul(pb, F["w2"][:, i, o * P:(o + 1) * P], H[:, i, sl],
                             start=(i == 0), stop=False)
                return t.matmul(pb, F["b2"][:, o * P:(o + 1) * P], F["m16"][:, sl],
                                start=False, stop=True)
            S.add("tensor", omm,
                  reads=[F["w2"][:, :, :], H[:, :, sl],
                         F["b2"][:, :], F["m16"][:, sl]],
                  writes=[pb], tag=f"{tag}_o{o}_{piece}")
            S.add("scalar", lambda s, o=o, sl=sl, pb=pb: s.activation(
                      out[:, o, sl], pb, AF.Identity, scale=out_scale),
                  reads=[pb], writes=[out[:, o, sl]], tag=f"{tag}_oe{o}_{piece}")


def emit_fff_route(S, F, tag):
    sco, rt, mb = F["sco"], F["rt"], F["mb16"]

    def sv(off, w):
        return sco[:, :].rearrange("p (t n) -> p t n", n=16)[:, :, off:off + w]

    def rv(off, w):
        return rt[:, off * 16:(off + w) * 16].rearrange("p (x t) -> p t x", t=16) \
            if False else rt[:, :].rearrange("p (a t) -> p t a", t=16)[:, :, off:off + w]

    V = "vector"
    A = lambda fn, r, w, tg: S.add(V, fn, reads=r, writes=w, tag=f"{tag}_{tg}")
    R = rv
    allr = [rt[:, :]]; alls = [sco[:, :]]
    A(lambda v: v.tensor_single_scalar(R(0, 1), sv(0, 1), 0.0, ALU.is_ge), alls, allr, "b1")
    A(lambda v: v.tensor_scalar(R(1, 1), R(0, 1), -1.0, 1.0, ALU.mult, ALU.add), allr, allr, "nb1")
    A(lambda v: v.tensor_sub(R(2, 1), sv(2, 1), sv(1, 1)), alls, allr, "d1")
    A(lambda v: v.tensor_mul(R(2, 1), R(2, 1), R(0, 1)), allr, allr, "d2")
    A(lambda v: v.tensor_add(R(2, 1), R(2, 1), sv(1, 1)), allr + alls, allr, "d3")
    A(lambda v: v.tensor_single_scalar(R(3, 1), R(2, 1), 0.0, ALU.is_ge), allr, allr, "b2")
    A(lambda v: v.tensor_scalar(R(4, 1), R(3, 1), -1.0, 1.0, ALU.mult, ALU.add), allr, allr, "nb2")
    A(lambda v: v.tensor_mul(R(9, 1), R(1, 1), R(4, 1)), allr, allr, "p200")
    A(lambda v: v.tensor_mul(R(10, 1), R(1, 1), R(3, 1)), allr, allr, "p201")
    A(lambda v: v.tensor_mul(R(11, 1), R(0, 1), R(4, 1)), allr, allr, "p210")
    A(lambda v: v.tensor_mul(R(12, 1), R(0, 1), R(3, 1)), allr, allr, "p211")
    A(lambda v: v.tensor_mul(R(21, 4), R(9, 4), sv(3, 4)), allr + alls, allr, "s2m")
    A(lambda v: v.tensor_reduce(R(2, 1).squeeze(2), R(21, 4), AX.X, ALU.add), allr, allr, "s2r")
    A(lambda v: v.tensor_single_scalar(R(5, 1), R(2, 1), 0.0, ALU.is_ge), allr, allr, "b3")
    A(lambda v: v.tensor_scalar(R(6, 1), R(5, 1), -1.0, 1.0, ALU.mult, ALU.add), allr, allr, "nb3")
    def p3view(b):
        return rt[:, :].rearrange("p (a t) -> p t a", t=16)[:, :, 13 + b:21 + b:2]
    A(lambda v: v.tensor_mul(p3view(0), R(9, 4), R(6, 1).broadcast_to([P, 16, 4])),
      allr, allr, "p3e")
    A(lambda v: v.tensor_mul(p3view(1), R(9, 4), R(5, 1).broadcast_to([P, 16, 4])),
      allr, allr, "p3o")
    A(lambda v: v.tensor_mul(R(21, 8), R(13, 8), sv(7, 8)), allr + alls, allr, "s3m")
    A(lambda v: v.tensor_reduce(R(2, 1).squeeze(2), R(21, 8), AX.X, ALU.add), allr, allr, "s3r")
    A(lambda v: v.tensor_single_scalar(R(7, 1), R(2, 1), 0.0, ALU.is_ge), allr, allr, "b4")
    A(lambda v: v.tensor_scalar(R(8, 1), R(7, 1), -1.0, 1.0, ALU.mult, ALU.add), allr, allr, "nb4")
    def mview(b):
        return mb[:, :].rearrange("p (t l) -> p t l", l=16)[:, :, b:16:2]
    A(lambda v: v.tensor_mul(mview(0), R(13, 8), R(8, 1).broadcast_to([P, 16, 8])),
      allr, [mb[:, :]], "me")
    A(lambda v: v.tensor_mul(mview(1), R(13, 8), R(7, 1).broadcast_to([P, 16, 8])),
      allr, [mb[:, :]], "mo")



def build_B1(FC=None):
    if FC is None:
        nc, S = _newprog()
        es = None
        x_d = nc.dram_tensor("x_fm", [D, TOK], F32, kind="ExternalInput")
        C = _mk_consts(nc)
        x_res = nc.alloc_sbuf_tensor("x_res", [P, NCH, TOK], F32)
        emit_consts(S, C)
        for c in range(NCH):
            S.dma("sync", x_res[:, c, :], x_d.ap()[c * P:(c + 1) * P, :], tag="ld_x")
        okind = "ExternalOutput"
    else:
        nc, S, C, x_res, es = FC["nc"], FC["S"], FC["C"], FC["x_res"], FC["es"]
        okind = "Internal"
    wd = {}
    for t in ("q", "k", "v"):
        wd[t] = dict(
            w1=nc.dram_tensor(f"{t}_w1", [D, D], F32, kind="ExternalInput"),
            w2=nc.dram_tensor(f"{t}_w2", [D, D], F32, kind="ExternalInput"),
            nwt=nc.dram_tensor(f"{t}_nwt", [D, 16], F32, kind="ExternalInput"),
            nb=nc.dram_tensor(f"{t}_nb", [1, 16], F32, kind="ExternalInput"),
            b1c=nc.dram_tensor(f"{t}_b1c", [P, NCH], F32, kind="ExternalInput"),
            b2=nc.dram_tensor(f"{t}_b2", [16, D], F32, kind="ExternalInput"),
        )
    q_o = nc.dram_tensor("q_fm_o", [D, TOK], BF16, kind=okind)
    qn_o = nc.dram_tensor("qn_o", [1, 16], F32, kind=okind)
    k_o = nc.dram_tensor("k_o", [1032, TOK], BF16, kind=okind)
    v_o = nc.dram_tensor("v_o", [TOK, 1032], BF16, kind=okind)

    if FC is None:
        xv = x_res.bitcast(BF16)
        xn = xv[:, :, 0:TOK]
        outb = nc.alloc_sbuf_tensor("outb", [P, NCH, TOK], BF16)
    else:
        xn = AL(nc, es, "B1_xn", [P, NCH, TOK], BF16)
        outb = FC["outb"]
    H = AL(nc, es, "B1_H", [P, NCH, TOK], BF16)
    Hf = H.bitcast(F32).rearrange("p c w -> p (c w)")
    wbuf = AL(nc, es, "B1_wbuf", [P, NCH, 1024], BF16)
    SCRF = AL(nc, es, "B1_SCRF", [P, TOK], F32)
    nwt = AL(nc, es, "B1_nwt_sb", [P, NCH, 16], BF16)
    nbb = AL(nc, es, "B1_nbb", [P, 16], F32)
    b1c = AL(nc, es, "B1_b1c_sb", [P, NCH], F32)
    b2 = AL(nc, es, "B1_b2_sb", [16, D], BF16)
    nbrow = AL(nc, es, "B1_nbrow", [1, 16], F32)
    sco = AL(nc, es, "B1_sco", [P, 256], F32)
    rt = AL(nc, es, "B1_rt", [P, 480], F32)
    mb16 = AL(nc, es, "B1_mb16", [P, 256], BF16)
    m16 = AL(nc, es, "B1_m16", [16, TOK], BF16)
    bsel = AL(nc, es, "B1_bsel", [16, NCH, P], BF16)
    bselscr = SCRF[0:16, 0:NCH * P]
    vstage = AL(nc, es, "B1_vstage", [P, 1032], BF16)
    sqstat = AL(nc, es, "B1_sqstat", [P, 512], BF16)
    statrow = AL(nc, es, "B1_statrow", [1, 16], F32)
    krow = SCRF.bitcast(BF16)[0:1, 0:TOK]

    S.add("gpsimd", lambda g: g.iota(bselscr[:, :], [[-2, NCH], [-1, 2], [0, 64]],
                                     base=0, channel_multiplier=1,
                                     allow_small_or_imprecise_dtypes=True),
          writes=[bselscr[:, :]], tag="bsel_i")
    S.add("vector", lambda v: v.tensor_single_scalar(
              bsel[:, :, :].rearrange("l c p -> l (c p)"), bselscr[:, :], 0.0,
              ALU.is_equal),
          reads=[bselscr[:, :]], writes=[bsel[:, :, :]], tag="bsel_e")

    lnscr = dict(
        xb=H[:, 0:2, :], sq=H[:, 2:4, :],
        mrow=Hf[0:1, 0:TOK], wrow=Hf[0:1, TOK:2 * TOK],
        mb=H[:, 4, :], ab=H[:, 5, :], t1=SCRF[:, :],
    )
    emit_ln(S, C, x_res, xn, lnscr, tag="ln2")

    with (nc.psum_tensor("fps", [P, 1024], F32) as fps,
          nc.psum_tensor("mps", [P, 512], F32) as mps,
          nc.psum_tensor("spsn", [P, 16], F32) as spsn,
          nc.psum_tensor("tpsm", [16, P], BF16) as tpsm,
          nc.psum_tensor("tpsv", [P, 1024], BF16) as tpsv,
          nc.psum_tensor("nbps", [P, 16], F32) as nbps):
        F = dict(xn=xn, H=H, sco=sco, rt=rt, mb16=mb16, m16=m16, bsel=bsel,
                 nwt=nwt, nbb=nbb, b1c=b1c, b2=b2, ps=fps, mps=mps, sps=spsn,
                 tps=tpsm,
                 w1=wbuf, w2=wbuf)
        for t, is_q in (("k", False), ("v", False), ("q", True)):
            d = wd[t]
            for c in range(NCH):
                S.dma("gpsimd", wbuf[:, c, :], d["w1"].ap()[c * P:(c + 1) * P, :],
                      tag=f"ld_{t}w1")
                S.dma("gpsimd", nwt[:, c, :], d["nwt"].ap()[c * P:(c + 1) * P, :],
                      tag=f"ld_{t}nw")

            def _ldw2(d=d, t=t):
                for c in range(NCH):
                    S.dma("gpsimd", wbuf[:, c, :], d["w2"].ap()[c * P:(c + 1) * P, :],
                          tag=f"ld_{t}w2")
            S.dma("sync", b1c[:, :], d["b1c"].ap(), tag=f"ld_{t}b1")
            S.dma("gpsimd", b2[:, :], d["b2"].ap(), tag=f"ld_{t}b2")
            S.dma("sync", nbrow[:, :], d["nb"].ap(), tag=f"ld_{t}nb")
            S.add("tensor", lambda tt: tt.matmul(nbps[:, :], C["ones_row_f"][:, :],
                                                 nbrow[:, :], start=True, stop=True),
                  reads=[C["ones_row_f"][:, :], nbrow[:, :]], writes=[nbps[:, :]],
                  tag=f"{t}_nbb")
            S.add("vector", lambda v: v.tensor_copy(nbb[:, :], nbps[:, :]),
                  reads=[nbps[:, :]], writes=[nbb[:, :]], tag=f"{t}_nbbc")
            F["out"] = outb
            emit_fff(S, C, F, tag=f"fff_{t}", out_scale=(1.0 / 32.0 if is_q else 1.0),
                     load_w2=_ldw2)
            if t == "k":
                emit_tok_norms(S, C, outb, TOK, statrow[:, 0:16], sqstat, False, "kn")
                S.add("vector", lambda v: v.memset(krow[:, :], 0.0),
                      writes=[krow[:, :]], tag="krow0")
                S.add("vector", lambda v: v.tensor_copy(krow[:, 0:1], statrow[:, 0:1]),
                      reads=[statrow[:, 0:1]], writes=[krow[:, 0:1]], tag="krowc")
                for c in range(NCH):
                    S.dma("sync", k_o.ap()[c * P:(c + 1) * P, :], outb[:, c, :],
                          tag="st_k")
                S.dma("sync", k_o.ap()[1024:1025, :], krow[0:1, :], tag="st_kst")
            elif t == "v":
                S.add("vector", lambda v: v.memset(vstage[:, 1024:1032], 0.0),
                      writes=[vstage[:, 1024:1032]], tag="vst0")
                S.add("vector", lambda v: v.memset(vstage[:, 1024:1025], 1.0),
                      writes=[vstage[:, 1024:1025]], tag="vst1")
                for st in range(NT):
                    for c in range(NCH):
                        S.add("tensor", lambda tt, st=st, c=c: tt.transpose(
                                  tpsv[:, c * P:(c + 1) * P],
                                  outb[:, c, st * P:(st + 1) * P], C["ident"][:, :]),
                              reads=[outb[:, c, st * P:(st + 1) * P], C["ident"][:, :]],
                              writes=[tpsv[:, c * P:(c + 1) * P]], tag=f"vtp{st}_{c}")
                    S.add("vector", lambda v, st=st: v.tensor_copy(
                              vstage[:, 0:1024], tpsv[:, :]),
                          reads=[tpsv[:, :]], writes=[vstage[:, 0:1024]],
                          tag=f"vtc{st}")
                    S.dma("sync", v_o.ap()[st * P:(st + 1) * P, :], vstage[:, :],
                          tag=f"st_v{st}")
            else:
                emit_tok_norms(S, C, outb, TOK, statrow[:, 0:16], sqstat, False, "qn")
                S.dma("sync", qn_o.ap(), statrow[:, :], tag="st_qn")
                for c in range(NCH):
                    S.dma("sync", q_o.ap()[c * P:(c + 1) * P, :], outb[:, c, :],
                          tag="st_q")

    if FC is None:
        S.barrier_read("sync", [q_o.ap(), qn_o.ap(), k_o.ap(), v_o.ap()])
        S.build()
        return nc
    kg = nc.dram_tensor("B1_kg", [8 * 1032, TOK], BF16, kind="Internal")
    vg = nc.dram_tensor("B1_vg", [8 * TOK, 1032], BF16, kind="Internal")
    if not NO_CC:
        S.cc("AllGather", ALU.bypass, GROUPS8, k_o.ap().opt(), kg.ap().opt(), tag="ccK0")
        S.cc("AllGather", ALU.bypass, GROUPS8, v_o.ap().opt(), vg.ap().opt(), tag="ccV0")
    FC["io"].update(B1_q=q_o, B1_qn=qn_o, B1_kg=kg, B1_vg=vg)
    return nc


def pack_B1(inputs, x1, core):
    sl = x1.reshape(NCORE * TOK, D)[core * TOK:(core + 1) * TOK]
    m = {"x_fm": np.ascontiguousarray(sl.T.astype(np.float32))}
    for t in ("q", "k", "v"):
        w1 = np.asarray(inputs[f"L0_ffn_{t}_w1"], np.float32)
        w2 = np.asarray(inputs[f"L0_ffn_{t}_w2"], np.float32)
        nw = np.asarray(inputs[f"L0_ffn_{t}_node_w"], np.float32)
        nb = np.asarray(inputs[f"L0_ffn_{t}_node_b"], np.float32)
        b1 = np.asarray(inputs[f"L0_ffn_{t}_b1"], np.float32)
        b2 = np.asarray(inputs[f"L0_ffn_{t}_b2"], np.float32)
        m[f"{t}_w1"] = np.ascontiguousarray(w1.transpose(1, 0, 2).reshape(D, D))
        m[f"{t}_w2"] = np.ascontiguousarray(w2.reshape(D, D))
        nwt = np.zeros((D, 16), np.float32); nwt[:, 0:15] = nw.T
        m[f"{t}_nwt"] = nwt
        nbp = np.zeros((1, 16), np.float32); nbp[0, 0:15] = nb
        m[f"{t}_nb"] = nbp
        m[f"{t}_b1c"] = chunk_col(b1.reshape(D))
        m[f"{t}_b2"] = np.ascontiguousarray(b2)
    return m


def build_B2(FC=None):
    if FC is None:
        nc, S = _newprog()
        es = None
        x_d = nc.dram_tensor("x_fm", [D, TOK], F32, kind="ExternalInput")
        q_d = nc.dram_tensor("q_fm", [D, TOK], BF16, kind="ExternalInput")
        qn_d = nc.dram_tensor("qn", [1, 16], F32, kind="ExternalInput")
        k_d = nc.dram_tensor("kfull", [4 * 1032, TOK], BF16, kind="ExternalInput")
        v_d = nc.dram_tensor("vfull", [4 * TOK, 1032], BF16, kind="ExternalInput")
        out_d = nc.dram_tensor("x2_fm", [D, TOK], F32, kind="ExternalOutput")
        C = _mk_consts(nc)
        x_res = nc.alloc_sbuf_tensor("x_res", [P, NCH, TOK], F32)
        emit_consts(S, C)
    else:
        nc, S, C, x_res, es = FC["nc"], FC["S"], FC["C"], FC["x_res"], FC["es"]
        q_d = FC["io"]["B1_q"]; qn_d = FC["io"]["B1_qn"]
        k_d = FC["io"]["B1_kg"]; v_d = FC["io"]["B1_vg"]
    if DEBUG_B2:
        dbg_l = nc.dram_tensor("dbg_l", [1, 1024], F32, kind="ExternalOutput")
        dbg_ctx = nc.dram_tensor("dbg_ctx", [P, 1024], F32, kind="ExternalOutput")
        dbg_lb = nc.dram_tensor("dbg_lb", [P, 1024], F32, kind="ExternalOutput")
        dbg_t1 = nc.dram_tensor("dbg_t1", [P, 1024], F32, kind="ExternalOutput")
        dbg_xr = nc.dram_tensor("dbg_xr", [P, 1024], F32, kind="ExternalOutput")
        dbg_x0 = nc.dram_tensor("dbg_x0", [P, 1024], F32, kind="ExternalOutput")

    q_fm = FC["outb"] if FC is not None else nc.alloc_sbuf_tensor("q_sb", [P, NCH, TOK], BF16)
    ctx = AL(nc, es, "B2_ctx", [P, NCH, 1024], F32)
    K_sb = AL(nc, es, "B2_K_sb", [P, NCH, 1024], BF16)
    V_sb = AL(nc, es, "B2_V_sb", [P, NCH, 1032], BF16)
    Pt = AL(nc, es, "B2_Pt", [P, NCH, 1024], BF16)
    rows = AL(nc, es, "B2_rows", [1, 64], F32)
    negc = AL(nc, es, "B2_negc", [P, 8], F32)
    lrow = AL(nc, es, "B2_lrow", [1, 1024], F32)
    linvB = AL(nc, es, "B2_linvB", [P, 1024], F32)
    t1f = AL(nc, es, "B2_t1f", [P, 1024], F32)

    for c in range(NCH):
        if FC is None:
            S.dma("sync", x_res[:, c, :], x_d.ap()[c * P:(c + 1) * P, :], tag="ld_x")
            S.dma("sync", q_fm[:, c, :], q_d.ap()[c * P:(c + 1) * P, :], tag="ld_q")
    if DEBUG_B2:
        S.dma("sync", dbg_x0.ap(), x_res[:, 0, 0:1024], tag="dbg_x0")
    S.dma("sync", rows[0:1, 0:16], qn_d.ap(), tag="ld_qn")
    nrk = 4 if FC is None else 8
    kst = bass.AP(k_d, 1024 * TOK, [[1, 1], [1032 * TOK, nrk]])
    S.dma("gpsimd", rows[0:1, 16:16 + nrk], kst, tag="ld_kst",
          allow_slow_non_contiguous=True)
    S.add("vector", lambda v: v.tensor_reduce(rows[:, 24:25], rows[:, 16:16 + nrk],
                                              AX.X, ALU.max),
          reads=[rows[:, 16:16 + nrk]], writes=[rows[:, 24:25]], tag="kmax")
    emit_shift(S, C, rows[:, 0:1], rows[:, 24:25], 1, negc, rows[:, 32:64], "shift")

    with (nc.psum_tensor("sps", [P, 1024], F32) as sps,
          nc.psum_tensor("sps2", [P, 1024], F32) as sps2,
          nc.psum_tensor("lps", [1, 1024], F32) as lps,
          nc.psum_tensor("cps", [P, 1024], F32) as cps):
        SPS = (sps, sps2)
        for qh in range(2):
            qsl = slice(qh * 1024, qh * 1024 + 1024)
            ng = 0
            pend = [None]

            def load_group(g, qh=qh):
                rank = g // 2
                kcol = (g % 2) * 1024
                for c in range(NCH):
                    if FC is None:
                        kap = k_d.ap()[1032 * rank + c * P:1032 * rank + (c + 1) * P,
                                       kcol:kcol + 1024]
                        vap = v_d.ap()[TOK * rank + kcol + c * P:
                                       TOK * rank + kcol + (c + 1) * P, :]
                        S.dma("sync", K_sb[:, c, :], kap, tag=f"ldK{qh}_{g}")
                        S.dma("sync", V_sb[:, c, :], vap, tag=f"ldV{qh}_{g}")
                    else:
                        kap = bass.AP(k_d, (2 * (1032 * rank + c * P)
                                            + kcol // 1024) * 1024,
                                      [[2048, P], [1, 1024]])
                        vap = bass.AP(v_d, (TOK * rank + kcol + c * P) * 1032,
                                      [[1032, P], [1, 1032]])
                        S.idma(K_sb[:, c, :], kap, FC["bk2"][:, :], tag=f"ldK{qh}_{g}")
                        S.idma(V_sb[:, c, :], vap, FC["bv"][:, :], tag=f"ldV{qh}_{g}")

            def mk_pv(g, qh=qh):
                def emit():
                    for c in range(NCH):
                        def pvmm(tt, c=c):
                            for piece in range(2):
                                psl = cps[:, piece * 512:piece * 512 + 512]
                                for t in range(NCH):
                                    ins = tt.matmul(
                                        psl, V_sb[:, t, c * P:(c + 1) * P],
                                        Pt[:, t, piece * 512:piece * 512 + 512],
                                        start=(t == 0), stop=(t == NCH - 1))
                            return ins
                        S.add("tensor", pvmm,
                              reads=[V_sb[:, :, c * P:(c + 1) * P], Pt[:, :, :]],
                              writes=[cps[:, :]], tag=f"pv{qh}_{g}_{c}")
                        if g == 0:
                            S.add("vector", lambda v, c=c: v.tensor_copy(
                                      ctx[:, c, :], cps[:, :]),
                                  reads=[cps[:, :]], writes=[ctx[:, c, :]],
                                  tag=f"fl{qh}_{g}_{c}")
                        else:
                            S.add("vector", lambda v, c=c: v.tensor_add(
                                      ctx[:, c, :], ctx[:, c, :], cps[:, :]),
                                  reads=[cps[:, :], ctx[:, c, :]],
                                  writes=[ctx[:, c, :]], tag=f"fl{qh}_{g}_{c}")
                return emit

            def emit_qk(g, t, qh=qh):
                sp = SPS[t % 2]

                def qkmm(tt, t=t, qh=qh, sp=sp):
                    for piece in range(2):
                        psl = sp[:, piece * 512:piece * 512 + 512]
                        for c in range(NCH):
                            ins = tt.matmul(
                                psl, K_sb[:, c, t * P:(t + 1) * P],
                                q_fm[:, c, qh * 1024 + piece * 512:
                                     qh * 1024 + piece * 512 + 512],
                                start=(c == 0), stop=(c == NCH - 1))
                    return ins
                S.add("tensor", qkmm,
                      reads=[K_sb[:, :, t * P:(t + 1) * P], q_fm[:, :, qsl]],
                      writes=[sp[:, :]], tag=f"qk{qh}_{g}_{t}")

            def emit_expl(g, t, ng_, qh=qh):
                sp = SPS[t % 2]
                S.add("scalar", lambda s, t=t, sp=sp: s.activation(
                          Pt[:, t, :], sp[:, :], AF.Exp, bias=negc[:, 0:1]),
                      reads=[sp[:, :], negc[:, 0:1]], writes=[Pt[:, t, :]],
                      tag=f"exp{qh}_{g}_{t}")
                def lmm(tt, t=t, ng_=ng_):
                    for piece in range(2):
                        ins = tt.matmul(lps[:, piece * 512:piece * 512 + 512],
                                        C["ones_bf"][:, :],
                                        Pt[:, t, piece * 512:piece * 512 + 512],
                                        start=(ng_ == 0), stop=(ng_ == 63))
                    return ins
                S.add("tensor", lmm, reads=[C["ones_bf"][:, :], Pt[:, t, :]],
                      writes=[lps[:, :]], tag=f"l{qh}_{g}_{t}")

            load_group(0)
            for g in range(8):
                if g + 1 < 8:
                    load_group(g + 1)
                emit_qk(g, 0)
                emit_qk(g, 1)
                if pend[0] is not None:
                    pend[0]()
                emit_expl(g, 0, ng)
                for t in range(2, NCH):
                    emit_qk(g, t)
                    emit_expl(g, t - 1, ng + t - 1)
                emit_expl(g, NCH - 1, ng + NCH - 1)
                ng += NCH
                pend[0] = mk_pv(g)
            pend[0]()
            pend[0] = None
            S.add("vector", lambda v: v.tensor_scalar_add(lrow[:, :], lps[:, :], 1e-30),
                  reads=[lps[:, :]], writes=[lrow[:, :]], tag=f"lc{qh}")
            S.add("vector", lambda v: v.reciprocal(lrow[:, :], lrow[:, :]),
                  reads=[lrow[:, :]], writes=[lrow[:, :]], tag=f"li{qh}")
            for p0 in (0, 512):
                S.add("tensor", lambda t, p0=p0: t.matmul(
                          cps[:, p0:p0 + 512], C["ones_row_f"][:, :],
                          lrow[:, p0:p0 + 512], start=True, stop=True),
                      reads=[C["ones_row_f"][:, :], lrow[:, :]],
                      writes=[cps[:, p0:p0 + 512]], tag=f"lb{qh}")
            S.add("vector", lambda v: v.tensor_copy(linvB[:, :], cps[:, :]),
                  reads=[cps[:, :]], writes=[linvB[:, :]], tag=f"lbc{qh}")
            if qh == 0 and DEBUG_B2:
                S.dma("sync", dbg_lb.ap(), linvB[:, :], tag="dbg_lb")
            if qh == 0 and DEBUG_B2:
                S.dma("sync", dbg_l.ap(), lrow[:, :], tag="dbg_l")
                S.dma("sync", dbg_ctx.ap(), ctx[:, 0, :], tag="dbg_ctx")
            for c in range(NCH):
                S.add("vector", lambda v, c=c: v.tensor_mul(t1f[:, :], ctx[:, c, :],
                                                            linvB[:, :]),
                      reads=[ctx[:, c, :], linvB[:, :]], writes=[t1f[:, :]],
                      tag=f"nm{qh}_{c}")
                if qh == 0 and c == 0 and DEBUG_B2:
                    S.dma("sync", dbg_t1.ap(), t1f[:, :], tag="dbg_t1")
                S.add("vector", lambda v, c=c, qsl=qsl: v.tensor_add(
                          x_res[:, c, qsl], x_res[:, c, qsl], t1f[:, :]),
                      reads=[t1f[:, :], x_res[:, c, qsl]], writes=[x_res[:, c, qsl]],
                      tag=f"ra{qh}_{c}")
                if qh == 0 and c == 0 and DEBUG_B2:
                    S.dma("sync", dbg_xr.ap(), x_res[:, 0, 0:1024], tag="dbg_xr")
    if FC is None:
        for c in range(NCH):
            S.dma("sync", out_d.ap()[c * P:(c + 1) * P, :], x_res[:, c, :], tag="st_x")
        S.barrier_read("sync", [out_d.ap()])
        S.build()
    return nc


def emit_proj_generic(S, C, wbuf, src_fn, dst_fn, ps, nchunks_in, width, act, bias_col,
                      scale, tag):
    npc = (width + 511) // 512
    for o in range(NCH):
        for piece in range(npc):
            w = min(512, width - piece * 512)
            sl = slice(piece * 512, piece * 512 + w)
            pb = ps[:, ((o * npc + piece) % 2) * 512:((o * npc + piece) % 2) * 512 + w]
            ni = nchunks_in

            def mmgroup(t, o=o, sl=sl, pb=pb, src_fn=src_fn, ni=ni, wbuf=wbuf):
                for i in range(ni):
                    ins = t.matmul(pb, wbuf[:, i, o * P:(o + 1) * P],
                                   src_fn(i)[:, sl],
                                   start=(i == 0), stop=(i == ni - 1))
                return ins
            S.add("tensor", mmgroup,
                  reads=[wbuf[:, i, o * P:(o + 1) * P] for i in range(ni)] +
                        [src_fn(i)[:, sl] for i in range(ni)],
                  writes=[pb], tag=f"{tag}_mm{o}_{piece}")
            b = bias_col(o) if bias_col is not None else 0.0
            dsl = dst_fn(o)[:, sl]
            S.add("scalar", lambda s, dsl=dsl, pb=pb, b=b, act=act, scale=scale:
                  s.activation(dsl, pb, act, bias=b, scale=scale),
                  reads=[pb] + ([b] if not isinstance(b, float) else []),
                  writes=[dsl], tag=f"{tag}_ev{o}_{piece}")


def build_C1(FC=None):
    if FC is None:
        nc, S = _newprog()
        es = None
        x_d = nc.dram_tensor("x_fm", [D, TOK], F32, kind="ExternalInput")
        C = _mk_consts(nc)
        x_res = nc.alloc_sbuf_tensor("x_res", [P, NCH, TOK], F32)
        emit_consts(S, C)
        for c in range(NCH):
            S.dma("sync", x_res[:, c, :], x_d.ap()[c * P:(c + 1) * P, :], tag="ld_x")
        okind = "ExternalOutput"
    else:
        nc, S, C, x_res, es = FC["nc"], FC["S"], FC["C"], FC["x_res"], FC["es"]
        okind = "Internal"
    past_d = nc.dram_tensor("past_fm", [D, 512], F32, kind="ExternalInput")
    wds = {}
    for t in ("q", "k", "v", "rk", "rv"):
        wds[t] = (nc.dram_tensor(f"W{t}", [D, D], F32, kind="ExternalInput"),
                  nc.dram_tensor(f"b{t}c", [P, NCH], F32, kind="ExternalInput"))
    q_o = nc.dram_tensor("C1_q_o", [D, TOK], BF16, kind=okind)
    qn_o = nc.dram_tensor("C1_qn_o", [1, 16], F32, kind=okind)
    k_o = nc.dram_tensor("C1_k_o", [1032, TOK], BF16, kind=okind)
    v_o = nc.dram_tensor("C1_v_o", [TOK, 1040], BF16, kind=okind)
    rk_o = nc.dram_tensor("C1_rk_o", [1032, 512], BF16, kind=okind)
    rv_o = nc.dram_tensor("C1_rv_o", [512, 1040], BF16, kind=okind)

    if FC is None:
        xv = x_res.bitcast(BF16)
        xn = xv[:, :, 0:TOK]
        outb = nc.alloc_sbuf_tensor("outb", [P, NCH, TOK], BF16)
    else:
        xn = AL(nc, es, "C1_xn", [P, NCH, TOK], BF16)
        outb = FC["outb"]
    H = AL(nc, es, "C1_H", [P, NCH, TOK], BF16)
    Hf = H.bitcast(F32).rearrange("p c w -> p (c w)")
    wbuf = AL(nc, es, "C1_wbuf", [P, NCH, 1024], BF16)
    SCRF = AL(nc, es, "C1_SCRF", [P, TOK], F32)
    past_sb = AL(nc, es, "C1_past", [P, NCH, 512], BF16)
    bcol = AL(nc, es, "C1_bcol", [P, NCH], F32)
    statrow = AL(nc, es, "C1_statrow", [1, 16], F32)
    sqstat = AL(nc, es, "C1_sqstat", [P, 512], BF16)
    krow = AL(nc, es, "C1_krow", [1, TOK], BF16)
    vstage = AL(nc, es, "C1_vstage", [P, 1040], BF16)

    for c in range(NCH):
        S.dma("gpsimd", past_sb[:, c, :], past_d.ap()[c * P:(c + 1) * P, :], tag="ld_p")
    lnscr = dict(xb=H[:, 0:2, :], sq=H[:, 2:4, :],
                 mrow=Hf[0:1, 0:TOK], wrow=Hf[0:1, TOK:2 * TOK],
                 mb=H[:, 4, :], ab=H[:, 5, :], t1=SCRF[:, :])
    emit_ln(S, C, x_res, xn, lnscr, tag="ln3")

    def stage_v(src, width, v_out, nt, tag):
        S.add("vector", lambda v: v.memset(vstage[:, :].rearrange(
                  "p (h w) -> p h w", w=65)[:, :, 64:65], 1.0),
              writes=[vstage[:, :]], tag=f"{tag}_ones")
        with nc.psum_tensor(f"{tag}_tps", [P, 1024], BF16) as tps:
            for st in range(nt):
                for c in range(NCH):
                    S.add("tensor", lambda tt, st=st, c=c: tt.transpose(
                              tps[:, c * P:(c + 1) * P], src[:, c, st * P:(st + 1) * P],
                              C["ident"][:, :]),
                          reads=[src[:, c, st * P:(st + 1) * P], C["ident"][:, :]],
                          writes=[tps[:, c * P:(c + 1) * P]], tag=f"{tag}_t{st}_{c}")
                S.add("vector", lambda v, st=st: v.tensor_copy(
                          vstage[:, :].rearrange("p (h w) -> p h w", w=65)[:, :, 0:64],
                          tps[:, :].rearrange("p (h w) -> p h w", w=64)),
                      reads=[tps[:, :]], writes=[vstage[:, :]], tag=f"{tag}_c{st}")
                S.dma("sync", v_out.ap()[st * P:(st + 1) * P, :], vstage[:, :],
                      tag=f"{tag}_st{st}")

    with nc.psum_tensor("C1_pps", [P, 1024], F32) as pps:
        for t, src_fn, n_in, width, outdst in (
                ("k", lambda i: xn[:, i, :], NCH, TOK, None),
                ("v", lambda i: xn[:, i, :], NCH, TOK, None),
                ("rk", lambda i: past_sb[:, i, :], NCH, 512, None),
                ("rv", lambda i: past_sb[:, i, :], NCH, 512, None),
                ("q", lambda i: xn[:, i, :], NCH, TOK, None)):
            w_d, b_d = wds[t]
            for c in range(NCH):
                S.dma("gpsimd", wbuf[:, c, :], w_d.ap()[c * P:(c + 1) * P, :],
                      tag=f"ld_w{t}")
            S.dma("sync", bcol[:, :], b_d.ap(), tag=f"ld_b{t}")
            scale = 1.0 / 8.0 if t in ("q",) else 1.0
            if t == "q":
                S.add("vector", lambda v: v.tensor_scalar_mul(bcol[:, :], bcol[:, :],
                                                              1.0 / 8.0),
                      reads=[bcol[:, :]], writes=[bcol[:, :]], tag="bq8")
            dstbuf = outb
            emit_proj_generic(S, C, wbuf, src_fn,
                              lambda o: dstbuf[:, o, 0:width], pps, n_in, width,
                              AF.Identity, lambda o: bcol[:, o:o + 1], scale,
                              tag=f"pj_{t}")
            if t == "q":
                emit_tok_norms(S, C, outb, TOK, statrow[:, 0:16], sqstat, True, "c1qn")
                S.dma("sync", qn_o.ap(), statrow[:, :], tag="st_qn")
                for c in range(NCH):
                    S.dma("sync", q_o.ap()[c * P:(c + 1) * P, :], outb[:, c, :],
                          tag="st_q")
            elif t in ("k", "rk"):
                wd_ = TOK if t == "k" else 512
                out_t = k_o if t == "k" else rk_o
                emit_tok_norms(S, C, outb[:, :, 0:wd_] if wd_ != TOK else outb,
                               wd_, statrow[:, 0:16], sqstat, True, f"c1{t}n")
                S.add("vector", lambda v, wd_=wd_: v.memset(krow[:, 0:wd_], 0.0),
                      writes=[krow[:, :]], tag=f"{t}row0")
                S.add("vector", lambda v: v.tensor_copy(krow[:, 0:8], statrow[:, 0:8]),
                      reads=[statrow[:, 0:8]], writes=[krow[:, 0:8]], tag=f"{t}rowc")
                for c in range(NCH):
                    S.dma("sync", out_t.ap()[c * P:(c + 1) * P, :], outb[:, c, 0:wd_],
                          tag=f"st_{t}")
                S.dma("sync", out_t.ap()[1024:1025, :], krow[0:1, 0:wd_], tag=f"st_{t}s")
            elif t == "v":
                stage_v(outb, TOK, v_o, NT, "vst")
            elif t == "rv":
                stage_v(outb[:, :, 0:512].rearrange("p c w -> p c w"), 512, rv_o, 4,
                        "rvst")

    if FC is None:
        S.barrier_read("sync", [q_o.ap(), qn_o.ap(), k_o.ap(), v_o.ap(), rk_o.ap(),
                                rv_o.ap()])
        S.build()
        return nc
    kg = nc.dram_tensor("C1_kg", [8 * 1032, TOK], BF16, kind="Internal")
    vg = nc.dram_tensor("C1_vg", [8 * TOK, 1040], BF16, kind="Internal")
    rkg = nc.dram_tensor("C1_rkg", [8 * 1032, 512], BF16, kind="Internal")
    rvg = nc.dram_tensor("C1_rvg", [8 * 512, 1040], BF16, kind="Internal")
    S.cc("AllGather", ALU.bypass, GROUPS8, k_o.ap().opt(), kg.ap().opt(), tag="ccK1")
    S.cc("AllGather", ALU.bypass, GROUPS8, v_o.ap().opt(), vg.ap().opt(), tag="ccV1")
    S.cc("AllGather", ALU.bypass, GROUPS8, rk_o.ap().opt(), rkg.ap().opt(), tag="ccRK")
    S.cc("AllGather", ALU.bypass, GROUPS8, rv_o.ap().opt(), rvg.ap().opt(), tag="ccRV")
    FC["io"].update(C1_q=q_o, C1_qn=qn_o, C1_kg=kg, C1_vg=vg, C1_rkg=rkg, C1_rvg=rvg)
    return nc


def pack_C1(inputs, x2, core):
    sl = x2.reshape(NCORE * TOK, D)[core * TOK:(core + 1) * TOK]
    past = np.asarray(inputs["L1_attn_past_states"], np.float32).reshape(4096, D)
    psl = past[core * 512:(core + 1) * 512]
    m = {"x_fm": np.ascontiguousarray(sl.T.astype(np.float32)),
         "past_fm": np.ascontiguousarray(psl.T)}
    for t, wk, bk in (("q", "Wq", "bq"), ("k", "Wk", "bk"), ("v", "Wv", "bv"),
                      ("rk", "Wrk", "brk"), ("rv", "Wrv", "brv")):
        m[f"W{t}"] = np.ascontiguousarray(np.asarray(inputs[f"L1_attn_{wk}"], np.float32))
        m[f"b{t}c"] = chunk_col(inputs[f"L1_attn_{bk}"])
    return m


def build_C2(FC=None):
    if FC is None:
        nc, S = _newprog()
        es = None
        x_d = nc.dram_tensor("x_fm", [D, TOK], F32, kind="ExternalInput")
        q_d = nc.dram_tensor("q_fm", [D, TOK], BF16, kind="ExternalInput")
        qn_d = nc.dram_tensor("qn", [1, 16], F32, kind="ExternalInput")
        k_d = nc.dram_tensor("kfull", [4 * 1032, TOK], BF16, kind="ExternalInput")
        v_d = nc.dram_tensor("vfull", [4 * TOK, 1040], BF16, kind="ExternalInput")
        rk_d = nc.dram_tensor("rkfull", [8 * 1032, 512], BF16, kind="ExternalInput")
        rv_d = nc.dram_tensor("rvfull", [8 * 512, 1040], BF16, kind="ExternalInput")
        out_d = nc.dram_tensor("x3_fm", [D, TOK], F32, kind="ExternalOutput")
        C = _mk_consts(nc)
        emit_consts(S, C)
        x_res = None
    else:
        nc, S, C, x_res, es = FC["nc"], FC["S"], FC["C"], FC["x_res"], FC["es"]
        q_d = FC["io"]["C1_q"]; qn_d = FC["io"]["C1_qn"]
        k_d = FC["io"]["C1_kg"]; v_d = FC["io"]["C1_vg"]
        rk_d = FC["io"]["C1_rkg"]; rv_d = FC["io"]["C1_rvg"]
    wo_d = nc.dram_tensor("Wo", [D, D], F32, kind="ExternalInput")
    boc_d = nc.dram_tensor("boc", [P, NCH], F32, kind="ExternalInput")
    wrq_d = nc.dram_tensor("Wrq", [D, D], F32, kind="ExternalInput")
    brqc_d = nc.dram_tensor("brqc", [P, NCH], F32, kind="ExternalInput")
    wgA_d = nc.dram_tensor("WgA", [D, D], F32, kind="ExternalInput")
    wgB_d = nc.dram_tensor("WgB", [D, D], F32, kind="ExternalInput")
    bgc_d = nc.dram_tensor("bgc", [P, NCH], F32, kind="ExternalInput")

    q_fm = FC["outb"] if FC is not None else nc.alloc_sbuf_tensor("q_sb", [P, NCH, TOK], BF16)
    nctx = AL(nc, es, "C2_nctx", [P, NCH, TOK], BF16)
    out_fm = AL(nc, es, "C2_out_fm", [P, NCH, TOK], BF16)
    rec_fm = q_fm
    gate_fm = nctx
    war = AL(nc, es, "C2_war", [P, 16384], BF16)
    nc_ctxan = (AL(nc, es, "C2_ctxAn", [64, 1024], BF16),
                AL(nc, es, "C2_ctxBn", [64, 1024], BF16))
    wbuf = war[:, 0:8192].rearrange("p (c w) -> p c w", w=1024)
    wbuf2 = war[:, 8192:16384].rearrange("p (c w) -> p c w", w=1024)
    K_sb = war[:, 0:4096].rearrange("p (s w) -> p s w", w=1024)
    V_sb = war[:, 4096:8256].rearrange("p (s t w) -> p s t w", t=8, w=130)
    Pt = war[:, 8256:12352].rearrange("p (s w) -> p s w", w=2048)
    shF = war[:, 8256:12352].bitcast(F32)
    ctxAn = nc_ctxan[0]
    ctxBn = nc_ctxan[1]
    onesF = AL(nc, es, "C2_onesF", [P, P], F32)
    Ilo = AL(nc, es, "C2_Ilo", [64, P], BF16)
    Ihi = AL(nc, es, "C2_Ihi", [64, P], BF16)
    ishft = AL(nc, es, "C2_ishft", [64, P], F32)
    rows = AL(nc, es, "C2_rows_sb", [1, 192], F32)
    negc_s = AL(nc, es, "C2_negc_s", [P, 8], F32)
    negc_r = AL(nc, es, "C2_negc_r", [P, 8], F32)
    bcol = AL(nc, es, "C2_bcol", [P, NCH], F32)
    bcol2 = AL(nc, es, "C2_bcol2", [P, NCH], F32)
    sqstat = AL(nc, es, "C2_sqstat", [P, 512], BF16)
    statrow = AL(nc, es, "C2_statrow", [1, 16], F32)
    t1f = AL(nc, es, "C2_t1f", [P, 512], F32)
    t2f = AL(nc, es, "C2_t2f", [P, 512], BF16)

    S.add("gpsimd", lambda g: g.memset(onesF[:, :], 1.0), writes=[onesF[:, :]])
    S.add("gpsimd", lambda g: g.iota(ishft[:, :], [[1, P]], base=0,
                                     channel_multiplier=-1,
                                     allow_small_or_imprecise_dtypes=True),
          writes=[ishft[:, :]], tag="ilo_i")
    S.add("vector", lambda v: v.tensor_single_scalar(Ilo[:, :], ishft[:, :], 0.0,
                                                     ALU.is_equal),
          reads=[ishft[:, :]], writes=[Ilo[:, :]], tag="ilo")
    S.add("vector", lambda v: v.tensor_single_scalar(Ihi[:, :], ishft[:, :], 64.0,
                                                     ALU.is_equal),
          reads=[ishft[:, :]], writes=[Ihi[:, :]], tag="ihi")

    if FC is None:
        xs = AL(nc, es, "C2_xs", [P, 2, 512], F32)
        ys = AL(nc, es, "C2_ys", [P, 2, 512], F32)
    for c in range(NCH):
        S.dma("sync", q_fm[:, c, :], q_d.ap()[c * P:(c + 1) * P, :], tag="ld_q")
    if DEBUG_B2:
        S.dma("sync", dbg_x0.ap(), x_res[:, 0, 0:1024], tag="dbg_x0")
    S.dma("sync", rows[0:1, 0:16], qn_d.ap(), tag="ld_qn")
    nrk = 4 if FC is None else 8
    kst = bass.AP(k_d, 1024 * TOK, [[1, 1], [1032 * TOK, nrk], [1, 8]])
    S.dma("gpsimd", rows[0:1, 16:16 + 8 * nrk], kst, tag="ld_kst",
          allow_slow_non_contiguous=True)
    S.add("vector", lambda v: v.tensor_reduce(
              rows[:, 80:88],
              rows[:, 16:16 + 8 * nrk].rearrange("o (r c) -> o c r", c=8),
              AX.X, ALU.max),
          reads=[rows[:, 16:16 + 8 * nrk]], writes=[rows[:, 80:88]], tag="knmax")
    emit_shift(S, C, rows[:, 0:8], rows[:, 80:88], 8, negc_s, rows[:, 96:160], "shs")

    def attn_pass(tag, q_src, negc, ranks, kwidth, kd, vd, ngroups_per_rank, ntile,
                  epilogue):
        with (nc.psum_tensor(f"{tag}_sps", [P, 2048], F32) as sps,
              nc.psum_tensor(f"{tag}_cA", [65, 1024], F32) as cA,
              nc.psum_tensor(f"{tag}_cB", [65, 1024], F32) as cB):
            for qh in range(2):
                for cp in range(NCH):
                    groups = [(rk, gg) for rk in ranks
                              for gg in range(ngroups_per_rank)]
                    nt_total = len(groups) * ntile
                    klen = kwidth if ngroups_per_rank == 1 else 1024
                    vw = kwidth if ngroups_per_rank == 1 else TOK

                    def load_group(gi, qh=qh, cp=cp, klen=klen, vw=vw, groups=groups):
                        rk, gg = groups[gi]
                        slot = gi % 4
                        kcol = gg * 1024
                        if FC is None or tag != "sa":
                            kap = kd.ap()[1032 * rk + cp * P:1032 * rk + (cp + 1) * P,
                                          kcol:kcol + klen]
                            S.dma("sync", K_sb[:, slot, 0:klen], kap,
                                  tag=f"{tag}K{qh}_{cp}_{gi}")
                        else:
                            kap = bass.AP(kd, (2 * (1032 * rk + cp * P)
                                               + kcol // 1024) * 1024,
                                          [[2048, P], [1, 1024]])
                            S.idma(K_sb[:, slot, 0:klen], kap, FC["bk2"][:, :],
                                   tag=f"{tag}K{qh}_{cp}_{gi}")
                        for t in range(ntile):
                            if FC is None or tag != "sa":
                                vap = vd.ap()[vw * rk + kcol + t * P:
                                              vw * rk + kcol + (t + 1) * P,
                                              130 * cp:130 * cp + 130]
                                S.dma("sync", V_sb[:, slot, t, :], vap,
                                      tag=f"{tag}V{qh}_{cp}_{gi}_{t}")
                            else:
                                vap = bass.AP(
                                    vd, ((TOK * rk + kcol + t * P) * 8 + cp) * 130,
                                    [[8 * 130, P], [1, 130]])
                                S.idma(V_sb[:, slot, t, :], vap, FC["bv8"][:, :],
                                       tag=f"{tag}V{qh}_{cp}_{gi}_{t}")

                    for pf in range(min(3, len(groups))):
                        load_group(pf)
                    pend = [None]
                    it = 0
                    for gi in range(len(groups)):
                        slot = gi % 4
                        if gi + 3 < len(groups):
                            load_group(gi + 3)
                        for t in range(ntile):
                            ps_ = it % 2

                            def qkmmA(tt, t=t, slot=slot, qh=qh, cp=cp):
                                for piece in range(2):
                                    qpc = slice(qh * 1024 + piece * 512,
                                                qh * 1024 + piece * 512 + 512)
                                    ins = tt.matmul(
                                        sps[:, piece * 512:piece * 512 + 512],
                                        K_sb[0:64, slot, t * P:(t + 1) * P],
                                        q_src[0:64, cp, qpc],
                                        start=True, stop=True)
                                return ins

                            def qkmmB(tt, t=t, slot=slot, qh=qh, cp=cp):
                                for piece in range(2):
                                    qpc = slice(qh * 1024 + piece * 512,
                                                qh * 1024 + piece * 512 + 512)
                                    ins = tt.matmul(
                                        sps[:, 1024 + piece * 512:
                                            1024 + piece * 512 + 512],
                                        K_sb[64:128, slot, t * P:(t + 1) * P],
                                        q_src[64:128, cp, qpc],
                                        start=True, stop=True,
                                        tile_position=(64, 0))
                                return ins
                            S.add("tensor", qkmmA,
                                  reads=[K_sb[0:64, slot, t * P:(t + 1) * P],
                                         q_src[0:64, cp,
                                               qh * 1024:qh * 1024 + 1024]],
                                  writes=[sps[:, 0:1024]], tag=f"{tag}qkA")
                            S.add("tensor", qkmmB,
                                  reads=[K_sb[64:128, slot, t * P:(t + 1) * P],
                                         q_src[64:128, cp,
                                               qh * 1024:qh * 1024 + 1024]],
                                  writes=[sps[:, 1024:2048]], tag=f"{tag}qkB")
                            if pend[0] is not None:
                                pend[0]()
                            S.add("scalar", lambda s, cp=cp, ps_=ps_: s.activation(
                                      Pt[:, ps_, 0:1024], sps[:, 0:1024], AF.Exp,
                                      bias=negc[:, cp:cp + 1]),
                                  reads=[sps[:, 0:1024], negc[:, cp:cp + 1]],
                                  writes=[Pt[:, ps_, 0:1024]], tag=f"{tag}expA")
                            S.add("scalar", lambda s, cp=cp, ps_=ps_: s.activation(
                                      Pt[:, ps_, 1024:2048], sps[:, 1024:2048], AF.Exp,
                                      bias=negc[:, cp:cp + 1]),
                                  reads=[sps[:, 1024:2048], negc[:, cp:cp + 1]],
                                  writes=[Pt[:, ps_, 1024:2048]], tag=f"{tag}expB")

                            def mk_pv(it=it, ps_=ps_, slot=slot, t=t,
                                      nt_total=nt_total):
                                def emit():
                                    def pvmm(tt):
                                        for piece in range(2):
                                            tt.matmul(
                                                cA[:, piece * 512:piece * 512 + 512],
                                                V_sb[:, slot, t, 0:65],
                                                Pt[:, ps_, piece * 512:
                                                   piece * 512 + 512],
                                                start=(it == 0),
                                                stop=(it == nt_total - 1))
                                        for piece in range(2):
                                            ins = tt.matmul(
                                                cB[:, piece * 512:piece * 512 + 512],
                                                V_sb[:, slot, t, 65:130],
                                                Pt[:, ps_, 1024 + piece * 512:
                                                   1024 + piece * 512 + 512],
                                                start=(it == 0),
                                                stop=(it == nt_total - 1))
                                        return ins
                                    S.add("tensor", pvmm,
                                          reads=[V_sb[:, slot, t, :],
                                                 Pt[:, ps_, :]],
                                          writes=[cA[:, :], cB[:, :]],
                                          tag=f"{tag}pv")
                                return emit
                            pend[0] = mk_pv()
                            it += 1
                    pend[0]()
                    pend[0] = None
                    lA = shF[64:65, 0:1024]
                    lB = shF[64:65, 1024:2048]
                    S.add("vector", lambda v: v.tensor_scalar_add(lA, cA[64:65, :],
                                                                  1e-30),
                          reads=[cA[:, :]], writes=[lA], tag=f"{tag}la")
                    S.add("vector", lambda v: v.tensor_scalar_add(lB, cB[64:65, :],
                                                                  1e-30),
                          reads=[cB[:, :]], writes=[lB], tag=f"{tag}lb")
                    S.add("vector", lambda v: v.reciprocal(lA, lA), reads=[lA],
                          writes=[lA], tag=f"{tag}lra")
                    S.add("vector", lambda v: v.reciprocal(lB, lB), reads=[lB],
                          writes=[lB], tag=f"{tag}lrb")
                    def bcmm(t):
                        for p0 in (0, 512):
                            t.matmul(sps[0:64, p0:p0 + 512], onesF[64:65, 0:64],
                                     lA[:, p0:p0 + 512], start=True, stop=True)
                            ins = t.matmul(sps[0:64, 1024 + p0:1024 + p0 + 512],
                                           onesF[64:65, 0:64], lB[:, p0:p0 + 512],
                                           start=True, stop=True)
                        return ins
                    S.add("tensor", bcmm, reads=[onesF[:, :], lA, lB],
                          writes=[sps[:, :]], tag=f"{tag}bc")
                    S.add("vector", lambda v: v.tensor_copy(shF[0:64, :],
                                                            sps[0:64, :]),
                          reads=[sps[0:64, :]], writes=[shF[0:64, :]], tag=f"{tag}lvc")
                    S.add("vector", lambda v: v.tensor_mul(ctxAn[:, :], cA[0:64, :],
                                                           shF[0:64, 0:1024]),
                          reads=[cA[:, :], shF[0:64, :]], writes=[ctxAn[:, :]],
                          tag=f"{tag}na")
                    S.add("vector", lambda v: v.tensor_mul(ctxBn[:, :], cB[0:64, :],
                                                           shF[0:64, 1024:2048]),
                          reads=[cB[:, :], shF[0:64, :]], writes=[ctxBn[:, :]],
                          tag=f"{tag}nb")
                    def pkmm(t):
                        for p0 in (0, 512):
                            t.matmul(sps[:, p0:p0 + 512], Ilo[:, :],
                                     ctxAn[:, p0:p0 + 512], start=True, stop=False)
                            ins = t.matmul(sps[:, p0:p0 + 512], Ihi[:, :],
                                           ctxBn[:, p0:p0 + 512], start=False,
                                           stop=True)
                        return ins
                    S.add("tensor", pkmm,
                          reads=[Ilo[:, :], Ihi[:, :], ctxAn[:, :], ctxBn[:, :]],
                          writes=[sps[:, 0:1024]], tag=f"{tag}pk")
                    epilogue(qh, cp, sps)
    def self_epi(qh, cp, sps):
        S.add("vector", lambda v, qh=qh, cp=cp: v.tensor_copy(
                  nctx[:, cp, qh * 1024:qh * 1024 + 1024], sps[:, 0:1024]),
              reads=[sps[:, 0:1024]], writes=[nctx[:, cp, qh * 1024:qh * 1024 + 1024]],
              tag=f"sepi{qh}_{cp}")
    attn_pass("sa", q_fm, negc_s, [0, 1, 2, 3], TOK, k_d, v_d, 2, 8, self_epi)

    for c in range(NCH):
        S.dma("gpsimd", wbuf[:, c, :], wo_d.ap()[c * P:(c + 1) * P, :], tag="ld_wo")
    S.dma("sync", bcol[:, :], boc_d.ap(), tag="ld_bo")
    with nc.psum_tensor("C2_pps", [P, 1024], F32) as pps:
        emit_proj_generic(S, C, wbuf, lambda i: nctx[:, i, :],
                          lambda o: out_fm[:, o, :], pps, NCH, TOK, AF.Identity,
                          lambda o: bcol[:, o:o + 1], 1.0, tag="wo")
        for c in range(NCH):
            S.dma("gpsimd", wbuf[:, c, :], wrq_d.ap()[c * P:(c + 1) * P, :],
                  tag="ld_wrq")
        S.dma("sync", bcol2[:, :], brqc_d.ap(), tag="ld_brq")
        S.add("vector", lambda v: v.tensor_scalar_mul(bcol2[:, :], bcol2[:, :],
                                                      1.0 / 8.0),
              reads=[bcol2[:, :]], writes=[bcol2[:, :]], tag="brq8")
        emit_proj_generic(S, C, wbuf, lambda i: out_fm[:, i, :],
                          lambda o: nctx[:, o, :], pps, NCH, TOK, AF.Identity,
                          lambda o: bcol2[:, o:o + 1], 1.0 / 8.0, tag="wrq")
    emit_tok_norms(S, C, nctx, TOK, statrow[:, 0:16], sqstat, True, "rqn")
    rkst = bass.AP(rk_d, 1024 * 512, [[1, 1], [1032 * 512, 8], [1, 8]])
    S.dma("gpsimd", rows[0:1, 16:80], rkst, tag="ld_rkst",
          allow_slow_non_contiguous=True)
    S.add("vector", lambda v: v.tensor_reduce(
              rows[:, 80:88], rows[:, 16:80].rearrange("o (r c) -> o c r", c=8),
              AX.X, ALU.max),
          reads=[rows[:, 16:80]], writes=[rows[:, 80:88]], tag="rknmax")
    emit_shift(S, C, statrow[:, 0:8], rows[:, 80:88], 8, negc_r, rows[:, 96:160],
               "shr")

    for p2 in range(2):
        def rec_epi(qh, cp, sps, p2=p2):
            if p2 == 0:
                S.add("vector", lambda v, qh=qh, cp=cp: v.tensor_scalar_mul(
                          rec_fm[:, cp, qh * 1024:qh * 1024 + 1024], sps[:, 0:1024],
                          0.5),
                      reads=[sps[:, 0:1024]],
                      writes=[rec_fm[:, cp, qh * 1024:qh * 1024 + 1024]],
                      tag=f"repi{p2}_{qh}_{cp}")
            else:
                S.add("vector", lambda v, qh=qh, cp=cp: v.scalar_tensor_tensor(
                          rec_fm[:, cp, qh * 1024:qh * 1024 + 1024], sps[:, 0:1024],
                          0.5, rec_fm[:, cp, qh * 1024:qh * 1024 + 1024],
                          ALU.mult, ALU.add),
                      reads=[sps[:, 0:1024],
                             rec_fm[:, cp, qh * 1024:qh * 1024 + 1024]],
                      writes=[rec_fm[:, cp, qh * 1024:qh * 1024 + 1024]],
                      tag=f"repi{p2}_{qh}_{cp}")
        attn_pass(f"ra{p2}", nctx, negc_r, [4 * p2 + i for i in range(4)], 512,
                  rk_d, rv_d, 1, 4, rec_epi)

    for c in range(NCH):
        S.dma("gpsimd", wbuf[:, c, :], wgA_d.ap()[c * P:(c + 1) * P, :], tag="ld_wga")
        S.dma("gpsimd", wbuf2[:, c, :], wgB_d.ap()[c * P:(c + 1) * P, :], tag="ld_wgb")
    S.dma("sync", bcol[:, :], bgc_d.ap(), tag="ld_bg")
    with nc.psum_tensor("C2_gps", [P, 1024], F32) as gps:
        for o in range(NCH):
            for piece in range(4):
                sl = slice(piece * 512, piece * 512 + 512)
                pb = gps[:, ((o * 4 + piece) % 2) * 512:((o * 4 + piece) % 2) * 512 + 512]
                def gmm(t, o=o, sl=sl, pb=pb):
                    for i in range(NCH):
                        t.matmul(pb, wbuf[:, i, o * P:(o + 1) * P],
                                 out_fm[:, i, sl], start=(i == 0), stop=False)
                    for i in range(NCH):
                        ins = t.matmul(pb, wbuf2[:, i, o * P:(o + 1) * P],
                                       rec_fm[:, i, sl],
                                       start=False, stop=(i == NCH - 1))
                    return ins
                S.add("tensor", gmm,
                      reads=[wbuf[:, :, :], wbuf2[:, :, :], out_fm[:, :, sl],
                             rec_fm[:, :, sl]],
                      writes=[pb], tag=f"g_{o}_{piece}")
                S.add("scalar", lambda s, o=o, sl=sl, pb=pb: s.activation(
                          gate_fm[:, o, sl], pb, AF.Sigmoid, bias=bcol[:, o:o + 1]),
                      reads=[pb, bcol[:, o:o + 1]], writes=[gate_fm[:, o, sl]],
                      tag=f"ge_{o}_{piece}")
    for c in range(NCH):
        for piece in range(4):
            sl = slice(piece * 512, piece * 512 + 512)
            S.add("vector", lambda v, c=c, sl=sl: v.tensor_sub(
                      t2f[:, :], rec_fm[:, c, sl], out_fm[:, c, sl]),
                  reads=[rec_fm[:, c, sl], out_fm[:, c, sl]], writes=[t2f[:, :]],
                  tag=f"cb1_{c}_{piece}")
            S.add("vector", lambda v, c=c, sl=sl: v.tensor_mul(
                      t2f[:, :], t2f[:, :], gate_fm[:, c, sl]),
                  reads=[t2f[:, :], gate_fm[:, c, sl]], writes=[t2f[:, :]],
                  tag=f"cb2_{c}_{piece}")
            S.add("vector", lambda v, c=c, sl=sl: v.tensor_add(
                      t1f[:, :], t2f[:, :], out_fm[:, c, sl]),
                  reads=[t2f[:, :], out_fm[:, c, sl]], writes=[t1f[:, :]],
                  tag=f"cb3_{c}_{piece}")
            if FC is None:
                bufi = (c * 4 + piece) % 2
                S.dma("sync", xs[:, bufi, :],
                      x_d.ap()[c * P:(c + 1) * P, sl], tag=f"ld_x{c}_{piece}")
                S.add("vector", lambda v, bufi=bufi: v.tensor_add(
                          ys[:, bufi, :], t1f[:, :], xs[:, bufi, :]),
                      reads=[t1f[:, :], xs[:, bufi, :]], writes=[ys[:, bufi, :]],
                      tag=f"cb4_{c}_{piece}")
                S.dma("sync", out_d.ap()[c * P:(c + 1) * P, sl], ys[:, bufi, :],
                      tag=f"st_x{c}_{piece}")
            else:
                S.add("vector", lambda v, c=c, sl=sl: v.tensor_add(
                          x_res[:, c, sl], x_res[:, c, sl], t1f[:, :]),
                      reads=[t1f[:, :], x_res[:, c, sl]], writes=[x_res[:, c, sl]],
                      tag=f"cb4_{c}_{piece}")
    if FC is None:
        S.barrier_read("sync", [out_d.ap()])
        S.build()
    return nc
def pack_C2(inputs, x2, core):
    sl = x2.reshape(NCORE * TOK, D)[core * TOK:(core + 1) * TOK]
    wg = np.asarray(inputs["L1_attn_Wg"], np.float32)
    return {"x_fm": np.ascontiguousarray(sl.T.astype(np.float32)),
            "Wo": np.ascontiguousarray(np.asarray(inputs["L1_attn_Wo"], np.float32)),
            "boc": chunk_col(inputs["L1_attn_bo"]),
            "Wrq": np.ascontiguousarray(np.asarray(inputs["L1_attn_Wrq"], np.float32)),
            "brqc": chunk_col(inputs["L1_attn_brq"]),
            "WgA": np.ascontiguousarray(wg[0:1024]),
            "WgB": np.ascontiguousarray(wg[1024:2048]),
            "bgc": chunk_col(inputs["L1_attn_bg"])}


def build_D(FC=None):
    if FC is None:
        nc, S = _newprog()
        es = None
        x_d = nc.dram_tensor("x_fm", [D, TOK], F32, kind="ExternalInput")
        C = _mk_consts(nc)
        x_res = nc.alloc_sbuf_tensor("x_res", [P, NCH, TOK], F32)
        emit_consts(S, C)
        for c in range(NCH):
            S.dma("sync", x_res[:, c, :], x_d.ap()[c * P:(c + 1) * P, :], tag="ld_x")
        H = nc.alloc_sbuf_tensor("Hscr", [P, NCH, TOK], BF16)
    else:
        nc, S, C, x_res, es = FC["nc"], FC["S"], FC["C"], FC["x_res"], FC["es"]
        H = FC["outb"]
    w_d = nc.dram_tensor("wT", [D, D], F32, kind="ExternalInput")
    al_d = nc.dram_tensor("alpha", [1, 1], F32, kind="ExternalInput")
    bc_d = nc.dram_tensor("biasc", [P, NCH], F32, kind="ExternalInput")
    out_d = nc.dram_tensor("y_fm", [D, TOK], F32, kind="ExternalOutput")

    xn = AL(nc, es, "D_xn", [P, NCH, TOK], BF16)
    Hf = H.bitcast(F32).rearrange("p c w -> p (c w)")
    SCRF = AL(nc, es, "D_SCRF", [P, TOK], F32)
    wfst = AL(nc, es, "D_wfst", [P, 2, 1024], F32)
    tern = AL(nc, es, "D_tern", [P, NCH, 1024], BF16)
    bcol = AL(nc, es, "D_bcol", [P, NCH], F32)
    sums8 = AL(nc, es, "D_sums8", [P, NCH], F32)
    srow = AL(nc, es, "D_srow", [1, 16], F32)
    deltaB = AL(nc, es, "D_deltaB", [P, 2], F32)
    absw = AL(nc, es, "D_absw", [P, 1024], F32)
    sgn = AL(nc, es, "D_sgn", [P, 1024], BF16)
    msk = AL(nc, es, "D_msk", [P, 1024], BF16)
    alr = AL(nc, es, "D_alr", [1, 1], F32)
    ones_col_f = AL(nc, es, "D_ones_col", [P, 1], F32)

    S.add("gpsimd", lambda g: g.memset(ones_col_f[:, :], 1.0),
          writes=[ones_col_f[:, :]])
    for c in range(NCH):
        S.dma("sync", wfst[:, c % 2, :], w_d.ap()[c * P:(c + 1) * P, :], tag="ld_w")
        S.add("scalar", lambda s, c=c: s.activation(absw[:, :], wfst[:, c % 2, :],
                                                    AF.Abs),
              reads=[wfst[:, c % 2, :]], writes=[absw[:, :]], tag=f"q_abs{c}")
        S.add("vector", lambda v, c=c: v.tensor_reduce(sums8[:, c:c + 1], absw[:, :],
                                                       AX.X, ALU.add),
              reads=[absw[:, :]], writes=[sums8[:, c:c + 1]], tag=f"q_sum{c}")
    S.dma("sync", bcol[:, :], bc_d.ap(), tag="ld_b")
    S.dma("sync", alr[:, :], al_d.ap(), tag="ld_al")

    lnscr = dict(xb=H[:, 0:2, :], sq=H[:, 2:4, :],
                 mrow=Hf[0:1, 0:TOK], wrow=Hf[0:1, TOK:2 * TOK],
                 mb=H[:, 4, :], ab=H[:, 5, :], t1=SCRF[:, :])
    emit_ln(S, C, x_res, xn, lnscr, tag="ln4")

    with nc.psum_tensor("qps", [1, 16], F32) as qps:
        S.add("tensor", lambda t: t.matmul(qps[:, 0:8], ones_col_f[:, :],
                                           sums8[:, :], start=True, stop=True),
              reads=[ones_col_f[:, :], sums8[:, :]], writes=[qps[:, 0:8]],
              tag="q_mm")
        S.add("vector", lambda v: v.tensor_reduce(srow[:, 0:1], qps[:, 0:8],
                                                  AX.X, ALU.add),
              reads=[qps[:, 0:8]], writes=[srow[:, 0:1]], tag="q_tot")
    S.add("vector", lambda v: v.tensor_scalar_mul(srow[:, 1:2], srow[:, 0:1],
                                                  0.7 / (1024.0 * 1024.0)),
          reads=[srow[:, 0:1]], writes=[srow[:, 1:2]], tag="q_del")
    with nc.psum_tensor("qbp", [P, 2], F32) as qbp:
        S.add("tensor", lambda t: t.matmul(qbp[:, 0:1], C["ones_row_f"][:, :],
                                           srow[:, 1:2], start=True, stop=True),
              reads=[C["ones_row_f"][:, :], srow[:, 1:2]], writes=[qbp[:, 0:1]],
              tag="q_b1")
        S.add("tensor", lambda t: t.matmul(qbp[:, 1:2], C["ones_row_f"][:, :],
                                           alr[:, :], start=True, stop=True),
              reads=[C["ones_row_f"][:, :], alr[:, :]], writes=[qbp[:, 1:2]],
              tag="q_b2")
        S.add("vector", lambda v: v.tensor_copy(deltaB[:, :], qbp[:, :]),
              reads=[qbp[:, :]], writes=[deltaB[:, :]], tag="q_bc")
    for c in range(NCH):
        S.dma("sync", wfst[:, c % 2, :], w_d.ap()[c * P:(c + 1) * P, :], tag="ld_w2")
        S.add("scalar", lambda s, c=c: s.sign(sgn[:, :], wfst[:, c % 2, :]),
              reads=[wfst[:, c % 2, :]], writes=[sgn[:, :]], tag=f"q_sgn{c}")
        S.add("scalar", lambda s, c=c: s.activation(absw[:, :], wfst[:, c % 2, :],
                                                    AF.Abs),
              reads=[wfst[:, c % 2, :]], writes=[absw[:, :]], tag=f"q_ab2{c}")
        S.add("vector", lambda v, c=c: v.tensor_scalar(msk[:, :], absw[:, :],
                                                       deltaB[:, 0:1], None,
                                                       ALU.is_gt),
              reads=[absw[:, :], deltaB[:, 0:1]], writes=[msk[:, :]], tag=f"q_m{c}")
        S.add("vector", lambda v, c=c: v.tensor_mul(tern[:, c, :], msk[:, :],
                                                    sgn[:, :]),
              reads=[msk[:, :], sgn[:, :]], writes=[tern[:, c, :]], tag=f"q_t{c}")
    with nc.psum_tensor("dps", [P, 1024], F32) as dps:
        for o in range(NCH):
            for piece in range(4):
                sl = slice(piece * 512, piece * 512 + 512)
                pb = dps[:, ((o * 4 + piece) % 2) * 512:((o * 4 + piece) % 2) * 512 + 512]
                def dmm(t, o=o, sl=sl, pb=pb):
                    for i in range(NCH):
                        ins = t.matmul(pb, tern[:, i, o * P:(o + 1) * P],
                                       xn[:, i, sl],
                                       start=(i == 0), stop=(i == NCH - 1))
                    return ins
                S.add("tensor", dmm, reads=[tern[:, :, :], xn[:, :, sl]],
                      writes=[pb], tag=f"d_mm{o}_{piece}")
                S.add("vector", lambda v, o=o, sl=sl, pb=pb: v.scalar_tensor_tensor(
                          SCRF[:, 0:512], pb, deltaB[:, 1:2],
                          bcol[:, o:o + 1].broadcast_to([P, 512]), ALU.mult, ALU.add),
                      reads=[pb, deltaB[:, 1:2], bcol[:, o:o + 1]],
                      writes=[SCRF[:, 0:512]], tag=f"d_e{o}_{piece}")
                S.add("vector", lambda v, o=o, sl=sl: v.tensor_add(
                          x_res[:, o, sl], x_res[:, o, sl], SCRF[:, 0:512]),
                      reads=[SCRF[:, 0:512], x_res[:, o, sl]],
                      writes=[x_res[:, o, sl]], tag=f"d_r{o}_{piece}")
    for c in range(NCH):
        S.dma("sync", out_d.ap()[c * P:(c + 1) * P, :], x_res[:, c, :], tag="st_y")
    S.barrier_read("sync", [out_d.ap()])
    if FC is None:
        S.build()
    return nc


def pack_D(inputs, x3, core):
    sl = x3.reshape(NCORE * TOK, D)[core * TOK:(core + 1) * TOK]
    return {"x_fm": np.ascontiguousarray(sl.T.astype(np.float32)),
            "wT": np.ascontiguousarray(np.asarray(inputs["L1_ffn_weight"],
                                                  np.float32).T),
            "alpha": np.asarray(inputs["L1_ffn_alpha"], np.float32).reshape(1, 1),
            "biasc": chunk_col(inputs["L1_ffn_bias"])}


def build_full(n_phases=6):
    nc, S = _newprog()
    C = _mk_consts(nc)
    x_d = nc.dram_tensor("x_fm", [D, TOK], F32, kind="ExternalInput")
    x_res = nc.alloc_sbuf_tensor("x_res", [P, NCH, TOK], F32)
    outb = nc.alloc_sbuf_tensor("outb", [P, NCH, TOK], BF16)
    emit_consts(S, C)
    bk2_d = nc.dram_tensor("bk2_idx", [P, 1], mybir.dt.int32, kind="ExternalInput")
    bv_d = nc.dram_tensor("bv_idx", [P, 1], mybir.dt.int32, kind="ExternalInput")
    bv8_d = nc.dram_tensor("bv8_idx", [P, 1], mybir.dt.int32, kind="ExternalInput")
    bk2 = nc.alloc_sbuf_tensor("bk2_sb", [P, 1], mybir.dt.int32)
    bv = nc.alloc_sbuf_tensor("bv_sb", [P, 1], mybir.dt.int32)
    bv8 = nc.alloc_sbuf_tensor("bv8_sb", [P, 1], mybir.dt.int32)
    S.dma("sync", bk2[:, :], bk2_d.ap(), tag="ld_bk2")
    S.dma("sync", bv[:, :], bv_d.ap(), tag="ld_bv")
    S.dma("sync", bv8[:, :], bv8_d.ap(), tag="ld_bv8")
    for c in range(NCH):
        S.dma("sync", x_res[:, c, :], x_d.ap()[c * P:(c + 1) * P, :], tag="ld_x")
    FC = dict(nc=nc, S=S, C=C, x_res=x_res, outb=outb, io={}, es=None,
              bk2=bk2, bv=bv, bv8=bv8)
    phases = (build_A, build_B1, build_B2, build_C1, build_C2, build_D)[:n_phases]
    for phase in phases:
        with ExitStack() as es:
            FC["es"] = es
            phase(FC)
    if n_phases < 6:
        yd = nc.dram_tensor("y_fm", [D, TOK], F32, kind="ExternalOutput")
        for c in range(NCH):
            S.dma("sync", yd.ap()[c * P:(c + 1) * P, :], x_res[:, c, :], tag="st_y")
        S.barrier_read("sync", [yd.ap()])
    S.build()
    return nc


def pack_full(inputs, core):
    m = pack_A(inputs, core)
    b = core // 4
    m["bk2_idx"] = (b * 2 * 4 * 1032 + 2 * np.arange(P, dtype=np.int32)).reshape(P, 1)
    m["bv_idx"] = (b * 4 * TOK + np.arange(P, dtype=np.int32)).reshape(P, 1)
    m["bv8_idx"] = (b * 4 * TOK * 8 + 8 * np.arange(P, dtype=np.int32)).reshape(P, 1)
    xdummy = np.zeros((NCORE * TOK, D), np.float32)
    for pk in (lambda: pack_B1(inputs, xdummy, core),
               lambda: pack_C1(inputs, xdummy, core),
               lambda: pack_C2(inputs, xdummy, core),
               lambda: pack_D(inputs, xdummy, core)):
        d = pk()
        d.pop("x_fm", None)
        m.update(d)
    return m


def kernel_fused(**inputs):
    nc = build_full()
    maps = [pack_full(inputs, c) for c in range(NCORE)]
    res = run_bass_kernel_spmd(nc, maps, core_ids=list(range(NCORE)))
    out = np.empty((NCORE * TOK, D), np.float32)
    for c in range(NCORE):
        out[c * TOK:(c + 1) * TOK] = res.results[c]["y_fm"].T
    return out.reshape(2, NCORE * TOK // 2, D)


def _run(nc, maps):
    res = run_bass_kernel_spmd(nc, maps, core_ids=list(range(NCORE)))
    return res.results


def _asm_x(results, key):
    out = np.empty((NCORE * TOK, D), np.float32)
    for c in range(NCORE):
        out[c * TOK:(c + 1) * TOK] = results[c][key].T
    return out


FUSED = True


def kernel(**inputs):
    if FUSED:
        return kernel_fused(**inputs)
    resA = _run(build_A(), [pack_A(inputs, c) for c in range(NCORE)])
    x1 = _asm_x(resA, "x1_fm")
    resB1 = _run(build_B1(), [pack_B1(inputs, x1, c) for c in range(NCORE)])
    kfull = np.concatenate([resB1[c]["k_o"] for c in range(NCORE)], axis=0)
    vfull = np.concatenate([resB1[c]["v_o"] for c in range(NCORE)], axis=0)
    mapsB2 = []
    for c in range(NCORE):
        b = c // 4
        mapsB2.append({
            "x_fm": np.ascontiguousarray(
                x1[c * TOK:(c + 1) * TOK].T.astype(np.float32)),
            "q_fm": np.ascontiguousarray(resB1[c]["q_fm_o"]),
            "qn": resB1[c]["qn_o"],
            "kfull": np.ascontiguousarray(kfull[b * 4 * 1032:(b + 1) * 4 * 1032]),
            "vfull": np.ascontiguousarray(vfull[b * 4 * TOK:(b + 1) * 4 * TOK]),
        })
    resB2 = _run(build_B2(), mapsB2)
    x2 = _asm_x(resB2, "x2_fm")
    resC1 = _run(build_C1(), [pack_C1(inputs, x2, c) for c in range(NCORE)])
    kfull = np.concatenate([resC1[c]["k_o"] for c in range(NCORE)], axis=0)
    vfull = np.concatenate([resC1[c]["v_o"] for c in range(NCORE)], axis=0)
    rkfull = np.concatenate([resC1[c]["rk_o"] for c in range(NCORE)], axis=0)
    rvfull = np.concatenate([resC1[c]["rv_o"] for c in range(NCORE)], axis=0)
    mapsC2 = []
    for c in range(NCORE):
        b = c // 4
        m = pack_C2(inputs, x2, c)
        m.update({
            "q_fm": np.ascontiguousarray(resC1[c]["q_fm_o"]),
            "qn": resC1[c]["qn_o"],
            "kfull": np.ascontiguousarray(kfull[b * 4 * 1032:(b + 1) * 4 * 1032]),
            "vfull": np.ascontiguousarray(vfull[b * 4 * TOK:(b + 1) * 4 * TOK]),
            "rkfull": np.ascontiguousarray(rkfull),
            "rvfull": np.ascontiguousarray(rvfull),
        })
        mapsC2.append(m)
    resC2 = _run(build_C2(), mapsC2)
    x3 = _asm_x(resC2, "x3_fm")
    resD = _run(build_D(), [pack_D(inputs, x3, c) for c in range(NCORE)])
    y = _asm_x(resD, "y_fm")
    return y.reshape(2, NCORE * TOK // 2, D)


# version 12
# speedup vs baseline: 1.1975x; 1.0297x over previous
import numpy as np
import ml_dtypes
import concourse.bass as bass
import concourse.mybir as mybir
from concourse.bass_utils import run_bass_kernel_spmd

F32 = mybir.dt.float32
BF16 = mybir.dt.bfloat16
AF = mybir.ActivationFunctionType
ALU = mybir.AluOpType
AX = mybir.AxisListType

P = 128
TOK = 2048
NT = TOK // P
NCH = 8
D = 1024
NCORE = 8
EPS_RMS = float(np.finfo(np.float32).eps)
EPS_LN = 1e-5
LAMBDA_INIT = 0.8 - 0.6 * float(np.exp(-0.3 * 0.0))
L0_SCALE = 1.0 / float(np.sqrt(512.0))
BF = ml_dtypes.bfloat16
DEBUG_A = False
DEBUG_B2 = False
B2_LOADS_ONLY = False
NO_CC = False
GRAN = 512

_DT_SIZE = {F32: 4, BF16: 2, mybir.dt.float16: 2, mybir.dt.int32: 4, mybir.dt.uint8: 1}


class Sched:
    COMPUTE = ("tensor", "vector", "scalar", "gpsimd", "sync")

    def __init__(self, nc):
        self.nc = nc
        self.ops = []
        self.last_w = {}
        self.readers = {}
        self._mloc_cache = {}

    def _granules(self, ap):
        h = ap.tensor
        sp = str(h.space)
        if "SBUF" not in sp and "PSUM" not in sp:
            return {("D", h.name)}
        key = h.name
        if key not in self._mloc_cache:
            m = self.nc.lookup_mloc(h)
            addr = m.addr
            if "PSUM" in sp:
                addr = (m.bank or 0) * 2048 + (addr or 0)
            self._mloc_cache[key] = (addr or 0, "P" if "PSUM" in sp else "S")
        base, spc = self._mloc_cache[key]
        esz = _DT_SIZE[ap.dtype]
        row = 1
        for s in h.shape[1:]:
            row *= s
        off = ap.offset
        if not isinstance(off, int):
            off = 0
            lo_e, hi_e = 0, row - 1
        else:
            off = off % row if row else 0
            lo_e = hi_e = off
            pairs = list(ap.ap)[1:]
            for st, ct in pairs:
                if st >= 0:
                    hi_e += st * (ct - 1)
                else:
                    lo_e += st * (ct - 1)
        lo = base + lo_e * esz
        hi = base + hi_e * esz + esz - 1
        return {(spc, g) for g in range(lo // GRAN, hi // GRAN + 1)}

    def _dep_ops(self, reads, writes):
        raw = set()
        war = set()
        rg = set()
        wg = set()
        for ap in reads:
            rg |= self._granules(ap)
        for ap in writes:
            wg |= self._granules(ap)
        for g in rg:
            w = self.last_w.get(g)
            if w is not None:
                raw.add(w)
        for g in wg:
            w = self.last_w.get(g)
            if w is not None:
                war.add(w)
            for r in self.readers.get(g, ()):
                war.add(r)
        return raw, war, rg, wg

    def _record(self, opid, rg, wg):
        for g in rg:
            self.readers.setdefault(g, []).append(opid)
        for g in wg:
            self.last_w[g] = opid
            self.readers[g] = []

    def add(self, engine, fn, reads=(), writes=(), kind="c", tag=""):
        opid = len(self.ops)
        raw, war, rg, wg = self._dep_ops(reads, writes)
        real = set()
        for d in war:
            o = self.ops[d]
            if kind == "c" and o["kind"] == "c" and o["engine"] == engine:
                continue
            real.add(d)
        for d in raw:
            if d == opid:
                continue
            real.add(d)
        self.ops.append(dict(engine=engine, fn=fn, kind=kind, deps=real, tag=tag,
                             need_inc=False))
        self._record(opid, rg, wg)
        return opid

    def dma(self, engine, out, in_, tag="", **kw):
        def fn(eng, out=out, in_=in_, kw=kw):
            return eng.dma_start(out=out, in_=in_, **kw)
        return self.add(engine, fn, reads=[in_], writes=[out], kind="dma", tag=tag)

    def idma(self, out, in_, idxap, tag=""):
        off = in_.offset
        in0 = bass.AP(in_.tensor, 0, [list(p) for p in in_.ap])

        def fn(eng, out=out, in0=in0, idxap=idxap, off=off):
            return eng.indirect_dma_start(
                out, None, in0, bass.IndirectOffsetOnAxis(ap=idxap, axis=0),
                element_offset=off)
        return self.add("gpsimd", fn, reads=[in_, idxap], writes=[out], kind="dma",
                        tag=tag)

    def cc(self, kind, op, replica_groups, in_ap, out_ap, tag=""):
        def fn(eng, kind=kind, op=op, rg=replica_groups, i=in_ap, o=out_ap):
            return eng.collective_compute(kind, op, replica_groups=rg, ins=[i], outs=[o])
        return self.add("gpsimd", fn, reads=[in_ap], writes=[out_ap], kind="cc", tag=tag)

    def build(self, final_waits=True):
        nc = self.nc
        for o in self.ops:
            if o["kind"] in ("dma", "cc"):
                o["need_inc"] = True
            for d in o["deps"]:
                self.ops[d]["need_inc"] = True
        if final_waits:
            pass
        num = {}
        for i, o in enumerate(self.ops):
            q = (o["engine"], "c") if o["kind"] == "c" else (o["engine"], o["kind"])
            o["queue"] = q
            if o["need_inc"]:
                num[q] = num.get(q, 0) + 1
                o["inc_no"] = num[q]
        sems = {}
        for q in num:
            sems[q] = nc.ctx.enter_context(nc.semaphore(f"sem_{q[0]}_{q[1]}"))
        by_engine = {}
        for i, o in enumerate(self.ops):
            by_engine.setdefault(o["engine"], []).append(i)

        def emit_for(eng_name):
            def body(eng):
                seen = {}
                for i in by_engine.get(eng_name, ()):
                    o = self.ops[i]
                    wv = {}
                    for d in o["deps"]:
                        od = self.ops[d]
                        q = od["queue"]
                        v = od["inc_no"] * (16 if q[1] == "dma" else 1)
                        if wv.get(q, 0) < v:
                            wv[q] = v
                    for q, v in wv.items():
                        if seen.get(q, -1) >= v:
                            continue
                        eng.wait_ge(sems[q], v)
                        seen[q] = v
                    ins = o["fn"](eng)
                    if o["need_inc"]:
                        amt = 16 if o["kind"] == "dma" else 1
                        ins.then_inc(sems[o["queue"]], amt)
            return body

        with nc.Block() as blk:
            for name in ("tensor", "vector", "scalar", "gpsimd", "sync"):
                if name in by_engine:
                    getattr(blk, name)(emit_for(name))
        return sems

    def barrier_read(self, engine, aps, tag="final"):
        def fn(eng):
            return eng.nop()
        return self.add(engine, fn, reads=list(aps), writes=(), kind="c", tag=tag)



from contextlib import ExitStack


def _newprog():
    nc = bass.Bass("TRN2", target_bir_lowering=False)
    S = Sched(nc)
    return nc, S


def _mk_consts(nc, pfx=""):
    return dict(
        ident=nc.alloc_sbuf_tensor(pfx + "ident", [P, P], BF16),
        ones_bf=nc.alloc_sbuf_tensor(pfx + "ones_bf", [P, 1], BF16),
        ones_row_f=nc.alloc_sbuf_tensor(pfx + "ones_row_f", [1, P], F32),
        ones_row_bf=nc.alloc_sbuf_tensor(pfx + "ones_row_bf", [1, P], BF16),
        iscr=nc.alloc_sbuf_tensor(pfx + "iscr", [P, P], F32),
        pscr=nc.alloc_sbuf_tensor(pfx + "pscr", [P, 512], F32),
    )


def AL(nc, es, name, shape, dtype):
    if es is None:
        return nc.alloc_sbuf_tensor(name, shape, dtype)
    return es.enter_context(nc.sbuf_tensor(name, shape, dtype))


GROUPS4 = [[0, 1, 2, 3], [4, 5, 6, 7]]
GROUPS8 = [list(range(8))]

def f32c(x):
    return np.ascontiguousarray(x, dtype=np.float32)


def chunk_col(vec):
    return np.ascontiguousarray(np.asarray(vec, np.float32).reshape(NCH, P).T)


def emit_consts(S, C):
    S.add("gpsimd", lambda g: g.iota(C["iscr"][:, :], [[1, P]], base=0,
                                     channel_multiplier=-1,
                                     allow_small_or_imprecise_dtypes=True),
          writes=[C["iscr"][:, :]], tag="iota")
    S.add("gpsimd", lambda g: g.memset(C["ones_bf"][:, :], 1.0), writes=[C["ones_bf"][:, :]])
    S.add("gpsimd", lambda g: g.memset(C["ones_row_f"][:, :], 1.0), writes=[C["ones_row_f"][:, :]])
    S.add("gpsimd", lambda g: g.memset(C["ones_row_bf"][:, :], 1.0), writes=[C["ones_row_bf"][:, :]])
    S.add("vector", lambda v: v.tensor_single_scalar(C["ident"][:, :], C["iscr"][:, :],
                                                     0.0, ALU.is_equal),
          reads=[C["iscr"][:, :]], writes=[C["ident"][:, :]], tag="ident")


def emit_row_bcast(S, C, src_row, dst_tile, ps, n, f32=True):
    ones = C["ones_row_f"]
    for p0 in range(0, n, 512):
        w = min(512, n - p0)
        S.add("tensor",
              lambda t, p0=p0, w=w: t.matmul(ps[:, p0:p0 + w], ones[:, :],
                                             src_row[:, p0:p0 + w], start=True, stop=True),
              reads=[ones[:, :], src_row[:, p0:p0 + w]], writes=[ps[:, p0:p0 + w]],
              tag="bcast_mm")
    S.add("vector", lambda v: v.tensor_copy(dst_tile[:, 0:n], ps[:, 0:n]),
          reads=[ps[:, 0:n]], writes=[dst_tile[:, 0:n]], tag="bcast_cp")


def emit_ln(S, C, src, dst, scr, gain_col=None, combined_rms=False, tag="ln"):
    nc = S.nc
    st_g = nc.psum_tensor(f"{tag}_stps", [1, 2 * TOK], F32)
    st_ps = st_g.__enter__()
    sums = st_ps[:, 0:TOK]; sumsq = st_ps[:, TOK:2 * TOK]
    for c in range(NCH):
        S.add("scalar", lambda s, c=c: s.square(scr["sq"][:, c % 2, :], src[:, c, :]),
              reads=[src[:, c, :]], writes=[scr["sq"][:, c % 2, :]], tag=f"{tag}_sq{c}")
        S.add("vector", lambda v, c=c: v.tensor_copy(scr["xb"][:, c % 2, :], src[:, c, :]),
              reads=[src[:, c, :]], writes=[scr["xb"][:, c % 2, :]], tag=f"{tag}_xb{c}")
        def stat_mm(t, c=c):
            for piece in range(TOK // 512):
                sl = slice(piece * 512, piece * 512 + 512)
                t.matmul(sums[:, sl], C["ones_bf"][:, :], scr["xb"][:, c % 2, sl],
                         start=(c == 0), stop=(c == NCH - 1))
                ins = t.matmul(sumsq[:, sl], C["ones_bf"][:, :],
                               scr["sq"][:, c % 2, sl],
                               start=(c == 0), stop=(c == NCH - 1))
            return ins
        S.add("tensor", stat_mm,
              reads=[scr["xb"][:, c % 2, :], scr["sq"][:, c % 2, :],
                     C["ones_bf"][:, :]],
              writes=[sums[:, :], sumsq[:, :]], tag=f"{tag}_mm{c}")
    mean = scr["mrow"][:, :]
    work = scr["wrow"][:, :]

    S.add("vector", lambda v: v.tensor_scalar_mul(mean, sums[:, :], 1.0 / D),
          reads=[sums[:, :]], writes=[mean], tag=f"{tag}_mean")
    S.add("vector", lambda v: v.tensor_scalar_mul(work, sumsq[:, :], 1.0 / D),
          reads=[sumsq[:, :]], writes=[work], tag=f"{tag}_ms")
    st_g.__exit__(None, None, None)
    S.add("vector", lambda v: v.tensor_mul(scr["t1"][0:1, 0:TOK], mean, mean),
          reads=[mean], writes=[scr["t1"][0:1, 0:TOK]], tag=f"{tag}_m2")
    S.add("vector", lambda v: v.tensor_sub(work, work, scr["t1"][0:1, 0:TOK]),
          reads=[work, scr["t1"][0:1, 0:TOK]], writes=[work], tag=f"{tag}_var")
    if combined_rms:
        k1, k2 = 1.0 + EPS_RMS, EPS_RMS * EPS_LN
    else:
        k1, k2 = 1.0, EPS_LN
    S.add("vector", lambda v: v.tensor_scalar(work, work, k1, k2, ALU.mult, ALU.add),
          reads=[work], writes=[work], tag=f"{tag}_vare")
    S.add("scalar", lambda s: s.activation(work, work, AF.Sqrt),
          reads=[work], writes=[work], tag=f"{tag}_sd")
    S.add("vector", lambda v: v.reciprocal(work, work),
          reads=[work], writes=[work], tag=f"{tag}_al")
    bc_g = nc.psum_tensor(f"{tag}_bcps", [P, TOK], F32)
    bc_ps = bc_g.__enter__()
    for rowsrc, dstt in ((mean, scr["mb"]), (work, scr["ab"])):
        def bmm(t, rowsrc=rowsrc):
            for p0 in range(0, TOK, 512):
                ins = t.matmul(bc_ps[:, p0:p0 + 512], C["ones_row_f"][:, :],
                               rowsrc[:, p0:p0 + 512], start=True, stop=True)
            return ins
        S.add("tensor", bmm, reads=[C["ones_row_f"][:, :], rowsrc],
              writes=[bc_ps[:, :]], tag=f"{tag}_bmm")
        S.add("vector", lambda v, dstt=dstt: v.tensor_copy(dstt[:, :], bc_ps[:, :]),
              reads=[bc_ps[:, :]], writes=[dstt[:, :]], tag=f"{tag}_bcp")
    bc_g.__exit__(None, None, None)
    for c in range(NCH):
        t1 = scr["t1"]
        S.add("vector", lambda v, c=c: v.tensor_sub(t1[:, :], src[:, c, :], scr["mb"][:, :]),
              reads=[src[:, c, :], scr["mb"][:, :]], writes=[t1[:, :]], tag=f"{tag}_n1")
        if gain_col is None:
            S.add("vector", lambda v, c=c: v.tensor_mul(dst[:, c, :], t1[:, :], scr["ab"][:, :]),
                  reads=[t1[:, :], scr["ab"][:, :]], writes=[dst[:, c, :]], tag=f"{tag}_n2")
        else:
            S.add("vector", lambda v, c=c: v.tensor_mul(t1[:, :], t1[:, :], scr["ab"][:, :]),
                  reads=[t1[:, :], scr["ab"][:, :]], writes=[t1[:, :]], tag=f"{tag}_n2")
            S.add("vector",
                  lambda v, c=c: v.tensor_scalar_mul(dst[:, c, :], t1[:, :],
                                                     gain_col[:, c:c + 1]),
                  reads=[t1[:, :], gain_col[:, c:c + 1]], writes=[dst[:, c, :]],
                  tag=f"{tag}_n3")


def emit_proj_fm(S, C, wch, src, dst, ps, act=None, bias_col=None, scale=1.0,
                 accum_into=None, nchunks_in=NCH, tag="proj", evac_engine="scalar",
                 out_f32_add=None, x_res=None):
    width = src(0).shape[-1]
    npc = (width + 511) // 512
    for o in range(NCH):
        for piece in range(npc):
            w = min(512, width - piece * 512)
            sl = slice(piece * 512, piece * 512 + w)
            pb = (o * npc + piece) % 2
            psl = ps[:, pb * 512:pb * 512 + w]
            rd = [C["ones_bf"][:, :]]
            for i in range(nchunks_in):
                S.add("tensor",
                      lambda t, i=i, o=o, sl=sl, psl=psl: t.matmul(
                          psl, wch(i, o), src(i)[:, sl],
                          start=(i == 0), stop=(i == nchunks_in - 1)),
                      reads=[wch(i, o), src(i)[:, sl]], writes=[psl],
                      tag=f"{tag}_mm{o}_{piece}")
            if out_f32_add is not None:
                attn_ap, xres_ap = out_f32_add(o)
                t1 = C["pscr"]
                S.add("vector",
                      lambda v, psl=psl, a=attn_ap[:, sl], sl=sl: v.tensor_add(
                          t1[:, 0:a.shape[-1]], psl, a),
                      reads=[psl, attn_ap[:, sl]], writes=[t1[:, 0:w]], tag=f"{tag}_e1")
                S.add("vector",
                      lambda v, x=xres_ap[:, sl], sl=sl: v.tensor_add(
                          x, x, t1[:, 0:w]),
                      reads=[t1[:, 0:w], xres_ap[:, sl]], writes=[xres_ap[:, sl]],
                      tag=f"{tag}_e2")
            else:
                dsl = dst(o)[:, sl]
                if act is not None or bias_col is not None or scale != 1.0:
                    b = bias_col(o) if bias_col is not None else 0.0
                    fn = act if act is not None else AF.Identity
                    S.add(evac_engine,
                          lambda s, psl=psl, dsl=dsl, b=b, fn=fn: s.activation(
                              dsl, psl, fn,
                              bias=b if not isinstance(b, float) else b,
                              scale=scale),
                          reads=[psl] + ([b] if not isinstance(b, float) else []),
                          writes=[dsl], tag=f"{tag}_ev")
                else:
                    S.add("vector", lambda v, psl=psl, dsl=dsl: v.tensor_copy(dsl, psl),
                          reads=[psl], writes=[dsl], tag=f"{tag}_ev")


def emit_l0_attn_tiles(S, C, B):
    nc = S.nc
    xn, wqkv, attn_fm, x_res = B["xn"], B["wqkv"], B["attn_fm"], B["x_res"]
    qkps, tps = B["qkps"], B["tps"]
    pqk, vt = B["pqk"], B["vt"]
    eluA, eluB = B["eluA"], B["eluB"]
    gs, gg = B["gs"], B["gg"]
    acc, tmp1k = B["acc"], B["tmp1k"]
    sqt, r16 = B["sqt"], B["r16"]
    a2, a2g, t4 = B["a2"], B["a2g"], B["t4"]
    attn_tm = B["attn_tm"]

    def emit_transpose(st):
        atm = attn_tm[:, st % 2, :]

        def trg(t, atm=atm):
            for c in range(NCH):
                ins = t.transpose(tps[:, c * P:(c + 1) * P],
                                  atm[:, c * P:(c + 1) * P], C["ident"][:, :])
            return ins
        S.add("tensor", trg, reads=[atm[:, :], C["ident"][:, :]],
              writes=[tps[:, :]], tag=f"l0a_tp{st}")
        S.add("vector", lambda v, st=st: v.tensor_copy(
                  attn_fm[:, :, st * P:st * P + P],
                  tps[:, :].rearrange("p (c t) -> p c t", t=P)),
              reads=[tps[:, :]], writes=[attn_fm[:, :, st * P:st * P + P]],
              tag=f"l0a_tcp{st}")

    for st in range(NT):
        ssl = slice(st * P, st * P + P)
        for piece in range(6):
            psl = qkps[:, piece * 512:(piece + 1) * 512]

            def qkvmm(t, piece=piece, psl=psl, ssl=ssl):
                for f in range(NCH):
                    ins = t.matmul(psl, xn[:, f, ssl],
                                   wqkv[:, f, piece * 512:(piece + 1) * 512],
                                   start=(f == 0), stop=(f == NCH - 1))
                return ins
            S.add("tensor", qkvmm,
                  reads=[xn[:, :, ssl], wqkv[:, :, piece * 512:(piece + 1) * 512]],
                  writes=[psl], tag=f"l0a_qkv{st}_{piece}")
        S.add("vector", lambda v: v.tensor_scalar_min(eluA[:, :], qkps[:, 0:2048], 0.0),
              reads=[qkps[:, 0:2048]], writes=[eluA[:, :]], tag=f"l0a_min{st}")
        S.add("scalar", lambda s: s.activation(eluB[:, :], eluA[:, :], AF.Exp),
              reads=[eluA[:, :]], writes=[eluB[:, :]], tag=f"l0a_exp{st}")
        S.add("vector", lambda v: v.scalar_tensor_tensor(pqk[:, :], eluB[:, :], -1.0,
                                                         qkps[:, 0:2048], ALU.add, ALU.max),
              reads=[eluB[:, :], qkps[:, 0:2048]], writes=[pqk[:, :]], tag=f"l0a_elu{st}")
        S.add("vector", lambda v: v.tensor_copy(vt[:, :], qkps[:, 2048:3072]),
              reads=[qkps[:, 2048:3072]], writes=[vt[:, :]], tag=f"l0a_v{st}")
        if st == 0 and "dbg" in B:
            S.dma("sync", B["dbg"]["pqk"].ap(), pqk[:, :], tag="dbg_pqk")
            S.dma("sync", B["dbg"]["vt"].ap(), vt[:, :], tag="dbg_vt")
        pq = pqk[:, 0:1024].rearrange("p (h e) -> p h e", e=64)
        pk = pqk[:, 1024:2048]
        for half, gslice in ((0, gg[:, 0:256]), (1, gg[:, 256:512])):
            for hp in range(16):
                k_ = pk[:, hp * 64 + 32 * half: hp * 64 + 32 * half + 32]
                kb = k_.unsqueeze(1).broadcast_to([P, 16, 32])
                qv = pq[:, :, 32 * half:32 * half + 32]
                gsl = gs[:, (hp % 2) * 512:(hp % 2) * 512 + 512]
                prod = gsl.rearrange("p (h e) -> p h e", e=32)
                S.add("gpsimd", lambda e, prod=prod, qv=qv, kb=kb: e.tensor_tensor(
                          prod, qv, kb, ALU.mult),
                      reads=[pqk[:, :]], writes=[gsl],
                      tag=f"l0a_g{half}m{st}_{hp}")
                out = gslice.rearrange("p (h x) -> p h x", x=16)[:, :, hp]
                S.add("vector", lambda e, out=out, prod=prod: e.tensor_reduce(
                          out, prod, AX.X, ALU.add),
                      reads=[gsl], writes=[out],
                      tag=f"l0a_g{half}r{st}_{hp}")
        S.add("vector", lambda v: v.scalar_tensor_tensor(gg[:, 0:256], gg[:, 256:512],
                                                         B["neglmb"][:, :], gg[:, 0:256],
                                                         ALU.mult, ALU.add),
              reads=[gg[:, :], B["neglmb"][:, :]], writes=[gg[:, 0:256]], tag=f"l0a_h{st}")
        S.add("vector", lambda v: v.tensor_scalar_mul(gg[:, 0:256], gg[:, 0:256], L0_SCALE),
              reads=[gg[:, 0:256]], writes=[gg[:, 0:256]], tag=f"l0a_hs{st}")
        if st == 0 and "dbg" in B:
            S.dma("sync", B["dbg"]["gg"].ap(), gg[:, :], tag="dbg_gg")
        accv = acc[:, :].rearrange("p (h d) -> p h d", d=64)
        for hp in range(16):
            hcol = gg[:, 0:256].rearrange("p (h x) -> p h x", x=16)[:, :, hp]
            hb = hcol.unsqueeze(2).broadcast_to([P, 16, 64])
            vb = vt[:, hp * 64:(hp + 1) * 64].unsqueeze(1).broadcast_to([P, 16, 64])
            if hp == 0:
                S.add("vector", lambda v, hb=hb, vb=vb, accv=accv: v.tensor_tensor(
                          accv, hb, vb, ALU.mult),
                      reads=[gg[:, 0:256], vt[:, :]], writes=[acc[:, :]],
                      tag=f"l0a_acc{st}_0")
            else:
                S.add("gpsimd" if hp % 2 else "vector",
                      lambda e, hb=hb, vb=vb: e.tensor_tensor(
                          tmp1k[:, :].rearrange("p (h d) -> p h d", d=64), hb, vb, ALU.mult),
                      reads=[gg[:, 0:256], vt[:, :]], writes=[tmp1k[:, :]],
                      tag=f"l0a_accm{st}_{hp}")
                S.add("vector", lambda v: v.tensor_add(acc[:, :], acc[:, :], tmp1k[:, :]),
                      reads=[acc[:, :], tmp1k[:, :]], writes=[acc[:, :]],
                      tag=f"l0a_acca{st}_{hp}")
        if st == 0 and "dbg" in B:
            S.dma("sync", B["dbg"]["acc"].ap(), acc[:, :], tag="dbg_acc")
        S.add("scalar", lambda s: s.square(sqt[:, :], acc[:, :]),
              reads=[acc[:, :]], writes=[sqt[:, :]], tag=f"l0a_sq{st}")
        S.add("vector", lambda v: v.tensor_reduce(r16[:, 0:16],
                                                  sqt[:, :].rearrange("p (h d) -> p h d", d=64),
                                                  AX.X, ALU.add),
              reads=[sqt[:, :]], writes=[r16[:, 0:16]], tag=f"l0a_ms{st}")
        fac = (1.0 - LAMBDA_INIT) ** 2
        S.add("vector", lambda v: v.tensor_scalar(r16[:, 16:32], r16[:, 0:16],
                                                  1.0 / (64.0 * fac), EPS_RMS / fac,
                                                  ALU.mult, ALU.add),
              reads=[r16[:, 0:16]], writes=[r16[:, 16:32]], tag=f"l0a_msf{st}")
        S.add("scalar", lambda s: s.sqrt(r16[:, 32:48], r16[:, 16:32]),
              reads=[r16[:, 16:32]], writes=[r16[:, 32:48]], tag=f"l0a_msq{st}")
        S.add("vector", lambda v: v.reciprocal(r16[:, 48:64], r16[:, 32:48]),
              reads=[r16[:, 32:48]], writes=[r16[:, 48:64]], tag=f"l0a_mr{st}")
        r2b = r16[:, 48:64].unsqueeze(2).broadcast_to([P, 16, 64])
        S.add("vector", lambda v: v.tensor_tensor(a2[:, :].rearrange("p (h d) -> p h d", d=64),
                                                  accv, r2b, ALU.mult),
              reads=[acc[:, :], r16[:, 48:64]], writes=[a2[:, :]], tag=f"l0a_a2{st}")
        S.add("vector", lambda v: v.tensor_mul(a2g[:, :], a2[:, :], B["g2t"][:, :]),
              reads=[a2[:, :], B["g2t"][:, :]], writes=[a2g[:, :]], tag=f"l0a_a2g{st}")
        S.add("scalar", lambda s: s.square(sqt[:, :], a2g[:, :]),
              reads=[a2g[:, :]], writes=[sqt[:, :]], tag=f"l0a_sq3{st}")
        S.add("vector", lambda v: v.tensor_reduce(r16[:, 64:65], sqt[:, :], AX.X, ALU.add),
              reads=[sqt[:, :]], writes=[r16[:, 64:65]], tag=f"l0a_m3{st}")
        S.add("vector", lambda v: v.tensor_scalar(r16[:, 65:66], r16[:, 64:65],
                                                  1.0 / D, EPS_RMS, ALU.mult, ALU.add),
              reads=[r16[:, 64:65]], writes=[r16[:, 65:66]], tag=f"l0a_m3f{st}")
        S.add("scalar", lambda s: s.sqrt(r16[:, 66:67], r16[:, 65:66]),
              reads=[r16[:, 65:66]], writes=[r16[:, 66:67]], tag=f"l0a_m3q{st}")
        S.add("vector", lambda v: v.reciprocal(r16[:, 67:68], r16[:, 66:67]),
              reads=[r16[:, 66:67]], writes=[r16[:, 67:68]], tag=f"l0a_m3r{st}")
        S.add("vector", lambda v: v.tensor_scalar_mul(t4[:, :], a2g[:, :], r16[:, 67:68]),
              reads=[a2g[:, :], r16[:, 67:68]], writes=[t4[:, :]], tag=f"l0a_t3{st}")
        S.add("vector", lambda v: v.tensor_mul(t4[:, :], t4[:, :], B["g3t"][:, :]),
              reads=[t4[:, :], B["g3t"][:, :]], writes=[t4[:, :]], tag=f"l0a_t4{st}")
        atm = attn_tm[:, st % 2, :]
        S.add("vector", lambda v, atm=atm: v.tensor_add(atm, t4[:, :], a2g[:, :]),
              reads=[t4[:, :], a2g[:, :]], writes=[atm], tag=f"l0a_atm{st}")
        if st == 0 and "dbg" in B:
            S.dma("sync", B["dbg"]["atm"].ap(), atm[:, :], tag="dbg_atm")
            S.dma("sync", B["dbg"]["r16"].ap(), r16[:, :], tag="dbg_r16")
            S.dma("sync", B["dbg"]["a2"].ap(), a2[:, :], tag="dbg_a2")
            S.dma("sync", B["dbg"]["a2g"].ap(), a2g[:, :], tag="dbg_a2g")
            S.dma("sync", B["dbg"]["t4"].ap(), t4[:, :], tag="dbg_t4")
        if st > 0:
            emit_transpose(st - 1)
        if st == NT - 1:
            emit_transpose(st)


def emit_l0_attn_mlp(S, C, B):
    attn_fm = B["attn_fm"]; x_res = B["x_res"]
    w12 = B["w12"]
    hbuf = B["hbuf"]
    ps2 = B["ps2"]
    for tp in range(4):
        tsl = slice(tp * 512, tp * 512 + 512)
        for o in range(NCH):
            pb = ps2[:, (o % 2) * 512:(o % 2) * 512 + 512]

            def w1mm(t, o=o, pb=pb, tsl=tsl):
                for i in range(NCH):
                    ins = t.matmul(pb, w12[:, i, o * P:(o + 1) * P],
                                   attn_fm[:, i, tsl],
                                   start=(i == 0), stop=(i == NCH - 1))
                return ins
            S.add("tensor", w1mm,
                  reads=[w12[:, :, 0:1024], attn_fm[:, :, tsl]],
                  writes=[pb], tag=f"l0a_w1_{tp}_{o}")
            S.add("scalar", lambda s, o=o, pb=pb: s.activation(hbuf[:, o, :], pb, AF.Gelu),
                  reads=[pb], writes=[hbuf[:, o, :]], tag=f"l0a_w1e_{tp}_{o}")
        for o in range(NCH):
            pb = ps2[:, (o % 2) * 512:(o % 2) * 512 + 512]

            def w2mm(t, o=o, pb=pb):
                for i in range(NCH):
                    ins = t.matmul(pb, w12[:, i, 1024 + o * P:1024 + (o + 1) * P],
                                   hbuf[:, i, :],
                                   start=(i == 0), stop=(i == NCH - 1))
                return ins
            S.add("tensor", w2mm,
                  reads=[w12[:, :, 1024:2048], hbuf[:, :, :]],
                  writes=[pb], tag=f"l0a_w2_{tp}_{o}")
            t1 = C["pscr"]
            S.add("vector", lambda v, o=o, pb=pb, tsl=tsl: v.tensor_add(
                      t1[:, :], pb, attn_fm[:, o, tsl]),
                  reads=[pb, attn_fm[:, o, tsl]], writes=[t1[:, :]], tag=f"l0a_w2a_{tp}_{o}")
            S.add("vector", lambda v, o=o, tsl=tsl: v.tensor_add(
                      x_res[:, o, tsl], x_res[:, o, tsl], t1[:, :]),
                  reads=[t1[:, :], x_res[:, o, tsl]], writes=[x_res[:, o, tsl]],
                  tag=f"l0a_w2r_{tp}_{o}")


def build_A(FC=None):
    if FC is None:
        nc, S = _newprog()
        es = None
        x_d = nc.dram_tensor("x_fm", [D, TOK], F32, kind="ExternalInput")
        out_d = nc.dram_tensor("x1_fm", [D, TOK], F32, kind="ExternalOutput")
        C = _mk_consts(nc)
        x_res = nc.alloc_sbuf_tensor("x_res", [P, NCH, TOK], F32)
        emit_consts(S, C)
        for c in range(NCH):
            S.dma("sync", x_res[:, c, :], x_d.ap()[c * P:(c + 1) * P, :], tag="ld_x")
    else:
        nc, S, C, x_res, es = FC["nc"], FC["S"], FC["C"], FC["x_res"], FC["es"]
    wqkv_d = nc.dram_tensor("wqkvT", [D, 3072], F32, kind="ExternalInput")
    w1_d = nc.dram_tensor("w1T", [D, D], F32, kind="ExternalInput")
    w2_d = nc.dram_tensor("w2T", [D, D], F32, kind="ExternalInput")
    g1_d = nc.dram_tensor("g1c", [P, NCH], F32, kind="ExternalInput")
    g2_d = nc.dram_tensor("g2row", [1, D], F32, kind="ExternalInput")
    g3_d = nc.dram_tensor("g3row", [1, D], F32, kind="ExternalInput")
    lam_d = nc.dram_tensor("lam", [1, 4], F32, kind="ExternalInput")
    dbg = None
    if DEBUG_A:
        dbg = dict(
            pqk=nc.dram_tensor("d_pqk", [P, 2048], BF16, kind="ExternalOutput"),
            vt=nc.dram_tensor("d_vt", [P, 1024], BF16, kind="ExternalOutput"),
            gg=nc.dram_tensor("d_gg", [P, 512], F32, kind="ExternalOutput"),
            acc=nc.dram_tensor("d_acc", [P, 1024], F32, kind="ExternalOutput"),
            atm=nc.dram_tensor("d_atm", [P, 1024], BF16, kind="ExternalOutput"),
            r16=nc.dram_tensor("d_r16", [P, 68], F32, kind="ExternalOutput"),
            a2=nc.dram_tensor("d_a2", [P, 1024], BF16, kind="ExternalOutput"),
            a2g=nc.dram_tensor("d_a2g", [P, 1024], BF16, kind="ExternalOutput"),
            t4=nc.dram_tensor("d_t4", [P, 1024], BF16, kind="ExternalOutput"),
            xn=nc.dram_tensor("d_xn", [D, TOK], BF16, kind="ExternalOutput"),
        )

    xn = FC["outb"] if FC is not None else AL(nc, es, "A_xn", [P, NCH, TOK], BF16)
    wqkv = AL(nc, es, "A_wqkv", [P, NCH, 3072], BF16)
    SCR = AL(nc, es, "A_SCR", [P, 14336], BF16)
    SCF = SCR.bitcast(F32)
    SCRF = AL(nc, es, "A_SCRF", [P, TOK], F32)
    mrow = SCF[0:1, 0:TOK]
    wrow = SCF[0:1, TOK:2 * TOK]
    r16 = AL(nc, es, "A_r16", [P, 68], F32)
    g1c = AL(nc, es, "A_g1c", [P, NCH], F32)
    g2t = AL(nc, es, "A_g2t", [P, D], BF16)
    g3t = AL(nc, es, "A_g3t", [P, D], BF16)
    grow = SCR[0:1, 12288:14336]
    lam = AL(nc, es, "A_lam", [1, 4], F32)
    lsc = AL(nc, es, "A_lsc", [1, 8], F32)
    neglmb = AL(nc, es, "A_neglmb", [P, 1], F32)

    lnscr = dict(
        xb=SCR[:, 0:4096].rearrange("p (b t) -> p b t", t=TOK),
        sq=SCR[:, 4096:8192].rearrange("p (b t) -> p b t", t=TOK),
        mrow=mrow, wrow=wrow,
        mb=SCR[:, 8192:10240], ab=SCR[:, 10240:12288],
        t1=SCRF,
    )
    Bt = dict(
        x_res=x_res, xn=xn, attn_fm=xn, wqkv=wqkv, w12=wqkv,
        pqk=SCR[:, 0:2048],
        eluA=SCR[:, 2048:4096],
        eluB=SCR[:, 4096:6144],
        vt=SCR[:, 6144:7168],
        gs=SCR[:, 4096:5120],
        tmp1k=SCR[:, 5120:6144],
        gg=SCF[:, 3584:4096],
        acc=SCF[:, 1024:2048],
        sqt=SCR[:, 4096:5120],
        a2=SCR[:, 6144:7168],
        a2g=SCR[:, 0:1024],
        t4=SCR[:, 1024:2048],
        attn_tm=SCR[:, 8192:10240].rearrange("p (b t) -> p b t", t=1024),
        hbuf=SCR[:, 10240:14336].rearrange("p (c t) -> p c t", t=512),
        g2t=g2t, g3t=g3t, neglmb=neglmb, r16=r16,
    )

    for c in range(NCH):
        S.dma("gpsimd", wqkv[:, c, :], wqkv_d.ap()[c * P:(c + 1) * P, :], tag="ld_w")
    S.dma("sync", g1c[:, :], g1_d.ap(), tag="ld_g1")
    S.dma("gpsimd", grow[:, 0:D], g2_d.ap(), tag="ld_g2")
    S.dma("gpsimd", grow[:, D:2 * D], g3_d.ap(), tag="ld_g3")
    S.dma("sync", lam[:, :], lam_d.ap(), tag="ld_lam")

    S.add("vector", lambda v: v.tensor_mul(lsc[:, 0:1], lam[:, 0:1], lam[:, 1:2]),
          reads=[lam[:, :]], writes=[lsc[:, 0:1]])
    S.add("vector", lambda v: v.tensor_mul(lsc[:, 1:2], lam[:, 2:3], lam[:, 3:4]),
          reads=[lam[:, :]], writes=[lsc[:, 1:2]])
    S.add("scalar", lambda s: s.activation(lsc[:, 2:4], lsc[:, 0:2], AF.Exp),
          reads=[lsc[:, 0:2]], writes=[lsc[:, 2:4]])
    S.add("vector", lambda v: v.tensor_sub(lsc[:, 4:5], lsc[:, 3:4], lsc[:, 2:3]),
          reads=[lsc[:, 2:4]], writes=[lsc[:, 4:5]])
    S.add("vector", lambda v: v.tensor_scalar_add(lsc[:, 5:6], lsc[:, 4:5], -LAMBDA_INIT),
          reads=[lsc[:, 4:5]], writes=[lsc[:, 5:6]])
    with nc.psum_tensor("bc1", [P, 1], F32) as bc1:
        S.add("tensor", lambda t: t.matmul(bc1[:, :], C["ones_row_f"][:, :], lsc[:, 5:6],
                                           start=True, stop=True),
              reads=[C["ones_row_f"][:, :], lsc[:, 5:6]], writes=[bc1[:, :]])
        S.add("vector", lambda v: v.tensor_copy(neglmb[:, :], bc1[:, :]),
              reads=[bc1[:, :]], writes=[neglmb[:, :]])
    with nc.psum_tensor("bc2", [P, 1024], F32) as bc2:
        for src_, dstt in ((grow[:, 0:D], g2t), (grow[:, D:2 * D], g3t)):
            for p0 in range(0, D, 512):
                S.add("tensor", lambda t, p0=p0, src_=src_: t.matmul(
                          bc2[:, p0:p0 + 512], C["ones_row_bf"][:, :], src_[:, p0:p0 + 512],
                          start=True, stop=True),
                      reads=[C["ones_row_bf"][:, :], src_], writes=[bc2[:, p0:p0 + 512]])
            S.add("vector", lambda v, dstt=dstt: v.tensor_copy(dstt[:, :], bc2[:, :]),
                  reads=[bc2[:, :]], writes=[dstt[:, :]])

    emit_ln(S, C, x_res, xn, lnscr, gain_col=g1c, combined_rms=True, tag="ln1")
    if DEBUG_A:
        Bt["dbg"] = dbg
        for c in range(NCH):
            S.dma("sync", dbg["xn"].ap()[c * P:(c + 1) * P, :], xn[:, c, :], tag="dbg_xn")

    with (nc.psum_tensor("qkps", [P, 3072], F32) as qkps,
          nc.psum_tensor("tps", [P, 1024], BF16) as tps):
        Bt["qkps"] = qkps
        Bt["tps"] = tps
        emit_l0_attn_tiles(S, C, Bt)
    for c in range(NCH):
        S.dma("gpsimd", wqkv[:, c, 0:1024], w1_d.ap()[c * P:(c + 1) * P, :], tag="ld_w1")
        S.dma("gpsimd", wqkv[:, c, 1024:2048], w2_d.ap()[c * P:(c + 1) * P, :], tag="ld_w2")

    with nc.psum_tensor("ps2", [P, 1024], F32) as ps2:
        Bt["ps2"] = ps2
        emit_l0_attn_mlp(S, C, Bt)

    if FC is None:
        for c in range(NCH):
            S.dma("sync", out_d.ap()[c * P:(c + 1) * P, :], x_res[:, c, :], tag="st_x")
        aps = [out_d.ap()] + ([v.ap() for v in dbg.values()] if DEBUG_A else [])
        S.barrier_read("sync", aps)
        S.build()
    return nc


def pack_A(inputs, core):
    x = np.asarray(inputs["L0_attn_x"], np.float32).reshape(NCORE * TOK, D)
    sl = x[core * TOK:(core + 1) * TOK]
    return {
        "x_fm": np.ascontiguousarray(sl.T),
        "wqkvT": np.ascontiguousarray(np.concatenate(
            [np.asarray(inputs["L0_attn_Wq"]).T, np.asarray(inputs["L0_attn_Wk"]).T,
             np.asarray(inputs["L0_attn_Wv"]).T], axis=1).astype(np.float32)),
        "w1T": np.ascontiguousarray(np.asarray(inputs["L0_attn_W1"], np.float32).T),
        "w2T": np.ascontiguousarray(np.asarray(inputs["L0_attn_W2"], np.float32).T),
        "g1c": chunk_col(inputs["L0_attn_g1"]),
        "g2row": np.ascontiguousarray(np.tile(np.asarray(inputs["L0_attn_g2"], np.float32),
                                              16)[None, :]),
        "g3row": f32c(np.asarray(inputs["L0_attn_g3"]))[None, :],
        "lam": f32c(np.asarray(inputs["L0_attn_lambda_params"]))[None, :],
    }


def run_A(inputs):
    nc = build_A()
    maps = [pack_A(inputs, c) for c in range(NCORE)]
    res = run_bass_kernel_spmd(nc, maps, core_ids=list(range(NCORE)))
    out = np.empty((NCORE * TOK, D), np.float32)
    for c in range(NCORE):
        out[c * TOK:(c + 1) * TOK] = res.results[c]["x1_fm"].T
    return out.reshape(2, NCORE * TOK // 2, D)



def emit_tok_norms(S, C, src, width, stat_row, sqbuf, per_chunk, tag):
    nc = S.nc
    HW = 512
    nh = (width + HW - 1) // HW
    ps_g = nc.psum_tensor(f"{tag}_nps", [1, HW], F32)
    ps = ps_g.__enter__()
    tmpc = stat_row[:, 8:9]
    first_red = [True]

    def red(tgt, w):
        if first_red[0]:
            S.add("vector", lambda v, tgt=tgt, w=w: v.tensor_reduce(
                      tgt, ps[:, 0:w], AX.X, ALU.max),
                  reads=[ps[:, 0:w]], writes=[tgt], tag=f"{tag}_mx")
        else:
            S.add("vector", lambda v, w=w: v.tensor_reduce(tmpc, ps[:, 0:w],
                                                           AX.X, ALU.max),
                  reads=[ps[:, 0:w]], writes=[tmpc], tag=f"{tag}_mxt")
            S.add("vector", lambda v, tgt=tgt: v.tensor_max(tgt, tgt, tmpc),
                  reads=[tgt, tmpc], writes=[tgt], tag=f"{tag}_mxc")
        first_red[0] = False

    if per_chunk:
        for c in range(NCH):
            first_red[0] = True
            tgt = stat_row[:, c:c + 1]
            for h in range(nh):
                w = min(HW, width - h * HW)
                hsl = slice(h * HW, h * HW + w)
                S.add("scalar", lambda s, c=c, hsl=hsl, w=w: s.square(
                          sqbuf[:, 0:w], src[:, c, hsl]),
                      reads=[src[:, c, hsl]], writes=[sqbuf[:, 0:w]], tag=f"{tag}_sq")
                S.add("tensor", lambda t, w=w: t.matmul(
                          ps[:, 0:w], C["ones_bf"][:, :], sqbuf[:, 0:w],
                          start=True, stop=True),
                      reads=[C["ones_bf"][:, :], sqbuf[:, 0:w]], writes=[ps[:, 0:w]],
                      tag=f"{tag}_mm")
                red(tgt, w)
    else:
        tgt = stat_row[:, 0:1]
        for h in range(nh):
            w = min(HW, width - h * HW)
            hsl = slice(h * HW, h * HW + w)
            for c in range(NCH):
                S.add("scalar", lambda s, c=c, hsl=hsl, w=w: s.square(
                          sqbuf[:, 0:w], src[:, c, hsl]),
                      reads=[src[:, c, hsl]], writes=[sqbuf[:, 0:w]], tag=f"{tag}_sq")
                S.add("tensor", lambda t, w=w, c=c: t.matmul(
                          ps[:, 0:w], C["ones_bf"][:, :], sqbuf[:, 0:w],
                          start=(c == 0), stop=(c == NCH - 1)),
                      reads=[C["ones_bf"][:, :], sqbuf[:, 0:w]], writes=[ps[:, 0:w]],
                      tag=f"{tag}_mm")
            red(tgt, w)
    ps_g.__exit__(None, None, None)


def emit_shift(S, C, qn, kn, n, negc_col, rowscr, tag):
    nc = S.nc
    a = rowscr[:, 0:n]; b = rowscr[:, n:2 * n]
    S.add("scalar", lambda s: s.activation(a, qn, AF.Sqrt), reads=[qn], writes=[a],
          tag=f"{tag}_s1")
    S.add("scalar", lambda s: s.activation(b, kn, AF.Sqrt), reads=[kn], writes=[b],
          tag=f"{tag}_s2")
    S.add("vector", lambda v: v.tensor_mul(a, a, b), reads=[a, b], writes=[a],
          tag=f"{tag}_m")
    S.add("vector", lambda v: v.tensor_scalar_add(a, a, -60.0), reads=[a], writes=[a],
          tag=f"{tag}_sub")
    S.add("vector", lambda v: v.tensor_scalar(a, a, -1.0, 0.0, ALU.mult, ALU.min),
          reads=[a], writes=[a], tag=f"{tag}_neg")
    ps_g = nc.psum_tensor(f"{tag}_bps", [P, max(n, 1)], F32)
    ps = ps_g.__enter__()
    S.add("tensor", lambda t: t.matmul(ps[:, 0:n], C["ones_row_f"][:, :], a,
                                       start=True, stop=True),
          reads=[C["ones_row_f"][:, :], a], writes=[ps[:, 0:n]], tag=f"{tag}_bc")
    S.add("vector", lambda v: v.tensor_copy(negc_col[:, 0:n], ps[:, 0:n]),
          reads=[ps[:, 0:n]], writes=[negc_col[:, 0:n]], tag=f"{tag}_cp")
    ps_g.__exit__(None, None, None)


def emit_fff(S, C, F, tag, out_scale=1.0, load_w2=None):
    nc = S.nc
    xn, H, out = F["xn"], F["H"], F["out"]
    for o in range(NCH):
        for piece in range(4):
            sl = slice(piece * 512, piece * 512 + 512)
            pb = F["ps"][:, ((o * 4 + piece) % 2) * 512:((o * 4 + piece) % 2) * 512 + 512]

            def hmm(t, o=o, sl=sl, pb=pb, F=F):
                for i in range(NCH):
                    ins = t.matmul(pb, F["w1"][:, i, o * P:(o + 1) * P], xn[:, i, sl],
                                   start=(i == 0), stop=(i == NCH - 1))
                return ins
            S.add("tensor", hmm,
                  reads=[F["w1"][:, :, :], xn[:, :, sl]],
                  writes=[pb], tag=f"{tag}_h{o}_{piece}")
            S.add("scalar", lambda s, o=o, sl=sl, pb=pb: s.activation(
                      H[:, o, sl], pb, AF.Relu, bias=F["b1c"][:, o:o + 1]),
                  reads=[pb, F["b1c"][:, o:o + 1]], writes=[H[:, o, sl]],
                  tag=f"{tag}_he{o}_{piece}")
    for st in range(NT):
        ssl = slice(st * P, st * P + P)
        def nsmm(t, ssl=ssl, F=F):
            for i in range(NCH):
                ins = t.matmul(F["sps"][:, :], xn[:, i, ssl], F["nwt"][:, i, :],
                               start=(i == 0), stop=(i == NCH - 1))
            return ins
        S.add("tensor", nsmm, reads=[xn[:, :, ssl], F["nwt"][:, :, :]],
              writes=[F["sps"][:, :]], tag=f"{tag}_ns{st}")
        S.add("vector", lambda v, st=st: v.tensor_add(
                  F["sco"][:, st * 16:st * 16 + 16], F["sps"][:, :], F["nbb"][:, :]),
              reads=[F["sps"][:, :], F["nbb"][:, :]],
              writes=[F["sco"][:, st * 16:st * 16 + 16]], tag=f"{tag}_nse{st}")
    emit_fff_route(S, F, tag)
    for st in range(NT):
        S.add("tensor", lambda t, st=st: t.transpose(
                  F["tps"][:, :], F["mb16"][:, st * 16:st * 16 + 16], C["ident"][:, :]),
              reads=[F["mb16"][:, st * 16:st * 16 + 16], C["ident"][:, :]],
              writes=[F["tps"][:, :]], tag=f"{tag}_mt{st}")
        S.add("vector", lambda v, st=st: v.tensor_copy(
                  F["m16"][:, st * P:st * P + P], F["tps"][:, :]),
              reads=[F["tps"][:, :]], writes=[F["m16"][:, st * P:st * P + P]],
              tag=f"{tag}_mtc{st}")
    for c in range(NCH):
        for piece in range(4):
            sl = slice(piece * 512, piece * 512 + 512)
            S.add("tensor", lambda t, c=c, sl=sl: t.matmul(
                      F["mps"][:, :], F["bsel"][:, c, :], F["m16"][:, sl],
                      start=True, stop=True),
                  reads=[F["bsel"][:, c, :], F["m16"][:, sl]], writes=[F["mps"][:, :]],
                  tag=f"{tag}_mk{c}_{piece}")
            S.add("vector", lambda v, c=c, sl=sl: v.tensor_mul(
                      H[:, c, sl], H[:, c, sl], F["mps"][:, :]),
                  reads=[H[:, c, sl], F["mps"][:, :]], writes=[H[:, c, sl]],
                  tag=f"{tag}_mka{c}_{piece}")
    if load_w2 is not None:
        load_w2()
    for o in range(NCH):
        for piece in range(4):
            sl = slice(piece * 512, piece * 512 + 512)
            pb = F["ps"][:, ((o * 4 + piece) % 2) * 512:((o * 4 + piece) % 2) * 512 + 512]

            def omm(t, o=o, sl=sl, pb=pb, F=F):
                for i in range(NCH):
                    t.matmul(pb, F["w2"][:, i, o * P:(o + 1) * P], H[:, i, sl],
                             start=(i == 0), stop=False)
                return t.matmul(pb, F["b2"][:, o * P:(o + 1) * P], F["m16"][:, sl],
                                start=False, stop=True)
            S.add("tensor", omm,
                  reads=[F["w2"][:, :, :], H[:, :, sl],
                         F["b2"][:, :], F["m16"][:, sl]],
                  writes=[pb], tag=f"{tag}_o{o}_{piece}")
            S.add("scalar", lambda s, o=o, sl=sl, pb=pb: s.activation(
                      out[:, o, sl], pb, AF.Identity, scale=out_scale),
                  reads=[pb], writes=[out[:, o, sl]], tag=f"{tag}_oe{o}_{piece}")


def emit_fff_route(S, F, tag):
    sco, rt, mb = F["sco"], F["rt"], F["mb16"]

    def sv(off, w):
        return sco[:, :].rearrange("p (t n) -> p t n", n=16)[:, :, off:off + w]

    def rv(off, w):
        return rt[:, off * 16:(off + w) * 16].rearrange("p (x t) -> p t x", t=16) \
            if False else rt[:, :].rearrange("p (a t) -> p t a", t=16)[:, :, off:off + w]

    V = "vector"
    A = lambda fn, r, w, tg: S.add(V, fn, reads=r, writes=w, tag=f"{tag}_{tg}")
    R = rv
    allr = [rt[:, :]]; alls = [sco[:, :]]
    A(lambda v: v.tensor_single_scalar(R(0, 1), sv(0, 1), 0.0, ALU.is_ge), alls, allr, "b1")
    A(lambda v: v.tensor_scalar(R(1, 1), R(0, 1), -1.0, 1.0, ALU.mult, ALU.add), allr, allr, "nb1")
    A(lambda v: v.tensor_sub(R(2, 1), sv(2, 1), sv(1, 1)), alls, allr, "d1")
    A(lambda v: v.tensor_mul(R(2, 1), R(2, 1), R(0, 1)), allr, allr, "d2")
    A(lambda v: v.tensor_add(R(2, 1), R(2, 1), sv(1, 1)), allr + alls, allr, "d3")
    A(lambda v: v.tensor_single_scalar(R(3, 1), R(2, 1), 0.0, ALU.is_ge), allr, allr, "b2")
    A(lambda v: v.tensor_scalar(R(4, 1), R(3, 1), -1.0, 1.0, ALU.mult, ALU.add), allr, allr, "nb2")
    A(lambda v: v.tensor_mul(R(9, 1), R(1, 1), R(4, 1)), allr, allr, "p200")
    A(lambda v: v.tensor_mul(R(10, 1), R(1, 1), R(3, 1)), allr, allr, "p201")
    A(lambda v: v.tensor_mul(R(11, 1), R(0, 1), R(4, 1)), allr, allr, "p210")
    A(lambda v: v.tensor_mul(R(12, 1), R(0, 1), R(3, 1)), allr, allr, "p211")
    A(lambda v: v.tensor_mul(R(21, 4), R(9, 4), sv(3, 4)), allr + alls, allr, "s2m")
    A(lambda v: v.tensor_reduce(R(2, 1).squeeze(2), R(21, 4), AX.X, ALU.add), allr, allr, "s2r")
    A(lambda v: v.tensor_single_scalar(R(5, 1), R(2, 1), 0.0, ALU.is_ge), allr, allr, "b3")
    A(lambda v: v.tensor_scalar(R(6, 1), R(5, 1), -1.0, 1.0, ALU.mult, ALU.add), allr, allr, "nb3")
    def p3view(b):
        return rt[:, :].rearrange("p (a t) -> p t a", t=16)[:, :, 13 + b:21 + b:2]
    A(lambda v: v.tensor_mul(p3view(0), R(9, 4), R(6, 1).broadcast_to([P, 16, 4])),
      allr, allr, "p3e")
    A(lambda v: v.tensor_mul(p3view(1), R(9, 4), R(5, 1).broadcast_to([P, 16, 4])),
      allr, allr, "p3o")
    A(lambda v: v.tensor_mul(R(21, 8), R(13, 8), sv(7, 8)), allr + alls, allr, "s3m")
    A(lambda v: v.tensor_reduce(R(2, 1).squeeze(2), R(21, 8), AX.X, ALU.add), allr, allr, "s3r")
    A(lambda v: v.tensor_single_scalar(R(7, 1), R(2, 1), 0.0, ALU.is_ge), allr, allr, "b4")
    A(lambda v: v.tensor_scalar(R(8, 1), R(7, 1), -1.0, 1.0, ALU.mult, ALU.add), allr, allr, "nb4")
    def mview(b):
        return mb[:, :].rearrange("p (t l) -> p t l", l=16)[:, :, b:16:2]
    A(lambda v: v.tensor_mul(mview(0), R(13, 8), R(8, 1).broadcast_to([P, 16, 8])),
      allr, [mb[:, :]], "me")
    A(lambda v: v.tensor_mul(mview(1), R(13, 8), R(7, 1).broadcast_to([P, 16, 8])),
      allr, [mb[:, :]], "mo")



def build_B1(FC=None):
    if FC is None:
        nc, S = _newprog()
        es = None
        x_d = nc.dram_tensor("x_fm", [D, TOK], F32, kind="ExternalInput")
        C = _mk_consts(nc)
        x_res = nc.alloc_sbuf_tensor("x_res", [P, NCH, TOK], F32)
        emit_consts(S, C)
        for c in range(NCH):
            S.dma("sync", x_res[:, c, :], x_d.ap()[c * P:(c + 1) * P, :], tag="ld_x")
        okind = "ExternalOutput"
    else:
        nc, S, C, x_res, es = FC["nc"], FC["S"], FC["C"], FC["x_res"], FC["es"]
        okind = "Internal"
    wd = {}
    for t in ("q", "k", "v"):
        wd[t] = dict(
            w1=nc.dram_tensor(f"{t}_w1", [D, D], F32, kind="ExternalInput"),
            w2=nc.dram_tensor(f"{t}_w2", [D, D], F32, kind="ExternalInput"),
            nwt=nc.dram_tensor(f"{t}_nwt", [D, 16], F32, kind="ExternalInput"),
            nb=nc.dram_tensor(f"{t}_nb", [1, 16], F32, kind="ExternalInput"),
            b1c=nc.dram_tensor(f"{t}_b1c", [P, NCH], F32, kind="ExternalInput"),
            b2=nc.dram_tensor(f"{t}_b2", [16, D], F32, kind="ExternalInput"),
        )
    q_o = nc.dram_tensor("q_fm_o", [D, TOK], BF16, kind=okind)
    qn_o = nc.dram_tensor("qn_o", [1, 16], F32, kind=okind)
    k_o = nc.dram_tensor("k_o", [1032, TOK], BF16, kind=okind)
    v_o = nc.dram_tensor("v_o", [TOK, 1032], BF16, kind=okind)

    if FC is None:
        xv = x_res.bitcast(BF16)
        xn = xv[:, :, 0:TOK]
        outb = nc.alloc_sbuf_tensor("outb", [P, NCH, TOK], BF16)
    else:
        xn = AL(nc, es, "B1_xn", [P, NCH, TOK], BF16)
        outb = FC["outb"]
    H = AL(nc, es, "B1_H", [P, NCH, TOK], BF16)
    Hf = H.bitcast(F32).rearrange("p c w -> p (c w)")
    wbuf = AL(nc, es, "B1_wbuf", [P, NCH, 1024], BF16)
    SCRF = AL(nc, es, "B1_SCRF", [P, TOK], F32)
    nwt = AL(nc, es, "B1_nwt_sb", [P, NCH, 16], BF16)
    nbb = AL(nc, es, "B1_nbb", [P, 16], F32)
    b1c = AL(nc, es, "B1_b1c_sb", [P, NCH], F32)
    b2 = AL(nc, es, "B1_b2_sb", [16, D], BF16)
    nbrow = AL(nc, es, "B1_nbrow", [1, 16], F32)
    sco = AL(nc, es, "B1_sco", [P, 256], F32)
    rt = AL(nc, es, "B1_rt", [P, 480], F32)
    mb16 = AL(nc, es, "B1_mb16", [P, 256], BF16)
    m16 = AL(nc, es, "B1_m16", [16, TOK], BF16)
    bsel = AL(nc, es, "B1_bsel", [16, NCH, P], BF16)
    bselscr = SCRF[0:16, 0:NCH * P]
    vstage = AL(nc, es, "B1_vstage", [P, 1032], BF16)
    sqstat = AL(nc, es, "B1_sqstat", [P, 512], BF16)
    statrow = AL(nc, es, "B1_statrow", [1, 16], F32)
    krow = SCRF.bitcast(BF16)[0:1, 0:TOK]

    S.add("gpsimd", lambda g: g.iota(bselscr[:, :], [[-2, NCH], [-1, 2], [0, 64]],
                                     base=0, channel_multiplier=1,
                                     allow_small_or_imprecise_dtypes=True),
          writes=[bselscr[:, :]], tag="bsel_i")
    S.add("vector", lambda v: v.tensor_single_scalar(
              bsel[:, :, :].rearrange("l c p -> l (c p)"), bselscr[:, :], 0.0,
              ALU.is_equal),
          reads=[bselscr[:, :]], writes=[bsel[:, :, :]], tag="bsel_e")

    lnscr = dict(
        xb=H[:, 0:2, :], sq=H[:, 2:4, :],
        mrow=Hf[0:1, 0:TOK], wrow=Hf[0:1, TOK:2 * TOK],
        mb=H[:, 4, :], ab=H[:, 5, :], t1=SCRF[:, :],
    )
    emit_ln(S, C, x_res, xn, lnscr, tag="ln2")

    with (nc.psum_tensor("fps", [P, 1024], F32) as fps,
          nc.psum_tensor("mps", [P, 512], F32) as mps,
          nc.psum_tensor("spsn", [P, 16], F32) as spsn,
          nc.psum_tensor("tpsm", [16, P], BF16) as tpsm,
          nc.psum_tensor("tpsv", [P, 1024], BF16) as tpsv,
          nc.psum_tensor("nbps", [P, 16], F32) as nbps):
        F = dict(xn=xn, H=H, sco=sco, rt=rt, mb16=mb16, m16=m16, bsel=bsel,
                 nwt=nwt, nbb=nbb, b1c=b1c, b2=b2, ps=fps, mps=mps, sps=spsn,
                 tps=tpsm,
                 w1=wbuf, w2=wbuf)
        for t, is_q in (("k", False), ("v", False), ("q", True)):
            d = wd[t]
            for c in range(NCH):
                S.dma("gpsimd", wbuf[:, c, :], d["w1"].ap()[c * P:(c + 1) * P, :],
                      tag=f"ld_{t}w1")
                S.dma("gpsimd", nwt[:, c, :], d["nwt"].ap()[c * P:(c + 1) * P, :],
                      tag=f"ld_{t}nw")

            def _ldw2(d=d, t=t):
                for c in range(NCH):
                    S.dma("gpsimd", wbuf[:, c, :], d["w2"].ap()[c * P:(c + 1) * P, :],
                          tag=f"ld_{t}w2")
            S.dma("sync", b1c[:, :], d["b1c"].ap(), tag=f"ld_{t}b1")
            S.dma("gpsimd", b2[:, :], d["b2"].ap(), tag=f"ld_{t}b2")
            S.dma("sync", nbrow[:, :], d["nb"].ap(), tag=f"ld_{t}nb")
            S.add("tensor", lambda tt: tt.matmul(nbps[:, :], C["ones_row_f"][:, :],
                                                 nbrow[:, :], start=True, stop=True),
                  reads=[C["ones_row_f"][:, :], nbrow[:, :]], writes=[nbps[:, :]],
                  tag=f"{t}_nbb")
            S.add("vector", lambda v: v.tensor_copy(nbb[:, :], nbps[:, :]),
                  reads=[nbps[:, :]], writes=[nbb[:, :]], tag=f"{t}_nbbc")
            F["out"] = outb
            emit_fff(S, C, F, tag=f"fff_{t}", out_scale=(1.0 / 32.0 if is_q else 1.0),
                     load_w2=_ldw2)
            if t == "k":
                emit_tok_norms(S, C, outb, TOK, statrow[:, 0:16], sqstat, False, "kn")
                S.add("vector", lambda v: v.memset(krow[:, :], 0.0),
                      writes=[krow[:, :]], tag="krow0")
                S.add("vector", lambda v: v.tensor_copy(krow[:, 0:1], statrow[:, 0:1]),
                      reads=[statrow[:, 0:1]], writes=[krow[:, 0:1]], tag="krowc")
                for c in range(NCH):
                    S.dma("sync", k_o.ap()[c * P:(c + 1) * P, :], outb[:, c, :],
                          tag="st_k")
                S.dma("sync", k_o.ap()[1024:1025, :], krow[0:1, :], tag="st_kst")
            elif t == "v":
                S.add("vector", lambda v: v.memset(vstage[:, 1024:1032], 0.0),
                      writes=[vstage[:, 1024:1032]], tag="vst0")
                S.add("vector", lambda v: v.memset(vstage[:, 1024:1025], 1.0),
                      writes=[vstage[:, 1024:1025]], tag="vst1")
                for st in range(NT):
                    for c in range(NCH):
                        S.add("tensor", lambda tt, st=st, c=c: tt.transpose(
                                  tpsv[:, c * P:(c + 1) * P],
                                  outb[:, c, st * P:(st + 1) * P], C["ident"][:, :]),
                              reads=[outb[:, c, st * P:(st + 1) * P], C["ident"][:, :]],
                              writes=[tpsv[:, c * P:(c + 1) * P]], tag=f"vtp{st}_{c}")
                    S.add("vector", lambda v, st=st: v.tensor_copy(
                              vstage[:, 0:1024], tpsv[:, :]),
                          reads=[tpsv[:, :]], writes=[vstage[:, 0:1024]],
                          tag=f"vtc{st}")
                    S.dma("sync", v_o.ap()[st * P:(st + 1) * P, :], vstage[:, :],
                          tag=f"st_v{st}")
            else:
                emit_tok_norms(S, C, outb, TOK, statrow[:, 0:16], sqstat, False, "qn")
                S.dma("sync", qn_o.ap(), statrow[:, :], tag="st_qn")
                for c in range(NCH):
                    S.dma("sync", q_o.ap()[c * P:(c + 1) * P, :], outb[:, c, :],
                          tag="st_q")

    if FC is None:
        S.barrier_read("sync", [q_o.ap(), qn_o.ap(), k_o.ap(), v_o.ap()])
        S.build()
        return nc
    kg = nc.dram_tensor("B1_kg", [8 * 1032, TOK], BF16, kind="Internal")
    vg = nc.dram_tensor("B1_vg", [8 * TOK, 1032], BF16, kind="Internal")
    if not NO_CC:
        S.cc("AllGather", ALU.bypass, GROUPS8, k_o.ap().opt(), kg.ap().opt(), tag="ccK0")
        S.cc("AllGather", ALU.bypass, GROUPS8, v_o.ap().opt(), vg.ap().opt(), tag="ccV0")
    FC["io"].update(B1_q=q_o, B1_qn=qn_o, B1_kg=kg, B1_vg=vg)
    return nc


def pack_B1(inputs, x1, core):
    sl = x1.reshape(NCORE * TOK, D)[core * TOK:(core + 1) * TOK]
    m = {"x_fm": np.ascontiguousarray(sl.T.astype(np.float32))}
    for t in ("q", "k", "v"):
        w1 = np.asarray(inputs[f"L0_ffn_{t}_w1"], np.float32)
        w2 = np.asarray(inputs[f"L0_ffn_{t}_w2"], np.float32)
        nw = np.asarray(inputs[f"L0_ffn_{t}_node_w"], np.float32)
        nb = np.asarray(inputs[f"L0_ffn_{t}_node_b"], np.float32)
        b1 = np.asarray(inputs[f"L0_ffn_{t}_b1"], np.float32)
        b2 = np.asarray(inputs[f"L0_ffn_{t}_b2"], np.float32)
        m[f"{t}_w1"] = np.ascontiguousarray(w1.transpose(1, 0, 2).reshape(D, D))
        m[f"{t}_w2"] = np.ascontiguousarray(w2.reshape(D, D))
        nwt = np.zeros((D, 16), np.float32); nwt[:, 0:15] = nw.T
        m[f"{t}_nwt"] = nwt
        nbp = np.zeros((1, 16), np.float32); nbp[0, 0:15] = nb
        m[f"{t}_nb"] = nbp
        m[f"{t}_b1c"] = chunk_col(b1.reshape(D))
        m[f"{t}_b2"] = np.ascontiguousarray(b2)
    return m


def build_B2(FC=None):
    if FC is None:
        nc, S = _newprog()
        es = None
        x_d = nc.dram_tensor("x_fm", [D, TOK], F32, kind="ExternalInput")
        q_d = nc.dram_tensor("q_fm", [D, TOK], BF16, kind="ExternalInput")
        qn_d = nc.dram_tensor("qn", [1, 16], F32, kind="ExternalInput")
        k_d = nc.dram_tensor("kfull", [4 * 1032, TOK], BF16, kind="ExternalInput")
        v_d = nc.dram_tensor("vfull", [4 * TOK, 1032], BF16, kind="ExternalInput")
        out_d = nc.dram_tensor("x2_fm", [D, TOK], F32, kind="ExternalOutput")
        C = _mk_consts(nc)
        x_res = nc.alloc_sbuf_tensor("x_res", [P, NCH, TOK], F32)
        emit_consts(S, C)
    else:
        nc, S, C, x_res, es = FC["nc"], FC["S"], FC["C"], FC["x_res"], FC["es"]
        q_d = FC["io"]["B1_q"]; qn_d = FC["io"]["B1_qn"]
        k_d = FC["io"]["B1_kg"]; v_d = FC["io"]["B1_vg"]
    if DEBUG_B2:
        dbg_l = nc.dram_tensor("dbg_l", [1, 1024], F32, kind="ExternalOutput")
        dbg_ctx = nc.dram_tensor("dbg_ctx", [P, 1024], F32, kind="ExternalOutput")
        dbg_lb = nc.dram_tensor("dbg_lb", [P, 1024], F32, kind="ExternalOutput")
        dbg_t1 = nc.dram_tensor("dbg_t1", [P, 1024], F32, kind="ExternalOutput")
        dbg_xr = nc.dram_tensor("dbg_xr", [P, 1024], F32, kind="ExternalOutput")
        dbg_x0 = nc.dram_tensor("dbg_x0", [P, 1024], F32, kind="ExternalOutput")

    q_fm = FC["outb"] if FC is not None else nc.alloc_sbuf_tensor("q_sb", [P, NCH, TOK], BF16)
    ctx = AL(nc, es, "B2_ctx", [P, NCH, 1024], F32)
    K_sb = AL(nc, es, "B2_K_sb", [P, NCH, 1024], BF16)
    V_sb = AL(nc, es, "B2_V_sb", [P, NCH, 1032], BF16)
    Pt = AL(nc, es, "B2_Pt", [P, NCH, 1024], BF16)
    rows = AL(nc, es, "B2_rows", [1, 64], F32)
    negc = AL(nc, es, "B2_negc", [P, 8], F32)
    lrow = AL(nc, es, "B2_lrow", [1, 1024], F32)
    linvB = AL(nc, es, "B2_linvB", [P, 1024], F32)
    t1f = AL(nc, es, "B2_t1f", [P, 1024], F32)

    for c in range(NCH):
        if FC is None:
            S.dma("sync", x_res[:, c, :], x_d.ap()[c * P:(c + 1) * P, :], tag="ld_x")
            S.dma("sync", q_fm[:, c, :], q_d.ap()[c * P:(c + 1) * P, :], tag="ld_q")
    if DEBUG_B2:
        S.dma("sync", dbg_x0.ap(), x_res[:, 0, 0:1024], tag="dbg_x0")
    S.dma("sync", rows[0:1, 0:16], qn_d.ap(), tag="ld_qn")
    nrk = 4 if FC is None else 8
    kst = bass.AP(k_d, 1024 * TOK, [[1, 1], [1032 * TOK, nrk]])
    S.dma("gpsimd", rows[0:1, 16:16 + nrk], kst, tag="ld_kst",
          allow_slow_non_contiguous=True)
    S.add("vector", lambda v: v.tensor_reduce(rows[:, 24:25], rows[:, 16:16 + nrk],
                                              AX.X, ALU.max),
          reads=[rows[:, 16:16 + nrk]], writes=[rows[:, 24:25]], tag="kmax")
    emit_shift(S, C, rows[:, 0:1], rows[:, 24:25], 1, negc, rows[:, 32:64], "shift")

    with (nc.psum_tensor("sps", [P, 1024], F32) as sps,
          nc.psum_tensor("sps2", [P, 1024], F32) as sps2,
          nc.psum_tensor("lps", [1, 1024], F32) as lps,
          nc.psum_tensor("cps", [P, 1024], F32) as cps):
        SPS = (sps, sps2)
        for qh in range(2):
            qsl = slice(qh * 1024, qh * 1024 + 1024)
            ng = 0
            pend = [None]

            def load_group(g, qh=qh):
                rank = g // 2
                kcol = (g % 2) * 1024
                for c in range(NCH):
                    if FC is None:
                        kap = k_d.ap()[1032 * rank + c * P:1032 * rank + (c + 1) * P,
                                       kcol:kcol + 1024]
                        vap = v_d.ap()[TOK * rank + kcol + c * P:
                                       TOK * rank + kcol + (c + 1) * P, :]
                        S.dma("sync", K_sb[:, c, :], kap, tag=f"ldK{qh}_{g}")
                        S.dma("sync", V_sb[:, c, :], vap, tag=f"ldV{qh}_{g}")
                    else:
                        kap = bass.AP(k_d, (2 * (1032 * rank + c * P)
                                            + kcol // 1024) * 1024,
                                      [[2048, P], [1, 1024]])
                        vap = bass.AP(v_d, (TOK * rank + kcol + c * P) * 1032,
                                      [[1032, P], [1, 1032]])
                        S.idma(K_sb[:, c, :], kap, FC["bk2"][:, :], tag=f"ldK{qh}_{g}")
                        S.idma(V_sb[:, c, :], vap, FC["bv"][:, :], tag=f"ldV{qh}_{g}")

            def mk_pv(g, qh=qh):
                def emit():
                    for c in range(NCH):
                        def pvmm(tt, c=c):
                            for piece in range(2):
                                psl = cps[:, piece * 512:piece * 512 + 512]
                                for t in range(NCH):
                                    ins = tt.matmul(
                                        psl, V_sb[:, t, c * P:(c + 1) * P],
                                        Pt[:, t, piece * 512:piece * 512 + 512],
                                        start=(t == 0), stop=(t == NCH - 1))
                            return ins
                        S.add("tensor", pvmm,
                              reads=[V_sb[:, :, c * P:(c + 1) * P], Pt[:, :, :]],
                              writes=[cps[:, :]], tag=f"pv{qh}_{g}_{c}")
                        if g == 0:
                            S.add("vector", lambda v, c=c: v.tensor_copy(
                                      ctx[:, c, :], cps[:, :]),
                                  reads=[cps[:, :]], writes=[ctx[:, c, :]],
                                  tag=f"fl{qh}_{g}_{c}")
                        else:
                            S.add("vector", lambda v, c=c: v.tensor_add(
                                      ctx[:, c, :], ctx[:, c, :], cps[:, :]),
                                  reads=[cps[:, :], ctx[:, c, :]],
                                  writes=[ctx[:, c, :]], tag=f"fl{qh}_{g}_{c}")
                return emit

            def emit_qk(g, t, qh=qh):
                sp = SPS[t % 2]

                def qkmm(tt, t=t, qh=qh, sp=sp):
                    for piece in range(2):
                        psl = sp[:, piece * 512:piece * 512 + 512]
                        for c in range(NCH):
                            ins = tt.matmul(
                                psl, K_sb[:, c, t * P:(t + 1) * P],
                                q_fm[:, c, qh * 1024 + piece * 512:
                                     qh * 1024 + piece * 512 + 512],
                                start=(c == 0), stop=(c == NCH - 1))
                    return ins
                S.add("tensor", qkmm,
                      reads=[K_sb[:, :, t * P:(t + 1) * P], q_fm[:, :, qsl]],
                      writes=[sp[:, :]], tag=f"qk{qh}_{g}_{t}")

            def emit_expl(g, t, ng_, qh=qh):
                sp = SPS[t % 2]
                S.add("scalar", lambda s, t=t, sp=sp: s.activation(
                          Pt[:, t, :], sp[:, :], AF.Exp, bias=negc[:, 0:1]),
                      reads=[sp[:, :], negc[:, 0:1]], writes=[Pt[:, t, :]],
                      tag=f"exp{qh}_{g}_{t}")
                def lmm(tt, t=t, ng_=ng_):
                    for piece in range(2):
                        ins = tt.matmul(lps[:, piece * 512:piece * 512 + 512],
                                        C["ones_bf"][:, :],
                                        Pt[:, t, piece * 512:piece * 512 + 512],
                                        start=(ng_ == 0), stop=(ng_ == 63))
                    return ins
                S.add("tensor", lmm, reads=[C["ones_bf"][:, :], Pt[:, t, :]],
                      writes=[lps[:, :]], tag=f"l{qh}_{g}_{t}")

            load_group(0)
            for g in range(8):
                if g + 1 < 8:
                    load_group(g + 1)
                emit_qk(g, 0)
                emit_qk(g, 1)
                if pend[0] is not None:
                    pend[0]()
                emit_expl(g, 0, ng)
                for t in range(2, NCH):
                    emit_qk(g, t)
                    emit_expl(g, t - 1, ng + t - 1)
                emit_expl(g, NCH - 1, ng + NCH - 1)
                ng += NCH
                pend[0] = mk_pv(g)
            pend[0]()
            pend[0] = None
            S.add("vector", lambda v: v.tensor_scalar_add(lrow[:, :], lps[:, :], 1e-30),
                  reads=[lps[:, :]], writes=[lrow[:, :]], tag=f"lc{qh}")
            S.add("vector", lambda v: v.reciprocal(lrow[:, :], lrow[:, :]),
                  reads=[lrow[:, :]], writes=[lrow[:, :]], tag=f"li{qh}")
            for p0 in (0, 512):
                S.add("tensor", lambda t, p0=p0: t.matmul(
                          cps[:, p0:p0 + 512], C["ones_row_f"][:, :],
                          lrow[:, p0:p0 + 512], start=True, stop=True),
                      reads=[C["ones_row_f"][:, :], lrow[:, :]],
                      writes=[cps[:, p0:p0 + 512]], tag=f"lb{qh}")
            S.add("vector", lambda v: v.tensor_copy(linvB[:, :], cps[:, :]),
                  reads=[cps[:, :]], writes=[linvB[:, :]], tag=f"lbc{qh}")
            if qh == 0 and DEBUG_B2:
                S.dma("sync", dbg_lb.ap(), linvB[:, :], tag="dbg_lb")
            if qh == 0 and DEBUG_B2:
                S.dma("sync", dbg_l.ap(), lrow[:, :], tag="dbg_l")
                S.dma("sync", dbg_ctx.ap(), ctx[:, 0, :], tag="dbg_ctx")
            for c in range(NCH):
                S.add("vector", lambda v, c=c: v.tensor_mul(t1f[:, :], ctx[:, c, :],
                                                            linvB[:, :]),
                      reads=[ctx[:, c, :], linvB[:, :]], writes=[t1f[:, :]],
                      tag=f"nm{qh}_{c}")
                if qh == 0 and c == 0 and DEBUG_B2:
                    S.dma("sync", dbg_t1.ap(), t1f[:, :], tag="dbg_t1")
                S.add("vector", lambda v, c=c, qsl=qsl: v.tensor_add(
                          x_res[:, c, qsl], x_res[:, c, qsl], t1f[:, :]),
                      reads=[t1f[:, :], x_res[:, c, qsl]], writes=[x_res[:, c, qsl]],
                      tag=f"ra{qh}_{c}")
                if qh == 0 and c == 0 and DEBUG_B2:
                    S.dma("sync", dbg_xr.ap(), x_res[:, 0, 0:1024], tag="dbg_xr")
    if FC is None:
        for c in range(NCH):
            S.dma("sync", out_d.ap()[c * P:(c + 1) * P, :], x_res[:, c, :], tag="st_x")
        S.barrier_read("sync", [out_d.ap()])
        S.build()
    return nc


def emit_proj_generic(S, C, wbuf, src_fn, dst_fn, ps, nchunks_in, width, act, bias_col,
                      scale, tag):
    npc = (width + 511) // 512
    for o in range(NCH):
        for piece in range(npc):
            w = min(512, width - piece * 512)
            sl = slice(piece * 512, piece * 512 + w)
            pb = ps[:, ((o * npc + piece) % 2) * 512:((o * npc + piece) % 2) * 512 + w]
            ni = nchunks_in

            def mmgroup(t, o=o, sl=sl, pb=pb, src_fn=src_fn, ni=ni, wbuf=wbuf):
                for i in range(ni):
                    ins = t.matmul(pb, wbuf[:, i, o * P:(o + 1) * P],
                                   src_fn(i)[:, sl],
                                   start=(i == 0), stop=(i == ni - 1))
                return ins
            S.add("tensor", mmgroup,
                  reads=[wbuf[:, i, o * P:(o + 1) * P] for i in range(ni)] +
                        [src_fn(i)[:, sl] for i in range(ni)],
                  writes=[pb], tag=f"{tag}_mm{o}_{piece}")
            b = bias_col(o) if bias_col is not None else 0.0
            dsl = dst_fn(o)[:, sl]
            S.add("scalar", lambda s, dsl=dsl, pb=pb, b=b, act=act, scale=scale:
                  s.activation(dsl, pb, act, bias=b, scale=scale),
                  reads=[pb] + ([b] if not isinstance(b, float) else []),
                  writes=[dsl], tag=f"{tag}_ev{o}_{piece}")


def build_C1(FC=None):
    if FC is None:
        nc, S = _newprog()
        es = None
        x_d = nc.dram_tensor("x_fm", [D, TOK], F32, kind="ExternalInput")
        C = _mk_consts(nc)
        x_res = nc.alloc_sbuf_tensor("x_res", [P, NCH, TOK], F32)
        emit_consts(S, C)
        for c in range(NCH):
            S.dma("sync", x_res[:, c, :], x_d.ap()[c * P:(c + 1) * P, :], tag="ld_x")
        okind = "ExternalOutput"
    else:
        nc, S, C, x_res, es = FC["nc"], FC["S"], FC["C"], FC["x_res"], FC["es"]
        okind = "Internal"
    past_d = nc.dram_tensor("past_fm", [D, 512], F32, kind="ExternalInput")
    wds = {}
    for t in ("q", "k", "v", "rk", "rv"):
        wds[t] = (nc.dram_tensor(f"W{t}", [D, D], F32, kind="ExternalInput"),
                  nc.dram_tensor(f"b{t}c", [P, NCH], F32, kind="ExternalInput"))
    q_o = nc.dram_tensor("C1_q_o", [D, TOK], BF16, kind=okind)
    qn_o = nc.dram_tensor("C1_qn_o", [1, 16], F32, kind=okind)
    k_o = nc.dram_tensor("C1_k_o", [1032, TOK], BF16, kind=okind)
    v_o = nc.dram_tensor("C1_v_o", [TOK, 1040], BF16, kind=okind)
    rk_o = nc.dram_tensor("C1_rk_o", [1032, 512], BF16, kind=okind)
    rv_o = nc.dram_tensor("C1_rv_o", [512, 1040], BF16, kind=okind)

    if FC is None:
        xv = x_res.bitcast(BF16)
        xn = xv[:, :, 0:TOK]
        outb = nc.alloc_sbuf_tensor("outb", [P, NCH, TOK], BF16)
    else:
        xn = AL(nc, es, "C1_xn", [P, NCH, TOK], BF16)
        outb = FC["outb"]
    H = AL(nc, es, "C1_H", [P, NCH, TOK], BF16)
    Hf = H.bitcast(F32).rearrange("p c w -> p (c w)")
    wbuf = AL(nc, es, "C1_wbuf", [P, NCH, 1024], BF16)
    SCRF = AL(nc, es, "C1_SCRF", [P, TOK], F32)
    past_sb = AL(nc, es, "C1_past", [P, NCH, 512], BF16)
    bcol = AL(nc, es, "C1_bcol", [P, NCH], F32)
    statrow = AL(nc, es, "C1_statrow", [1, 16], F32)
    sqstat = AL(nc, es, "C1_sqstat", [P, 512], BF16)
    krow = AL(nc, es, "C1_krow", [1, TOK], BF16)
    vstage = AL(nc, es, "C1_vstage", [P, 1040], BF16)

    for c in range(NCH):
        S.dma("gpsimd", past_sb[:, c, :], past_d.ap()[c * P:(c + 1) * P, :], tag="ld_p")
    lnscr = dict(xb=H[:, 0:2, :], sq=H[:, 2:4, :],
                 mrow=Hf[0:1, 0:TOK], wrow=Hf[0:1, TOK:2 * TOK],
                 mb=H[:, 4, :], ab=H[:, 5, :], t1=SCRF[:, :])
    emit_ln(S, C, x_res, xn, lnscr, tag="ln3")

    def stage_v(src, width, v_out, nt, tag):
        S.add("vector", lambda v: v.memset(vstage[:, :].rearrange(
                  "p (h w) -> p h w", w=65)[:, :, 64:65], 1.0),
              writes=[vstage[:, :]], tag=f"{tag}_ones")
        with nc.psum_tensor(f"{tag}_tps", [P, 1024], BF16) as tps:
            for st in range(nt):
                for c in range(NCH):
                    S.add("tensor", lambda tt, st=st, c=c: tt.transpose(
                              tps[:, c * P:(c + 1) * P], src[:, c, st * P:(st + 1) * P],
                              C["ident"][:, :]),
                          reads=[src[:, c, st * P:(st + 1) * P], C["ident"][:, :]],
                          writes=[tps[:, c * P:(c + 1) * P]], tag=f"{tag}_t{st}_{c}")
                S.add("vector", lambda v, st=st: v.tensor_copy(
                          vstage[:, :].rearrange("p (h w) -> p h w", w=65)[:, :, 0:64],
                          tps[:, :].rearrange("p (h w) -> p h w", w=64)),
                      reads=[tps[:, :]], writes=[vstage[:, :]], tag=f"{tag}_c{st}")
                S.dma("sync", v_out.ap()[st * P:(st + 1) * P, :], vstage[:, :],
                      tag=f"{tag}_st{st}")

    with nc.psum_tensor("C1_pps", [P, 1024], F32) as pps:
        for t, src_fn, n_in, width, outdst in (
                ("k", lambda i: xn[:, i, :], NCH, TOK, None),
                ("v", lambda i: xn[:, i, :], NCH, TOK, None),
                ("rk", lambda i: past_sb[:, i, :], NCH, 512, None),
                ("rv", lambda i: past_sb[:, i, :], NCH, 512, None),
                ("q", lambda i: xn[:, i, :], NCH, TOK, None)):
            w_d, b_d = wds[t]
            for c in range(NCH):
                S.dma("gpsimd", wbuf[:, c, :], w_d.ap()[c * P:(c + 1) * P, :],
                      tag=f"ld_w{t}")
            S.dma("sync", bcol[:, :], b_d.ap(), tag=f"ld_b{t}")
            scale = 1.0 / 8.0 if t in ("q",) else 1.0
            if t == "q":
                S.add("vector", lambda v: v.tensor_scalar_mul(bcol[:, :], bcol[:, :],
                                                              1.0 / 8.0),
                      reads=[bcol[:, :]], writes=[bcol[:, :]], tag="bq8")
            dstbuf = outb
            emit_proj_generic(S, C, wbuf, src_fn,
                              lambda o: dstbuf[:, o, 0:width], pps, n_in, width,
                              AF.Identity, lambda o: bcol[:, o:o + 1], scale,
                              tag=f"pj_{t}")
            if t == "q":
                emit_tok_norms(S, C, outb, TOK, statrow[:, 0:16], sqstat, True, "c1qn")
                S.dma("sync", qn_o.ap(), statrow[:, :], tag="st_qn")
                for c in range(NCH):
                    S.dma("sync", q_o.ap()[c * P:(c + 1) * P, :], outb[:, c, :],
                          tag="st_q")
            elif t in ("k", "rk"):
                wd_ = TOK if t == "k" else 512
                out_t = k_o if t == "k" else rk_o
                emit_tok_norms(S, C, outb[:, :, 0:wd_] if wd_ != TOK else outb,
                               wd_, statrow[:, 0:16], sqstat, True, f"c1{t}n")
                S.add("vector", lambda v, wd_=wd_: v.memset(krow[:, 0:wd_], 0.0),
                      writes=[krow[:, :]], tag=f"{t}row0")
                S.add("vector", lambda v: v.tensor_copy(krow[:, 0:8], statrow[:, 0:8]),
                      reads=[statrow[:, 0:8]], writes=[krow[:, 0:8]], tag=f"{t}rowc")
                for c in range(NCH):
                    S.dma("sync", out_t.ap()[c * P:(c + 1) * P, :], outb[:, c, 0:wd_],
                          tag=f"st_{t}")
                S.dma("sync", out_t.ap()[1024:1025, :], krow[0:1, 0:wd_], tag=f"st_{t}s")
            elif t == "v":
                stage_v(outb, TOK, v_o, NT, "vst")
            elif t == "rv":
                stage_v(outb[:, :, 0:512].rearrange("p c w -> p c w"), 512, rv_o, 4,
                        "rvst")

    if FC is None:
        S.barrier_read("sync", [q_o.ap(), qn_o.ap(), k_o.ap(), v_o.ap(), rk_o.ap(),
                                rv_o.ap()])
        S.build()
        return nc
    kg = nc.dram_tensor("C1_kg", [8 * 1032, TOK], BF16, kind="Internal")
    vg = nc.dram_tensor("C1_vg", [8 * TOK, 1040], BF16, kind="Internal")
    rkg = nc.dram_tensor("C1_rkg", [8 * 1032, 512], BF16, kind="Internal")
    rvg = nc.dram_tensor("C1_rvg", [8 * 512, 1040], BF16, kind="Internal")
    S.cc("AllGather", ALU.bypass, GROUPS8, k_o.ap().opt(), kg.ap().opt(), tag="ccK1")
    S.cc("AllGather", ALU.bypass, GROUPS8, v_o.ap().opt(), vg.ap().opt(), tag="ccV1")
    S.cc("AllGather", ALU.bypass, GROUPS8, rk_o.ap().opt(), rkg.ap().opt(), tag="ccRK")
    S.cc("AllGather", ALU.bypass, GROUPS8, rv_o.ap().opt(), rvg.ap().opt(), tag="ccRV")
    FC["io"].update(C1_q=q_o, C1_qn=qn_o, C1_kg=kg, C1_vg=vg, C1_rkg=rkg, C1_rvg=rvg)
    return nc


def pack_C1(inputs, x2, core):
    sl = x2.reshape(NCORE * TOK, D)[core * TOK:(core + 1) * TOK]
    past = np.asarray(inputs["L1_attn_past_states"], np.float32).reshape(4096, D)
    psl = past[core * 512:(core + 1) * 512]
    m = {"x_fm": np.ascontiguousarray(sl.T.astype(np.float32)),
         "past_fm": np.ascontiguousarray(psl.T)}
    for t, wk, bk in (("q", "Wq", "bq"), ("k", "Wk", "bk"), ("v", "Wv", "bv"),
                      ("rk", "Wrk", "brk"), ("rv", "Wrv", "brv")):
        m[f"W{t}"] = np.ascontiguousarray(np.asarray(inputs[f"L1_attn_{wk}"], np.float32))
        m[f"b{t}c"] = chunk_col(inputs[f"L1_attn_{bk}"])
    return m


def build_C2(FC=None):
    if FC is None:
        nc, S = _newprog()
        es = None
        x_d = nc.dram_tensor("x_fm", [D, TOK], F32, kind="ExternalInput")
        q_d = nc.dram_tensor("q_fm", [D, TOK], BF16, kind="ExternalInput")
        qn_d = nc.dram_tensor("qn", [1, 16], F32, kind="ExternalInput")
        k_d = nc.dram_tensor("kfull", [4 * 1032, TOK], BF16, kind="ExternalInput")
        v_d = nc.dram_tensor("vfull", [4 * TOK, 1040], BF16, kind="ExternalInput")
        rk_d = nc.dram_tensor("rkfull", [8 * 1032, 512], BF16, kind="ExternalInput")
        rv_d = nc.dram_tensor("rvfull", [8 * 512, 1040], BF16, kind="ExternalInput")
        out_d = nc.dram_tensor("x3_fm", [D, TOK], F32, kind="ExternalOutput")
        C = _mk_consts(nc)
        emit_consts(S, C)
        x_res = None
    else:
        nc, S, C, x_res, es = FC["nc"], FC["S"], FC["C"], FC["x_res"], FC["es"]
        q_d = FC["io"]["C1_q"]; qn_d = FC["io"]["C1_qn"]
        k_d = FC["io"]["C1_kg"]; v_d = FC["io"]["C1_vg"]
        rk_d = FC["io"]["C1_rkg"]; rv_d = FC["io"]["C1_rvg"]
    wo_d = nc.dram_tensor("Wo", [D, D], F32, kind="ExternalInput")
    boc_d = nc.dram_tensor("boc", [P, NCH], F32, kind="ExternalInput")
    wrq_d = nc.dram_tensor("Wrq", [D, D], F32, kind="ExternalInput")
    brqc_d = nc.dram_tensor("brqc", [P, NCH], F32, kind="ExternalInput")
    wgA_d = nc.dram_tensor("WgA", [D, D], F32, kind="ExternalInput")
    wgB_d = nc.dram_tensor("WgB", [D, D], F32, kind="ExternalInput")
    bgc_d = nc.dram_tensor("bgc", [P, NCH], F32, kind="ExternalInput")

    q_fm = FC["outb"] if FC is not None else nc.alloc_sbuf_tensor("q_sb", [P, NCH, TOK], BF16)
    nctx = AL(nc, es, "C2_nctx", [P, NCH, TOK], BF16)
    out_fm = AL(nc, es, "C2_out_fm", [P, NCH, TOK], BF16)
    rec_fm = q_fm
    gate_fm = nctx
    war = AL(nc, es, "C2_war", [P, 16384], BF16)
    nc_ctxan = (AL(nc, es, "C2_ctxAn", [64, 1024], BF16),
                AL(nc, es, "C2_ctxBn", [64, 1024], BF16))
    wbuf = war[:, 0:8192].rearrange("p (c w) -> p c w", w=1024)
    wbuf2 = war[:, 8192:16384].rearrange("p (c w) -> p c w", w=1024)
    K_sb = war[:, 0:4096].rearrange("p (s w) -> p s w", w=1024)
    V_sb = war[:, 4096:8256].rearrange("p (s t w) -> p s t w", t=8, w=130)
    Pt = war[:, 8256:12352].rearrange("p (s w) -> p s w", w=2048)
    shF = war[:, 8256:12352].bitcast(F32)
    ctxAn = nc_ctxan[0]
    ctxBn = nc_ctxan[1]
    onesF = AL(nc, es, "C2_onesF", [P, P], F32)
    Ilo = AL(nc, es, "C2_Ilo", [64, P], BF16)
    Ihi = AL(nc, es, "C2_Ihi", [64, P], BF16)
    ishft = AL(nc, es, "C2_ishft", [64, P], F32)
    rows = AL(nc, es, "C2_rows_sb", [1, 192], F32)
    negc_s = AL(nc, es, "C2_negc_s", [P, 8], F32)
    negc_r = AL(nc, es, "C2_negc_r", [P, 8], F32)
    bcol = AL(nc, es, "C2_bcol", [P, NCH], F32)
    bcol2 = AL(nc, es, "C2_bcol2", [P, NCH], F32)
    sqstat = AL(nc, es, "C2_sqstat", [P, 512], BF16)
    statrow = AL(nc, es, "C2_statrow", [1, 16], F32)
    t1f = AL(nc, es, "C2_t1f", [P, 512], F32)
    t2f = AL(nc, es, "C2_t2f", [P, 512], BF16)

    S.add("gpsimd", lambda g: g.memset(onesF[:, :], 1.0), writes=[onesF[:, :]])
    S.add("gpsimd", lambda g: g.iota(ishft[:, :], [[1, P]], base=0,
                                     channel_multiplier=-1,
                                     allow_small_or_imprecise_dtypes=True),
          writes=[ishft[:, :]], tag="ilo_i")
    S.add("vector", lambda v: v.tensor_single_scalar(Ilo[:, :], ishft[:, :], 0.0,
                                                     ALU.is_equal),
          reads=[ishft[:, :]], writes=[Ilo[:, :]], tag="ilo")
    S.add("vector", lambda v: v.tensor_single_scalar(Ihi[:, :], ishft[:, :], 64.0,
                                                     ALU.is_equal),
          reads=[ishft[:, :]], writes=[Ihi[:, :]], tag="ihi")

    if FC is None:
        xs = AL(nc, es, "C2_xs", [P, 2, 512], F32)
        ys = AL(nc, es, "C2_ys", [P, 2, 512], F32)
    for c in range(NCH):
        S.dma("sync", q_fm[:, c, :], q_d.ap()[c * P:(c + 1) * P, :], tag="ld_q")
    if DEBUG_B2:
        S.dma("sync", dbg_x0.ap(), x_res[:, 0, 0:1024], tag="dbg_x0")
    S.dma("sync", rows[0:1, 0:16], qn_d.ap(), tag="ld_qn")
    nrk = 4 if FC is None else 8
    kst = bass.AP(k_d, 1024 * TOK, [[1, 1], [1032 * TOK, nrk], [1, 8]])
    S.dma("gpsimd", rows[0:1, 16:16 + 8 * nrk], kst, tag="ld_kst",
          allow_slow_non_contiguous=True)
    S.add("vector", lambda v: v.tensor_reduce(
              rows[:, 80:88],
              rows[:, 16:16 + 8 * nrk].rearrange("o (r c) -> o c r", c=8),
              AX.X, ALU.max),
          reads=[rows[:, 16:16 + 8 * nrk]], writes=[rows[:, 80:88]], tag="knmax")
    emit_shift(S, C, rows[:, 0:8], rows[:, 80:88], 8, negc_s, rows[:, 96:160], "shs")

    def attn_pass(tag, q_src, negc, ranks, kwidth, kd, vd, ngroups_per_rank, ntile,
                  epilogue):
        with (nc.psum_tensor(f"{tag}_sps", [P, 2048], F32) as sps,
              nc.psum_tensor(f"{tag}_cA", [65, 1024], F32) as cA,
              nc.psum_tensor(f"{tag}_cB", [65, 1024], F32) as cB):
            for qh in range(2):
                for cp in range(NCH):
                    groups = [(rk, gg) for rk in ranks
                              for gg in range(ngroups_per_rank)]
                    nt_total = len(groups) * ntile
                    klen = kwidth if ngroups_per_rank == 1 else 1024
                    vw = kwidth if ngroups_per_rank == 1 else TOK

                    def load_group(gi, qh=qh, cp=cp, klen=klen, vw=vw, groups=groups):
                        rk, gg = groups[gi]
                        slot = gi % 4
                        kcol = gg * 1024
                        if FC is None or tag != "sa":
                            kap = kd.ap()[1032 * rk + cp * P:1032 * rk + (cp + 1) * P,
                                          kcol:kcol + klen]
                            S.dma("sync", K_sb[:, slot, 0:klen], kap,
                                  tag=f"{tag}K{qh}_{cp}_{gi}")
                        else:
                            kap = bass.AP(kd, (2 * (1032 * rk + cp * P)
                                               + kcol // 1024) * 1024,
                                          [[2048, P], [1, 1024]])
                            S.idma(K_sb[:, slot, 0:klen], kap, FC["bk2"][:, :],
                                   tag=f"{tag}K{qh}_{cp}_{gi}")
                        for t in range(ntile):
                            if FC is None or tag != "sa":
                                vap = vd.ap()[vw * rk + kcol + t * P:
                                              vw * rk + kcol + (t + 1) * P,
                                              130 * cp:130 * cp + 130]
                                S.dma("sync", V_sb[:, slot, t, :], vap,
                                      tag=f"{tag}V{qh}_{cp}_{gi}_{t}")
                            else:
                                vap = bass.AP(
                                    vd, ((TOK * rk + kcol + t * P) * 8 + cp) * 130,
                                    [[8 * 130, P], [1, 130]])
                                S.idma(V_sb[:, slot, t, :], vap, FC["bv8"][:, :],
                                       tag=f"{tag}V{qh}_{cp}_{gi}_{t}")

                    for pf in range(min(3, len(groups))):
                        load_group(pf)
                    pend = [None]
                    it = 0
                    for gi in range(len(groups)):
                        slot = gi % 4
                        if gi + 3 < len(groups):
                            load_group(gi + 3)
                        for t in range(ntile):
                            ps_ = it % 2

                            def qkmmA(tt, t=t, slot=slot, qh=qh, cp=cp):
                                for piece in range(2):
                                    qpc = slice(qh * 1024 + piece * 512,
                                                qh * 1024 + piece * 512 + 512)
                                    ins = tt.matmul(
                                        sps[:, piece * 512:piece * 512 + 512],
                                        K_sb[0:64, slot, t * P:(t + 1) * P],
                                        q_src[0:64, cp, qpc],
                                        start=True, stop=True)
                                return ins

                            def qkmmB(tt, t=t, slot=slot, qh=qh, cp=cp):
                                for piece in range(2):
                                    qpc = slice(qh * 1024 + piece * 512,
                                                qh * 1024 + piece * 512 + 512)
                                    ins = tt.matmul(
                                        sps[:, 1024 + piece * 512:
                                            1024 + piece * 512 + 512],
                                        K_sb[64:128, slot, t * P:(t + 1) * P],
                                        q_src[64:128, cp, qpc],
                                        start=True, stop=True,
                                        tile_position=(64, 0))
                                return ins
                            S.add("tensor", qkmmA,
                                  reads=[K_sb[0:64, slot, t * P:(t + 1) * P],
                                         q_src[0:64, cp,
                                               qh * 1024:qh * 1024 + 1024]],
                                  writes=[sps[:, 0:1024]], tag=f"{tag}qkA")
                            S.add("tensor", qkmmB,
                                  reads=[K_sb[64:128, slot, t * P:(t + 1) * P],
                                         q_src[64:128, cp,
                                               qh * 1024:qh * 1024 + 1024]],
                                  writes=[sps[:, 1024:2048]], tag=f"{tag}qkB")
                            if pend[0] is not None:
                                pend[0]()
                            S.add("scalar", lambda s, cp=cp, ps_=ps_: s.activation(
                                      Pt[:, ps_, 0:1024], sps[:, 0:1024], AF.Exp,
                                      bias=negc[:, cp:cp + 1]),
                                  reads=[sps[:, 0:1024], negc[:, cp:cp + 1]],
                                  writes=[Pt[:, ps_, 0:1024]], tag=f"{tag}expA")
                            S.add("scalar", lambda s, cp=cp, ps_=ps_: s.activation(
                                      Pt[:, ps_, 1024:2048], sps[:, 1024:2048], AF.Exp,
                                      bias=negc[:, cp:cp + 1]),
                                  reads=[sps[:, 1024:2048], negc[:, cp:cp + 1]],
                                  writes=[Pt[:, ps_, 1024:2048]], tag=f"{tag}expB")

                            def mk_pv(it=it, ps_=ps_, slot=slot, t=t,
                                      nt_total=nt_total):
                                def emit():
                                    def pvmm(tt):
                                        for piece in range(2):
                                            tt.matmul(
                                                cA[:, piece * 512:piece * 512 + 512],
                                                V_sb[:, slot, t, 0:65],
                                                Pt[:, ps_, piece * 512:
                                                   piece * 512 + 512],
                                                start=(it == 0),
                                                stop=(it == nt_total - 1))
                                        for piece in range(2):
                                            ins = tt.matmul(
                                                cB[:, piece * 512:piece * 512 + 512],
                                                V_sb[:, slot, t, 65:130],
                                                Pt[:, ps_, 1024 + piece * 512:
                                                   1024 + piece * 512 + 512],
                                                start=(it == 0),
                                                stop=(it == nt_total - 1))
                                        return ins
                                    S.add("tensor", pvmm,
                                          reads=[V_sb[:, slot, t, :],
                                                 Pt[:, ps_, :]],
                                          writes=[cA[:, :], cB[:, :]],
                                          tag=f"{tag}pv")
                                return emit
                            pend[0] = mk_pv()
                            it += 1
                    pend[0]()
                    pend[0] = None
                    lA = shF[64:65, 0:1024]
                    lB = shF[64:65, 1024:2048]
                    S.add("vector", lambda v: v.tensor_scalar_add(lA, cA[64:65, :],
                                                                  1e-30),
                          reads=[cA[:, :]], writes=[lA], tag=f"{tag}la")
                    S.add("vector", lambda v: v.tensor_scalar_add(lB, cB[64:65, :],
                                                                  1e-30),
                          reads=[cB[:, :]], writes=[lB], tag=f"{tag}lb")
                    S.add("vector", lambda v: v.reciprocal(lA, lA), reads=[lA],
                          writes=[lA], tag=f"{tag}lra")
                    S.add("vector", lambda v: v.reciprocal(lB, lB), reads=[lB],
                          writes=[lB], tag=f"{tag}lrb")
                    def bcmm(t):
                        for p0 in (0, 512):
                            t.matmul(sps[0:64, p0:p0 + 512], onesF[64:65, 0:64],
                                     lA[:, p0:p0 + 512], start=True, stop=True)
                            ins = t.matmul(sps[0:64, 1024 + p0:1024 + p0 + 512],
                                           onesF[64:65, 0:64], lB[:, p0:p0 + 512],
                                           start=True, stop=True)
                        return ins
                    S.add("tensor", bcmm, reads=[onesF[:, :], lA, lB],
                          writes=[sps[:, :]], tag=f"{tag}bc")
                    S.add("vector", lambda v: v.tensor_copy(shF[0:64, :],
                                                            sps[0:64, :]),
                          reads=[sps[0:64, :]], writes=[shF[0:64, :]], tag=f"{tag}lvc")
                    S.add("vector", lambda v: v.tensor_mul(ctxAn[:, :], cA[0:64, :],
                                                           shF[0:64, 0:1024]),
                          reads=[cA[:, :], shF[0:64, :]], writes=[ctxAn[:, :]],
                          tag=f"{tag}na")
                    S.add("vector", lambda v: v.tensor_mul(ctxBn[:, :], cB[0:64, :],
                                                           shF[0:64, 1024:2048]),
                          reads=[cB[:, :], shF[0:64, :]], writes=[ctxBn[:, :]],
                          tag=f"{tag}nb")
                    def pkmm(t):
                        for p0 in (0, 512):
                            t.matmul(sps[:, p0:p0 + 512], Ilo[:, :],
                                     ctxAn[:, p0:p0 + 512], start=True, stop=False)
                            ins = t.matmul(sps[:, p0:p0 + 512], Ihi[:, :],
                                           ctxBn[:, p0:p0 + 512], start=False,
                                           stop=True)
                        return ins
                    S.add("tensor", pkmm,
                          reads=[Ilo[:, :], Ihi[:, :], ctxAn[:, :], ctxBn[:, :]],
                          writes=[sps[:, 0:1024]], tag=f"{tag}pk")
                    epilogue(qh, cp, sps)
    def self_epi(qh, cp, sps):
        S.add("vector", lambda v, qh=qh, cp=cp: v.tensor_copy(
                  nctx[:, cp, qh * 1024:qh * 1024 + 1024], sps[:, 0:1024]),
              reads=[sps[:, 0:1024]], writes=[nctx[:, cp, qh * 1024:qh * 1024 + 1024]],
              tag=f"sepi{qh}_{cp}")
    attn_pass("sa", q_fm, negc_s, [0, 1, 2, 3], TOK, k_d, v_d, 2, 8, self_epi)

    for c in range(NCH):
        S.dma("gpsimd", wbuf[:, c, :], wo_d.ap()[c * P:(c + 1) * P, :], tag="ld_wo")
    S.dma("sync", bcol[:, :], boc_d.ap(), tag="ld_bo")
    with nc.psum_tensor("C2_pps", [P, 1024], F32) as pps:
        emit_proj_generic(S, C, wbuf, lambda i: nctx[:, i, :],
                          lambda o: out_fm[:, o, :], pps, NCH, TOK, AF.Identity,
                          lambda o: bcol[:, o:o + 1], 1.0, tag="wo")
        for c in range(NCH):
            S.dma("gpsimd", wbuf[:, c, :], wrq_d.ap()[c * P:(c + 1) * P, :],
                  tag="ld_wrq")
        S.dma("sync", bcol2[:, :], brqc_d.ap(), tag="ld_brq")
        S.add("vector", lambda v: v.tensor_scalar_mul(bcol2[:, :], bcol2[:, :],
                                                      1.0 / 8.0),
              reads=[bcol2[:, :]], writes=[bcol2[:, :]], tag="brq8")
        emit_proj_generic(S, C, wbuf, lambda i: out_fm[:, i, :],
                          lambda o: nctx[:, o, :], pps, NCH, TOK, AF.Identity,
                          lambda o: bcol2[:, o:o + 1], 1.0 / 8.0, tag="wrq")
    emit_tok_norms(S, C, nctx, TOK, statrow[:, 0:16], sqstat, True, "rqn")
    rkst = bass.AP(rk_d, 1024 * 512, [[1, 1], [1032 * 512, 8], [1, 8]])
    S.dma("gpsimd", rows[0:1, 16:80], rkst, tag="ld_rkst",
          allow_slow_non_contiguous=True)
    S.add("vector", lambda v: v.tensor_reduce(
              rows[:, 80:88], rows[:, 16:80].rearrange("o (r c) -> o c r", c=8),
              AX.X, ALU.max),
          reads=[rows[:, 16:80]], writes=[rows[:, 80:88]], tag="rknmax")
    emit_shift(S, C, statrow[:, 0:8], rows[:, 80:88], 8, negc_r, rows[:, 96:160],
               "shr")

    for p2 in range(2):
        def rec_epi(qh, cp, sps, p2=p2):
            if p2 == 0:
                S.add("vector", lambda v, qh=qh, cp=cp: v.tensor_scalar_mul(
                          rec_fm[:, cp, qh * 1024:qh * 1024 + 1024], sps[:, 0:1024],
                          0.5),
                      reads=[sps[:, 0:1024]],
                      writes=[rec_fm[:, cp, qh * 1024:qh * 1024 + 1024]],
                      tag=f"repi{p2}_{qh}_{cp}")
            else:
                S.add("vector", lambda v, qh=qh, cp=cp: v.scalar_tensor_tensor(
                          rec_fm[:, cp, qh * 1024:qh * 1024 + 1024], sps[:, 0:1024],
                          0.5, rec_fm[:, cp, qh * 1024:qh * 1024 + 1024],
                          ALU.mult, ALU.add),
                      reads=[sps[:, 0:1024],
                             rec_fm[:, cp, qh * 1024:qh * 1024 + 1024]],
                      writes=[rec_fm[:, cp, qh * 1024:qh * 1024 + 1024]],
                      tag=f"repi{p2}_{qh}_{cp}")
        attn_pass(f"ra{p2}", nctx, negc_r, [4 * p2 + i for i in range(4)], 512,
                  rk_d, rv_d, 1, 4, rec_epi)

    for c in range(NCH):
        S.dma("gpsimd", wbuf[:, c, :], wgA_d.ap()[c * P:(c + 1) * P, :], tag="ld_wga")
        S.dma("gpsimd", wbuf2[:, c, :], wgB_d.ap()[c * P:(c + 1) * P, :], tag="ld_wgb")
    S.dma("sync", bcol[:, :], bgc_d.ap(), tag="ld_bg")
    with nc.psum_tensor("C2_gps", [P, 1024], F32) as gps:
        for o in range(NCH):
            for piece in range(4):
                sl = slice(piece * 512, piece * 512 + 512)
                pb = gps[:, ((o * 4 + piece) % 2) * 512:((o * 4 + piece) % 2) * 512 + 512]
                def gmm(t, o=o, sl=sl, pb=pb):
                    for i in range(NCH):
                        t.matmul(pb, wbuf[:, i, o * P:(o + 1) * P],
                                 out_fm[:, i, sl], start=(i == 0), stop=False)
                    for i in range(NCH):
                        ins = t.matmul(pb, wbuf2[:, i, o * P:(o + 1) * P],
                                       rec_fm[:, i, sl],
                                       start=False, stop=(i == NCH - 1))
                    return ins
                S.add("tensor", gmm,
                      reads=[wbuf[:, :, :], wbuf2[:, :, :], out_fm[:, :, sl],
                             rec_fm[:, :, sl]],
                      writes=[pb], tag=f"g_{o}_{piece}")
                S.add("scalar", lambda s, o=o, sl=sl, pb=pb: s.activation(
                          gate_fm[:, o, sl], pb, AF.Sigmoid, bias=bcol[:, o:o + 1]),
                      reads=[pb, bcol[:, o:o + 1]], writes=[gate_fm[:, o, sl]],
                      tag=f"ge_{o}_{piece}")
    for c in range(NCH):
        for piece in range(4):
            sl = slice(piece * 512, piece * 512 + 512)
            S.add("vector", lambda v, c=c, sl=sl: v.tensor_sub(
                      t2f[:, :], rec_fm[:, c, sl], out_fm[:, c, sl]),
                  reads=[rec_fm[:, c, sl], out_fm[:, c, sl]], writes=[t2f[:, :]],
                  tag=f"cb1_{c}_{piece}")
            S.add("vector", lambda v, c=c, sl=sl: v.tensor_mul(
                      t2f[:, :], t2f[:, :], gate_fm[:, c, sl]),
                  reads=[t2f[:, :], gate_fm[:, c, sl]], writes=[t2f[:, :]],
                  tag=f"cb2_{c}_{piece}")
            S.add("vector", lambda v, c=c, sl=sl: v.tensor_add(
                      t1f[:, :], t2f[:, :], out_fm[:, c, sl]),
                  reads=[t2f[:, :], out_fm[:, c, sl]], writes=[t1f[:, :]],
                  tag=f"cb3_{c}_{piece}")
            if FC is None:
                bufi = (c * 4 + piece) % 2
                S.dma("sync", xs[:, bufi, :],
                      x_d.ap()[c * P:(c + 1) * P, sl], tag=f"ld_x{c}_{piece}")
                S.add("vector", lambda v, bufi=bufi: v.tensor_add(
                          ys[:, bufi, :], t1f[:, :], xs[:, bufi, :]),
                      reads=[t1f[:, :], xs[:, bufi, :]], writes=[ys[:, bufi, :]],
                      tag=f"cb4_{c}_{piece}")
                S.dma("sync", out_d.ap()[c * P:(c + 1) * P, sl], ys[:, bufi, :],
                      tag=f"st_x{c}_{piece}")
            else:
                S.add("vector", lambda v, c=c, sl=sl: v.tensor_add(
                          x_res[:, c, sl], x_res[:, c, sl], t1f[:, :]),
                      reads=[t1f[:, :], x_res[:, c, sl]], writes=[x_res[:, c, sl]],
                      tag=f"cb4_{c}_{piece}")
    if FC is None:
        S.barrier_read("sync", [out_d.ap()])
        S.build()
    return nc
def pack_C2(inputs, x2, core):
    sl = x2.reshape(NCORE * TOK, D)[core * TOK:(core + 1) * TOK]
    wg = np.asarray(inputs["L1_attn_Wg"], np.float32)
    return {"x_fm": np.ascontiguousarray(sl.T.astype(np.float32)),
            "Wo": np.ascontiguousarray(np.asarray(inputs["L1_attn_Wo"], np.float32)),
            "boc": chunk_col(inputs["L1_attn_bo"]),
            "Wrq": np.ascontiguousarray(np.asarray(inputs["L1_attn_Wrq"], np.float32)),
            "brqc": chunk_col(inputs["L1_attn_brq"]),
            "WgA": np.ascontiguousarray(wg[0:1024]),
            "WgB": np.ascontiguousarray(wg[1024:2048]),
            "bgc": chunk_col(inputs["L1_attn_bg"])}


def build_D(FC=None):
    if FC is None:
        nc, S = _newprog()
        es = None
        x_d = nc.dram_tensor("x_fm", [D, TOK], F32, kind="ExternalInput")
        C = _mk_consts(nc)
        x_res = nc.alloc_sbuf_tensor("x_res", [P, NCH, TOK], F32)
        emit_consts(S, C)
        for c in range(NCH):
            S.dma("sync", x_res[:, c, :], x_d.ap()[c * P:(c + 1) * P, :], tag="ld_x")
        H = nc.alloc_sbuf_tensor("Hscr", [P, NCH, TOK], BF16)
    else:
        nc, S, C, x_res, es = FC["nc"], FC["S"], FC["C"], FC["x_res"], FC["es"]
        H = FC["outb"]
    w_d = nc.dram_tensor("wT", [D, D], F32, kind="ExternalInput")
    al_d = nc.dram_tensor("alpha", [1, 1], F32, kind="ExternalInput")
    bc_d = nc.dram_tensor("biasc", [P, NCH], F32, kind="ExternalInput")
    out_d = nc.dram_tensor("y_fm", [D, TOK], F32, kind="ExternalOutput")

    xn = AL(nc, es, "D_xn", [P, NCH, TOK], BF16)
    Hf = H.bitcast(F32).rearrange("p c w -> p (c w)")
    SCRF = AL(nc, es, "D_SCRF", [P, TOK], F32)
    wfst = AL(nc, es, "D_wfst", [P, 2, 1024], F32)
    tern = AL(nc, es, "D_tern", [P, NCH, 1024], BF16)
    bcol = AL(nc, es, "D_bcol", [P, NCH], F32)
    sums8 = AL(nc, es, "D_sums8", [P, NCH], F32)
    srow = AL(nc, es, "D_srow", [1, 16], F32)
    deltaB = AL(nc, es, "D_deltaB", [P, 2], F32)
    absw = AL(nc, es, "D_absw", [P, 1024], F32)
    sgn = AL(nc, es, "D_sgn", [P, 1024], BF16)
    msk = AL(nc, es, "D_msk", [P, 1024], BF16)
    alr = AL(nc, es, "D_alr", [1, 1], F32)
    ones_col_f = AL(nc, es, "D_ones_col", [P, 1], F32)

    S.add("gpsimd", lambda g: g.memset(ones_col_f[:, :], 1.0),
          writes=[ones_col_f[:, :]])
    for c in range(NCH):
        S.dma("sync", wfst[:, c % 2, :], w_d.ap()[c * P:(c + 1) * P, :], tag="ld_w")
        S.add("scalar", lambda s, c=c: s.activation(absw[:, :], wfst[:, c % 2, :],
                                                    AF.Abs),
              reads=[wfst[:, c % 2, :]], writes=[absw[:, :]], tag=f"q_abs{c}")
        S.add("vector", lambda v, c=c: v.tensor_reduce(sums8[:, c:c + 1], absw[:, :],
                                                       AX.X, ALU.add),
              reads=[absw[:, :]], writes=[sums8[:, c:c + 1]], tag=f"q_sum{c}")
    S.dma("sync", bcol[:, :], bc_d.ap(), tag="ld_b")
    S.dma("sync", alr[:, :], al_d.ap(), tag="ld_al")

    lnscr = dict(xb=H[:, 0:2, :], sq=H[:, 2:4, :],
                 mrow=Hf[0:1, 0:TOK], wrow=Hf[0:1, TOK:2 * TOK],
                 mb=H[:, 4, :], ab=H[:, 5, :], t1=SCRF[:, :])
    emit_ln(S, C, x_res, xn, lnscr, tag="ln4")

    with nc.psum_tensor("qps", [1, 16], F32) as qps:
        S.add("tensor", lambda t: t.matmul(qps[:, 0:8], ones_col_f[:, :],
                                           sums8[:, :], start=True, stop=True),
              reads=[ones_col_f[:, :], sums8[:, :]], writes=[qps[:, 0:8]],
              tag="q_mm")
        S.add("vector", lambda v: v.tensor_reduce(srow[:, 0:1], qps[:, 0:8],
                                                  AX.X, ALU.add),
              reads=[qps[:, 0:8]], writes=[srow[:, 0:1]], tag="q_tot")
    S.add("vector", lambda v: v.tensor_scalar_mul(srow[:, 1:2], srow[:, 0:1],
                                                  0.7 / (1024.0 * 1024.0)),
          reads=[srow[:, 0:1]], writes=[srow[:, 1:2]], tag="q_del")
    with nc.psum_tensor("qbp", [P, 2], F32) as qbp:
        S.add("tensor", lambda t: t.matmul(qbp[:, 0:1], C["ones_row_f"][:, :],
                                           srow[:, 1:2], start=True, stop=True),
              reads=[C["ones_row_f"][:, :], srow[:, 1:2]], writes=[qbp[:, 0:1]],
              tag="q_b1")
        S.add("tensor", lambda t: t.matmul(qbp[:, 1:2], C["ones_row_f"][:, :],
                                           alr[:, :], start=True, stop=True),
              reads=[C["ones_row_f"][:, :], alr[:, :]], writes=[qbp[:, 1:2]],
              tag="q_b2")
        S.add("vector", lambda v: v.tensor_copy(deltaB[:, :], qbp[:, :]),
              reads=[qbp[:, :]], writes=[deltaB[:, :]], tag="q_bc")
    for c in range(NCH):
        S.dma("sync", wfst[:, c % 2, :], w_d.ap()[c * P:(c + 1) * P, :], tag="ld_w2")
        S.add("scalar", lambda s, c=c: s.sign(sgn[:, :], wfst[:, c % 2, :]),
              reads=[wfst[:, c % 2, :]], writes=[sgn[:, :]], tag=f"q_sgn{c}")
        S.add("scalar", lambda s, c=c: s.activation(absw[:, :], wfst[:, c % 2, :],
                                                    AF.Abs),
              reads=[wfst[:, c % 2, :]], writes=[absw[:, :]], tag=f"q_ab2{c}")
        S.add("vector", lambda v, c=c: v.tensor_scalar(msk[:, :], absw[:, :],
                                                       deltaB[:, 0:1], None,
                                                       ALU.is_gt),
              reads=[absw[:, :], deltaB[:, 0:1]], writes=[msk[:, :]], tag=f"q_m{c}")
        S.add("vector", lambda v, c=c: v.tensor_mul(tern[:, c, :], msk[:, :],
                                                    sgn[:, :]),
              reads=[msk[:, :], sgn[:, :]], writes=[tern[:, c, :]], tag=f"q_t{c}")
    with nc.psum_tensor("dps", [P, 1024], F32) as dps:
        for o in range(NCH):
            for piece in range(4):
                sl = slice(piece * 512, piece * 512 + 512)
                pb = dps[:, ((o * 4 + piece) % 2) * 512:((o * 4 + piece) % 2) * 512 + 512]
                def dmm(t, o=o, sl=sl, pb=pb):
                    for i in range(NCH):
                        ins = t.matmul(pb, tern[:, i, o * P:(o + 1) * P],
                                       xn[:, i, sl],
                                       start=(i == 0), stop=(i == NCH - 1))
                    return ins
                S.add("tensor", dmm, reads=[tern[:, :, :], xn[:, :, sl]],
                      writes=[pb], tag=f"d_mm{o}_{piece}")
                S.add("vector", lambda v, o=o, sl=sl, pb=pb: v.scalar_tensor_tensor(
                          SCRF[:, 0:512], pb, deltaB[:, 1:2],
                          bcol[:, o:o + 1].broadcast_to([P, 512]), ALU.mult, ALU.add),
                      reads=[pb, deltaB[:, 1:2], bcol[:, o:o + 1]],
                      writes=[SCRF[:, 0:512]], tag=f"d_e{o}_{piece}")
                S.add("vector", lambda v, o=o, sl=sl: v.tensor_add(
                          x_res[:, o, sl], x_res[:, o, sl], SCRF[:, 0:512]),
                      reads=[SCRF[:, 0:512], x_res[:, o, sl]],
                      writes=[x_res[:, o, sl]], tag=f"d_r{o}_{piece}")
    for c in range(NCH):
        S.dma("sync", out_d.ap()[c * P:(c + 1) * P, :], x_res[:, c, :], tag="st_y")
    S.barrier_read("sync", [out_d.ap()])
    if FC is None:
        S.build()
    return nc


def pack_D(inputs, x3, core):
    sl = x3.reshape(NCORE * TOK, D)[core * TOK:(core + 1) * TOK]
    return {"x_fm": np.ascontiguousarray(sl.T.astype(np.float32)),
            "wT": np.ascontiguousarray(np.asarray(inputs["L1_ffn_weight"],
                                                  np.float32).T),
            "alpha": np.asarray(inputs["L1_ffn_alpha"], np.float32).reshape(1, 1),
            "biasc": chunk_col(inputs["L1_ffn_bias"])}


def build_full(n_phases=6):
    nc, S = _newprog()
    C = _mk_consts(nc)
    x_d = nc.dram_tensor("x_fm", [D, TOK], F32, kind="ExternalInput")
    x_res = nc.alloc_sbuf_tensor("x_res", [P, NCH, TOK], F32)
    outb = nc.alloc_sbuf_tensor("outb", [P, NCH, TOK], BF16)
    emit_consts(S, C)
    bk2_d = nc.dram_tensor("bk2_idx", [P, 1], mybir.dt.int32, kind="ExternalInput")
    bv_d = nc.dram_tensor("bv_idx", [P, 1], mybir.dt.int32, kind="ExternalInput")
    bv8_d = nc.dram_tensor("bv8_idx", [P, 1], mybir.dt.int32, kind="ExternalInput")
    bk2 = nc.alloc_sbuf_tensor("bk2_sb", [P, 1], mybir.dt.int32)
    bv = nc.alloc_sbuf_tensor("bv_sb", [P, 1], mybir.dt.int32)
    bv8 = nc.alloc_sbuf_tensor("bv8_sb", [P, 1], mybir.dt.int32)
    S.dma("sync", bk2[:, :], bk2_d.ap(), tag="ld_bk2")
    S.dma("sync", bv[:, :], bv_d.ap(), tag="ld_bv")
    S.dma("sync", bv8[:, :], bv8_d.ap(), tag="ld_bv8")
    for c in range(NCH):
        S.dma("sync", x_res[:, c, :], x_d.ap()[c * P:(c + 1) * P, :], tag="ld_x")
    FC = dict(nc=nc, S=S, C=C, x_res=x_res, outb=outb, io={}, es=None,
              bk2=bk2, bv=bv, bv8=bv8)
    phases = (build_A, build_B1, build_B2, build_C1, build_C2, build_D)[:n_phases]
    for phase in phases:
        with ExitStack() as es:
            FC["es"] = es
            phase(FC)
    if n_phases < 6:
        yd = nc.dram_tensor("y_fm", [D, TOK], F32, kind="ExternalOutput")
        for c in range(NCH):
            S.dma("sync", yd.ap()[c * P:(c + 1) * P, :], x_res[:, c, :], tag="st_y")
        S.barrier_read("sync", [yd.ap()])
    S.build()
    return nc


def pack_full(inputs, core):
    m = pack_A(inputs, core)
    b = core // 4
    m["bk2_idx"] = (b * 2 * 4 * 1032 + 2 * np.arange(P, dtype=np.int32)).reshape(P, 1)
    m["bv_idx"] = (b * 4 * TOK + np.arange(P, dtype=np.int32)).reshape(P, 1)
    m["bv8_idx"] = (b * 4 * TOK * 8 + 8 * np.arange(P, dtype=np.int32)).reshape(P, 1)
    xdummy = np.zeros((NCORE * TOK, D), np.float32)
    for pk in (lambda: pack_B1(inputs, xdummy, core),
               lambda: pack_C1(inputs, xdummy, core),
               lambda: pack_C2(inputs, xdummy, core),
               lambda: pack_D(inputs, xdummy, core)):
        d = pk()
        d.pop("x_fm", None)
        m.update(d)
    return m


def kernel_fused(**inputs):
    nc = build_full()
    maps = [pack_full(inputs, c) for c in range(NCORE)]
    res = run_bass_kernel_spmd(nc, maps, core_ids=list(range(NCORE)))
    out = np.empty((NCORE * TOK, D), np.float32)
    for c in range(NCORE):
        out[c * TOK:(c + 1) * TOK] = res.results[c]["y_fm"].T
    return out.reshape(2, NCORE * TOK // 2, D)


def _run(nc, maps):
    res = run_bass_kernel_spmd(nc, maps, core_ids=list(range(NCORE)))
    return res.results


def _asm_x(results, key):
    out = np.empty((NCORE * TOK, D), np.float32)
    for c in range(NCORE):
        out[c * TOK:(c + 1) * TOK] = results[c][key].T
    return out


FUSED = True


def kernel(**inputs):
    if FUSED:
        return kernel_fused(**inputs)
    resA = _run(build_A(), [pack_A(inputs, c) for c in range(NCORE)])
    x1 = _asm_x(resA, "x1_fm")
    resB1 = _run(build_B1(), [pack_B1(inputs, x1, c) for c in range(NCORE)])
    kfull = np.concatenate([resB1[c]["k_o"] for c in range(NCORE)], axis=0)
    vfull = np.concatenate([resB1[c]["v_o"] for c in range(NCORE)], axis=0)
    mapsB2 = []
    for c in range(NCORE):
        b = c // 4
        mapsB2.append({
            "x_fm": np.ascontiguousarray(
                x1[c * TOK:(c + 1) * TOK].T.astype(np.float32)),
            "q_fm": np.ascontiguousarray(resB1[c]["q_fm_o"]),
            "qn": resB1[c]["qn_o"],
            "kfull": np.ascontiguousarray(kfull[b * 4 * 1032:(b + 1) * 4 * 1032]),
            "vfull": np.ascontiguousarray(vfull[b * 4 * TOK:(b + 1) * 4 * TOK]),
        })
    resB2 = _run(build_B2(), mapsB2)
    x2 = _asm_x(resB2, "x2_fm")
    resC1 = _run(build_C1(), [pack_C1(inputs, x2, c) for c in range(NCORE)])
    kfull = np.concatenate([resC1[c]["k_o"] for c in range(NCORE)], axis=0)
    vfull = np.concatenate([resC1[c]["v_o"] for c in range(NCORE)], axis=0)
    rkfull = np.concatenate([resC1[c]["rk_o"] for c in range(NCORE)], axis=0)
    rvfull = np.concatenate([resC1[c]["rv_o"] for c in range(NCORE)], axis=0)
    mapsC2 = []
    for c in range(NCORE):
        b = c // 4
        m = pack_C2(inputs, x2, c)
        m.update({
            "q_fm": np.ascontiguousarray(resC1[c]["q_fm_o"]),
            "qn": resC1[c]["qn_o"],
            "kfull": np.ascontiguousarray(kfull[b * 4 * 1032:(b + 1) * 4 * 1032]),
            "vfull": np.ascontiguousarray(vfull[b * 4 * TOK:(b + 1) * 4 * TOK]),
            "rkfull": np.ascontiguousarray(rkfull),
            "rvfull": np.ascontiguousarray(rvfull),
        })
        mapsC2.append(m)
    resC2 = _run(build_C2(), mapsC2)
    x3 = _asm_x(resC2, "x3_fm")
    resD = _run(build_D(), [pack_D(inputs, x3, c) for c in range(NCORE)])
    y = _asm_x(resD, "y_fm")
    return y.reshape(2, NCORE * TOK // 2, D)
